# version 53
# baseline (speedup 1.0000x reference)
import numpy as np
import ml_dtypes
import concourse.bass as bass
import concourse.mybir as mybir
from concourse.bass_utils import run_bass_kernel_spmd

F32 = mybir.dt.float32
BF16 = mybir.dt.bfloat16
AF = mybir.ActivationFunctionType
OP = mybir.AluOpType
AX = mybir.AxisListType

D = 1024
NCH = 8
TU = 512


class Prog:

    NPOOL = 6

    def __init__(self, nc):
        self.nc = nc
        self.events = []
        self.cnt = {}
        self.sems = {}
        self.last_wait = {}
        self._drained = {}
        names = ["tv", "vv", "sv", "gv"]
        self.pool = {"sync": [], "scalar": [], "gpsimd": []}
        for q in ("sync", "scalar", "gpsimd"):
            for i in range(self.NPOOL):
                n = f"dma_{q}_{i}"
                names.append(n)
                self.pool[q].append(n)
        for s in names:
            self.sems[s] = nc.alloc_semaphore(s)
            self.cnt[s] = 0
        self.pool_idx = {"sync": 0, "scalar": 0, "gpsimd": 0}
        self.DMA_SEMS = tuple(n for n in names if n.startswith("dma_"))

    def ev(self, eng, fn):
        self.events.append((eng, fn))
        self._drained[eng] = False

    def op(self, eng, fn, inc=None):
        if inc is None:
            self.ev(eng, fn)
            return None
        n = 16 if inc in self.DMA_SEMS else 1
        self.cnt[inc] += n
        tgt = self.cnt[inc]

        def run(e, f=fn, s=self.sems[inc], k=n):
            f(e).then_inc(s, k)

        self.ev(eng, run)
        return (inc, tgt)

    def dma_group(self, eng, pairs, accum=False):
        sem = self.pool[eng][self.pool_idx[eng] % self.NPOOL]
        self.pool_idx[eng] += 1
        if self.cnt[sem] > 0:
            self.wait(eng, (sem, self.cnt[sem]))
        tok = None
        for out, in_ in pairs:
            if accum:
                tok = self.op(eng, lambda e, o=out, i=in_:
                              e.dma_start(out=o, in_=i, accum_op=OP.add),
                              inc=sem)
            else:
                tok = self.op(eng, lambda e, o=out, i=in_:
                              e.dma_start(out=o, in_=i), inc=sem)
        return tok

    def dma(self, eng, out, in_):
        return self.dma_group(eng, [(out, in_)])

    def all_dma_toks(self, eng):
        return [(s, self.cnt[s]) for s in self.pool[eng] if self.cnt[s] > 0]

    def wait(self, eng, tok):
        if tok is None:
            return
        sem, val = tok
        k = (eng, sem)
        if self.last_wait.get(k, 0) >= val:
            return
        self.last_wait[k] = val
        self.ev(eng, lambda e, s=self.sems[sem], v=val: e.wait_ge(s, v))

    def waits(self, eng, toks):
        for t in toks:
            self.wait(eng, t)

    def drain(self, eng):
        if self._drained.get(eng):
            return
        self.events.append((eng, lambda e: e.drain()))
        self._drained[eng] = True

    def mm_group(self, psum_ap, mms):
        n = len(mms)
        tok = None
        for i, (w, m) in enumerate(mms):
            tok = self.op("tensor", lambda e, o=psum_ap, ww=w, mm=m,
                          s=(i == 0), sp=(i == n - 1):
                          e.matmul(o, ww, mm, start=s, stop=sp),
                          inc="tv" if i == n - 1 else None)
        return tok

    def mm_group_w(self, psum_ap, mms, pre):
        n = len(mms)
        tok = None
        for i, (w, m) in enumerate(mms):
            if pre[i]:
                self.waits("tensor", pre[i])
            tok = self.op("tensor", lambda e, o=psum_ap, ww=w, mm=m,
                          s=(i == 0), sp=(i == n - 1):
                          e.matmul(o, ww, mm, start=s, stop=sp),
                          inc="tv" if i == n - 1 else None)
        return tok

    def mm_group_halves(self, half_aps, mms):
        firsts, lasts = {}, {}
        for i, (_, _, h) in enumerate(mms):
            if h not in firsts:
                firsts[h] = i
            lasts[h] = i
        tok = None
        for i, (w, m, h) in enumerate(mms):
            tok = self.op("tensor", lambda e, o=half_aps[h], ww=w, mm=m,
                          s=(i == firsts[h]), sp=(i == lasts[h]):
                          e.matmul(o, ww, mm, start=s, stop=sp),
                          inc="tv" if i == len(mms) - 1 else None)
        return tok


class Region:
    def __init__(self, nc, base, top):
        self.nc = nc
        self.cur = base
        self.top = top

    def t(self, name, shape, dtype):
        per_part = int(np.prod(shape[1:])) * mybir.dt.size(dtype)
        addr = (self.cur + 63) & ~63
        self.cur = addr + per_part
        assert self.cur <= self.top, (
            f"SBUF overflow: {name} ends at {self.cur} > {self.top}")
        return self.nc.alloc_sbuf_tensor_at(name, shape, dtype, offset=addr)


def hilo_mms(whi_t, wlo_t, xhi_t, xlo_t, wslices, xslice):
    mms = []
    for ki in range(NCH):
        wh, wl = wslices(whi_t, ki), wslices(wlo_t, ki)
        xh, xl = xslice(xhi_t, ki), xslice(xlo_t, ki)
        mms.append((wh, xh))
        mms.append((wh, xl))
        mms.append((wl, xh))
    return mms


def build_core(S):
    nc = bass.Bass("TRN2", target_bir_lowering=False)
    NU = S // TU
    NBLK = S // 128
    QG = 1024 if S % 1024 == 0 else 512
    NG = S // QG
    SUBS = QG // 128
    KCH_PER_G = QG // 512

    xd = nc.dram_tensor("x", [S, D], F32, kind="ExternalInput")
    wd = {w: nc.dram_tensor(w + "t", [D, D], F32, kind="ExternalInput")
          for w in ("wq", "wk", "wv", "wo", "wr", "wk2", "wv2")}
    bqf = nc.dram_tensor("bqf", [128, 8], F32, kind="ExternalInput")
    bkf = nc.dram_tensor("bkf", [128, 8], F32, kind="ExternalInput")
    bof = nc.dram_tensor("bof", [128, 8], F32, kind="ExternalInput")
    murf = nc.dram_tensor("murf", [128, 8], F32, kind="ExternalInput")
    mukf = nc.dram_tensor("mukf", [128, 8], F32, kind="ExternalInput")
    bvbd = nc.dram_tensor("bvb", [128, D], F32, kind="ExternalInput")
    idbd = nc.dram_tensor("idb", [128, 128], BF16, kind="ExternalInput")
    idfd = nc.dram_tensor("idf", [128, 128], F32, kind="ExternalInput")
    maskd = nc.dram_tensor("mask", [128, 128], F32, kind="ExternalInput")
    epsd = nc.dram_tensor("epsc", [128, 1], F32, kind="ExternalInput")
    yd = nc.dram_tensor("y", [S, D], F32, kind="ExternalOutput")

    sqh = nc.dram_tensor("sqh", [NCH, 128, S], BF16, kind="Internal")
    sql = nc.dram_tensor("sql", [NCH, 128, S], BF16, kind="Internal")
    skh = nc.dram_tensor("skh", [NCH, 128, S], BF16, kind="Internal")
    skl = nc.dram_tensor("skl", [NCH, 128, S], BF16, kind="Internal")
    svh = nc.dram_tensor("svh", [NBLK, 128, D], BF16, kind="Internal")
    svl = nc.dram_tensor("svl", [NBLK, 128, D], BF16, kind="Internal")
    sx1 = nc.dram_tensor("sx1", [NBLK, 128, D], F32, kind="Internal")
    xr3 = xd.ap().rearrange("(n p) m -> n p m", p=128)
    yr3 = yd.ap().rearrange("(n p) m -> n p m", p=128)

    ps_a = nc.alloc_psum_tensor("ps_a", [128, 512], F32)
    ps_b = nc.alloc_psum_tensor("ps_b", [128, 512], F32)
    ps_c = nc.alloc_psum_tensor("ps_c", [128, 1024], F32)
    ps_d = nc.alloc_psum_tensor("ps_d", [128, 1024], F32)
    ps_t1 = nc.alloc_psum_tensor("ps_t1", [128, 1024], BF16)
    ps_t2 = nc.alloc_psum_tensor("ps_t2", [128, 1024], BF16)
    psS = [ps_a, ps_b]
    psH = [ps_c.ap()[:, 0:512], ps_c.ap()[:, 512:1024],
           ps_d.ap()[:, 0:512], ps_d.ap()[:, 512:1024]]
    psO = [ps_c, ps_d]
    psT = [ps_t1, ps_t2]

    c_idb = nc.alloc_sbuf_tensor("c_idb", [128, 128], BF16)
    c_idf = nc.alloc_sbuf_tensor("c_idf", [128, 128], F32)
    c_mask = nc.alloc_sbuf_tensor("c_mask", [128, 128], F32)
    c_bq = nc.alloc_sbuf_tensor("c_bq", [128, 8], F32)
    c_bk = nc.alloc_sbuf_tensor("c_bk", [128, 8], F32)
    c_bo = nc.alloc_sbuf_tensor("c_bo", [128, 8], F32)
    c_mur = nc.alloc_sbuf_tensor("c_mur", [128, 8], F32)
    c_muk = nc.alloc_sbuf_tensor("c_muk", [128, 8], F32)
    c_bv = nc.alloc_sbuf_tensor("c_bv", [128, D], F32)
    c_eps = nc.alloc_sbuf_tensor("c_eps", [128, 1], F32)

    ov_base = nc.sbuf_base + 4096
    ov_top = nc.sbuf_top

    p = Prog(nc)

    tok_consts = p.dma_group("sync", [
        (dst.ap(), srcd.ap()) for dst, srcd in
        ((c_idb, idbd), (c_idf, idfd), (c_mask, maskd),
         (c_bq, bqf), (c_bk, bkf), (c_bo, bof),
         (c_mur, murf), (c_muk, mukf), (c_bv, bvbd),
         (c_eps, epsd))])
    for engx in ("tensor", "vector", "scalar", "gpsimd"):
        p.wait(engx, tok_consts)

    def weight_piece(wdram, whi, wlo, stg, copy_toks, sub_toks, ki):
        wr = wdram.ap().rearrange("(c p) m -> c p m", p=128)
        par = ki % len(stg)
        p.waits("sync", [copy_toks[par], sub_toks[par]])
        ld = p.dma("sync", stg[par].ap(), wr[ki])
        p.wait("vector", ld)
        dsth = whi.ap()[:, ki * 1024:(ki + 1) * 1024]
        tok_c = p.op("vector", lambda e, d=dsth, s=stg[par].ap():
                     e.tensor_copy(d, s), inc="vv")
        p.wait("gpsimd", tok_c)
        g1 = p.op("gpsimd", lambda e, d=wlo.ap()[:, ki * 1024:
                  ki * 1024 + 512],
                  s=stg[par].ap()[:, 0:512],
                  h=whi.ap()[:, ki * 1024: ki * 1024 + 512]:
                  e.tensor_tensor(d, s, h, op=OP.subtract), inc="gv")
        p.drain("vector")
        g2 = p.op("vector", lambda e, d=wlo.ap()[:, ki * 1024 + 512:
                  (ki + 1) * 1024],
                  s=stg[par].ap()[:, 512:1024],
                  h=whi.ap()[:, ki * 1024 + 512:(ki + 1) * 1024]:
                  e.tensor_tensor(d, s, h, op=OP.subtract), inc="vv")
        copy_toks[par] = g2
        sub_toks[par] = g1

    def load_weight(wdram, whi, wlo, stg, copy_toks, sub_toks):
        for ki in range(NCH):
            weight_piece(wdram, whi, wlo, stg, copy_toks, sub_toks, ki)
        return copy_toks[:] + sub_toks[:]

    def load_weight_sg(wdram, whi, wlo, stg, copy_toks, sub_toks):
        wr = wdram.ap().rearrange("(c p) m -> c p m", p=128)
        ns = len(stg)
        for ki in range(NCH):
            par = ki % ns
            p.waits("sync", [copy_toks[par], sub_toks[par]])
            ld = p.dma("sync", stg[par].ap(), wr[ki])
            p.wait("scalar", ld)
            p.drain("scalar")
            tok_c = p.op("scalar", lambda e,
                         d=whi.ap()[:, ki * 1024:(ki + 1) * 1024],
                         s=stg[par].ap(): e.copy(d, s), inc="sv")
            p.wait("gpsimd", tok_c)
            g1 = p.op("gpsimd", lambda e, d=wlo.ap()[:, ki * 1024:
                      ki * 1024 + 512],
                      s=stg[par].ap()[:, 0:512],
                      h=whi.ap()[:, ki * 1024: ki * 1024 + 512]:
                      e.tensor_tensor(d, s, h, op=OP.subtract), inc="gv")
            g2 = p.op("gpsimd", lambda e, d=wlo.ap()[:, ki * 1024 + 512:
                      (ki + 1) * 1024],
                      s=stg[par].ap()[:, 512:1024],
                      h=whi.ap()[:, ki * 1024 + 512:(ki + 1) * 1024]:
                      e.tensor_tensor(d, s, h, op=OP.subtract), inc="gv")
            copy_toks[par] = tok_c
            sub_toks[par] = g2
        return copy_toks[:] + sub_toks[:] + [g1]

    def layernorm(x_tm, bn_sb, mv_sb, rst4, rinv4, in_toks, out_tm=None):
        p.waits("vector", in_toks)
        for blk in range(4):
            for h in range(2):
                src = x_tm.ap()[:, blk * 1024 + h * 512: blk * 1024 + (h + 1) * 512]
                dst = bn_sb.ap()[:, (blk * 2 + h) * 6:(blk * 2 + h + 1) * 6]
                p.op("vector", lambda e, d=dst, s=src: e.bn_stats(d, s))
        p.drain("vector")
        tok_mv = None
        for blk in range(4):
            src = bn_sb.ap()[:, blk * 12:(blk + 1) * 12]
            dst = mv_sb.ap()[:, blk * 2:(blk + 1) * 2]
            tok_mv = p.op("vector", lambda e, d=dst, s=src: e.bn_aggr(d, s),
                          inc="vv")
        p.wait("scalar", tok_mv)
        var_ap = bass.AP(mv_sb, 1, [[8, 128], [2, 4]])
        tok_sq = p.op("scalar", lambda e, d=rst4.ap(), s=var_ap:
                      e.activation(d, s, AF.Sqrt, bias=c_eps.ap()), inc="sv")
        p.wait("vector", tok_sq)
        p.op("vector", lambda e, d=rinv4.ap(), s=rst4.ap(): e.reciprocal(d, s))
        p.drain("vector")
        tok = None
        dst_t = x_tm if out_tm is None else out_tm
        for blk in range(4):
            xa = x_tm.ap()[:, blk * 1024:(blk + 1) * 1024]
            da = dst_t.ap()[:, blk * 1024:(blk + 1) * 1024]
            mcol = mv_sb.ap()[:, blk * 2: blk * 2 + 1]
            rcol = rinv4.ap()[:, blk: blk + 1]
            tok = p.op("vector", lambda e, d=da, s=xa, m=mcol, r=rcol:
                       e.tensor_scalar(d, s, m, r, op0=OP.subtract, op1=OP.mult),
                       inc="vv")
        return tok

    ra = Region(nc, ov_base, ov_top)
    wq_hi = ra.t("wq_hi", [128, 8192], BF16)
    wq_lo = ra.t("wq_lo", [128, 8192], BF16)
    wk_hi = ra.t("wk_hi", [128, 8192], BF16)
    wk_lo = ra.t("wk_lo", [128, 8192], BF16)
    wv_hi = ra.t("wv_hi", [128, 8192], BF16)
    wv_lo = ra.t("wv_lo", [128, 8192], BF16)
    stgA = [ra.t(f"stgA{i}", [128, 1024], F32) for i in range(3)]
    xa_tm2 = [ra.t("xa_tm", [128, 4096], F32), ra.t("xa_tmB", [128, 4096], F32)]
    yT = ra.t("yT", [128, 4096], F32)
    yT_hi = ra.t("yT_hi", [128, 4096], BF16)
    yT_lo = ra.t("yT_lo", [128, 4096], BF16)
    bnA = ra.t("bnA", [128, 48], F32)
    mvA = ra.t("mvA", [128, 8], F32)
    rstA = ra.t("rstA", [128, 4], F32)
    rinvA = ra.t("rinvA", [128, 4], F32)
    qk_f = [ra.t("qk_f0", [128, 512], F32), ra.t("qk_f1", [128, 512], F32)]
    qk_h = [ra.t("qk_h0", [128, 512], BF16), ra.t("qk_h1", [128, 512], BF16)]
    qk_l = [ra.t("qk_l0", [128, 512], BF16), ra.t("qk_l1", [128, 512], BF16)]
    v_f = [ra.t("v_f0", [128, 1024], F32), ra.t("v_f1", [128, 1024], F32)]
    v_h = [ra.t("v_h0", [128, 1024], BF16), ra.t("v_h1", [128, 1024], BF16)]
    v_l = [ra.t("v_l0", [128, 1024], BF16), ra.t("v_l1", [128, 1024], BF16)]
    assert ra.cur <= ov_top

    def wsl(t, ki, mo):
        return t.ap()[:, ki * 1024 + mo * 128: ki * 1024 + (mo + 1) * 128]

    def wsl512(t, ki, h):
        return t.ap()[:, ki * 1024 + h * 512: ki * 1024 + (h + 1) * 512]

    def ysl(t, ki):
        return t.ap()[:, ki * 512:(ki + 1) * 512]

    def ysl_blk(t, ki, blk):
        return t.ap()[:, ki * 512 + blk * 128: ki * 512 + (blk + 1) * 128]

    x_free_tok = [None, None]
    psS_free = [None, None]
    psHO_free = [None, None]
    qk_busy = [()] * 2
    v_busy = [()] * 2
    gqk = 0
    gv_i = 0
    prev_ylo = None
    prev_ylo2 = None

    def split_halves(dst_t, src_t, hi_t, n, tok_hi):
        h = (n * 5) // 32 // 128 * 128
        p.wait("gpsimd", tok_hi)
        g1 = p.op("gpsimd", lambda e, d=dst_t.ap()[:, 0:h],
                  s=src_t.ap()[:, 0:h], hh=hi_t.ap()[:, 0:h]:
                  e.tensor_tensor(d, s, hh, op=OP.subtract), inc="gv")
        p.drain("vector")
        g2 = p.op("vector", lambda e, d=dst_t.ap()[:, h:n],
                  s=src_t.ap()[:, h:n], hh=hi_t.ap()[:, h:n]:
                  e.tensor_tensor(d, s, hh, op=OP.subtract), inc="vv")
        return g1, g2

    def pa_ln(u):
        xa_tm = xa_tm2[u % 2]
        p.wait("sync", x_free_tok[u % 2])
        if u == 0:
            tok_ldb = p.dma_group("gpsimd", [
                (xa_tm.ap()[:, blk * 1024:(blk + 1) * 1024],
                 xr3[u * 4 + blk]) for blk in range(2, 4)])
            tok_ldx = p.dma_group("sync", [
                (xa_tm.ap()[:, blk * 1024:(blk + 1) * 1024],
                 xr3[u * 4 + blk]) for blk in range(2)])
            return layernorm(xa_tm, bnA, mvA, rstA, rinvA,
                             [tok_ldx, tok_ldb])
        tok_ldx = p.dma_group("sync", [
            (xa_tm.ap()[:, blk * 1024:(blk + 1) * 1024], xr3[u * 4 + blk])
            for blk in range(4)])
        return layernorm(xa_tm, bnA, mvA, rstA, rinvA, [tok_ldx])

    def pa_front(u, tok_y):
        nonlocal prev_ylo, prev_ylo2
        xa_tm = xa_tm2[u % 2]
        p.wait("tensor", tok_y)
        p.wait("vector", prev_ylo)
        p.wait("vector", prev_ylo2)
        last_t = None
        for c in range(NCH):
            par = c % 2
            p.wait("tensor", psS_free[par])
            for blk in range(4):
                srcx = xa_tm.ap()[:, blk * 1024 + c * 128:
                                  blk * 1024 + (c + 1) * 128]
                last_t = p.op("tensor", lambda e,
                              d=psS[par].ap()[:, blk * 128:(blk + 1) * 128],
                              s=srcx: e.transpose(d, s, c_idf.ap()),
                              inc="tv" if blk == 3 else None)
            p.wait("vector", last_t)
            dst = yT.ap()[:, c * 512:(c + 1) * 512]
            psS_free[par] = p.op("vector", lambda e, d=dst,
                                 s=psS[par].ap()[:, 0:512]:
                                 e.tensor_copy(d, s), inc="vv")
        x_free_tok[u % 2] = last_t

        p.drain("vector")
        tok_yhi = p.op("vector", lambda e: e.tensor_copy(yT_hi.ap(), yT.ap()),
                       inc="vv")
        g1, g2 = split_halves(yT_lo, yT, yT_hi, 4096, tok_yhi)
        prev_ylo, prev_ylo2 = g1, g2
        return [tok_yhi, g1, g2]

    def pa_back(u, ytoks, wtoks):
        nonlocal gqk, gv_i
        wq_toks, wk_toks, wv_toks = wtoks
        p.waits("tensor", ytoks)

        for proj, whi, wlo, bias, sh_d, sl_d in (
                (0, wq_hi, wq_lo, c_bq, sqh, sql),
                (1, wk_hi, wk_lo, c_bk, skh, skl)):
            p.waits("tensor", wq_toks if proj == 0 else wk_toks)
            for mo in range(NCH):
                par = gqk % 2
                p.wait("tensor", psS_free[par])
                mms = hilo_mms(whi, wlo, yT_hi, yT_lo,
                               lambda t, ki, m=mo: wsl(t, ki, m), ysl)
                tok_mm = p.mm_group(psS[par].ap(), mms)
                for t in qk_busy[par]:
                    p.wait("vector", t)
                p.wait("vector", tok_mm)
                bcol = bias.ap()[:, mo:mo + 1]
                p.op("vector", lambda e, d=qk_f[par].ap(), s=psS[par].ap(),
                     bc=bcol: e.tensor_scalar(d, s, bc, None, op0=OP.add))
                p.drain("vector")
                psS_free[par] = p.op("vector", lambda e, d=qk_h[par].ap(),
                                     s=qk_f[par].ap(): e.tensor_copy(d, s),
                                     inc="vv")
                tok_hi = psS_free[par]
                p.wait("gpsimd", tok_hi)
                tok_lo = p.op("gpsimd", lambda e, d=qk_l[par].ap(),
                              s=qk_f[par].ap(), h=qk_h[par].ap():
                              e.tensor_tensor(d, s, h, op=OP.subtract), inc="gv")
                p.wait("scalar", tok_hi)
                st1 = p.dma("scalar", sh_d.ap()[mo][:, u * 512:(u + 1) * 512],
                            qk_h[par].ap())
                p.wait("scalar", tok_lo)
                st2 = p.dma("scalar", sl_d.ap()[mo][:, u * 512:(u + 1) * 512],
                            qk_l[par].ap())
                qk_busy[par] = (st1, st2, tok_lo)
                gqk += 1

        p.waits("tensor", wv_toks)
        for blk in range(4):
            par = gv_i % 2
            pst = psO[par]
            p.wait("tensor", psHO_free[par])
            mms = []
            for ki in range(NCH):
                yh = ysl_blk(yT_hi, ki, blk)
                yl = ysl_blk(yT_lo, ki, blk)
                for h in range(2):
                    mms.append((yh, wsl512(wv_hi, ki, h), h))
                    mms.append((yh, wsl512(wv_lo, ki, h), h))
                    mms.append((yl, wsl512(wv_hi, ki, h), h))
            halves = [pst.ap()[:, 0:512], pst.ap()[:, 512:1024]]
            tok_mm = p.mm_group_halves(halves, mms)
            for t in v_busy[par]:
                p.wait("vector", t)
            p.wait("vector", tok_mm)
            p.op("vector", lambda e, d=v_f[par].ap(), s=pst.ap(),
                 b_=c_bv.ap(): e.tensor_tensor(d, s, b_, op=OP.add))
            p.drain("vector")
            psHO_free[par] = p.op("vector", lambda e, d=v_h[par].ap(),
                                  s=v_f[par].ap(): e.tensor_copy(d, s),
                                  inc="vv")
            tok_hi = psHO_free[par]
            p.wait("gpsimd", tok_hi)
            tok_lo = p.op("gpsimd", lambda e, d=v_l[par].ap(),
                          s=v_f[par].ap(), h=v_h[par].ap():
                          e.tensor_tensor(d, s, h, op=OP.subtract), inc="gv")
            p.wait("scalar", tok_hi)
            st1 = p.dma("scalar", svh.ap()[u * 4 + blk], v_h[par].ap())
            p.wait("scalar", tok_lo)
            st2 = p.dma("scalar", svl.ap()[u * 4 + blk], v_l[par].ap())
            v_busy[par] = (st1, st2, tok_lo)
            gv_i += 1

    ty0 = pa_ln(0)
    ytoks0 = pa_front(0, ty0)
    stg_c, stg_s = [None] * 3, [None] * 3
    wq_toks = load_weight_sg(wd["wq"], wq_hi, wq_lo, stgA, stg_c, stg_s)
    wk_toks = load_weight_sg(wd["wk"], wk_hi, wk_lo, stgA, stg_c, stg_s)
    wv_toks = load_weight_sg(wd["wv"], wv_hi, wv_lo, stgA, stg_c, stg_s)
    wtoks = (wq_toks, wk_toks, wv_toks)
    ty_next = pa_ln(1) if NU > 1 else None
    pa_back(0, ytoks0, wtoks)
    for u in range(1, NU):
        yt = pa_front(u, ty_next)
        ty_next = pa_ln(u + 1) if u + 1 < NU else None
        pa_back(u, yt, wtoks)

    tok_pa_stores = p.all_dma_toks("scalar")

    TUC = 256
    NU2 = S // TUC
    rd = Region(nc, ov_base, ov_top)
    wk2_hi = rd.t("wk2_hi", [128, 8192], BF16)
    wk2_lo = rd.t("wk2_lo", [128, 8192], BF16)
    stgD = [rd.t("stgD0", [128, 1024], F32), rd.t("stgD1", [128, 1024], F32)]
    wr_hi = rd.t("wr_hi", [128, 8192], BF16)
    wr_lo = rd.t("wr_lo", [128, 8192], BF16)
    wv2_hi = rd.t("wv2_hi", [128, 8192], BF16)
    wv2_lo = rd.t("wv2_lo", [128, 8192], BF16)
    x1c = [rd.t("x1c0", [128, 2048], F32), rd.t("x1c1", [128, 2048], F32)]
    y2c = [rd.t("y2c0", [128, 2048], F32), rd.t("y2c1", [128, 2048], F32)]
    y2Tc = [rd.t("y2Tc0", [128, 257], F32), rd.t("y2Tc1", [128, 257], F32)]
    sliv = rd.t("sliv", [128, 8], F32)
    d1c = [rd.t("d1c0", [128, 256], F32), rd.t("d1c1", [128, 256], F32)]
    xrf = [rd.t("xrf0", [128, 256], F32), rd.t("xrf1", [128, 256], F32)]
    xkf = [rd.t("xkf0", [128, 256], F32), rd.t("xkf1", [128, 256], F32)]
    xr_hi = [rd.t("xr_hi0", [128, 2048], BF16), rd.t("xr_hi1", [128, 2048], BF16)]
    xr_lo = [rd.t("xr_lo0", [128, 2048], BF16), rd.t("xr_lo1", [128, 2048], BF16)]
    xk_hi = [rd.t("xk_hi0", [128, 2048], BF16), rd.t("xk_hi1", [128, 2048], BF16)]
    xk_lo = [rd.t("xk_lo0", [128, 2048], BF16), rd.t("xk_lo1", [128, 2048], BF16)]
    bnD = rd.t("bnD", [128, 24], F32)
    mvD = rd.t("mvD", [128, 4], F32)
    rstD = rd.t("rstD", [128, 2], F32)
    rinvD = rd.t("rinvD", [128, 2], F32)
    k2_hi = rd.t("k2_hi", [128, 2048], BF16)
    k2_lo = rd.t("k2_lo", [128, 2048], BF16)
    e_sb = [rd.t("e_sb0", [128, 256], F32), rd.t("e_sb1", [128, 256], F32)]
    t_sb = [rd.t("t_sb0", [128, 256], F32), rd.t("t_sb1", [128, 256], F32)]
    of32 = rd.t("of32", [128, 256], F32)
    xin = rd.t("xin", [128, 2048], F32)
    assert rd.cur <= ov_top

    def ln2_stats(x_tm, in_toks):
        p.waits("vector", in_toks)
        for blk in range(2):
            for h in range(2):
                srcb = x_tm.ap()[:, blk * 1024 + h * 512:
                                 blk * 1024 + (h + 1) * 512]
                dstb = bnD.ap()[:, (blk * 2 + h) * 6:(blk * 2 + h + 1) * 6]
                p.op("vector", lambda e, d=dstb, s=srcb: e.bn_stats(d, s))
        p.drain("vector")

    def ln2_norm(x_tm, out_tm):
        tok_mv = None
        for blk in range(2):
            srcb = bnD.ap()[:, blk * 12:(blk + 1) * 12]
            dstb = mvD.ap()[:, blk * 2:(blk + 1) * 2]
            tok_mv = p.op("vector", lambda e, d=dstb, s=srcb: e.bn_aggr(d, s),
                          inc="vv")
        p.wait("scalar", tok_mv)
        var_ap = bass.AP(mvD, 1, [[4, 128], [2, 2]])
        tok_sq = p.op("scalar", lambda e, d=rstD.ap(), s=var_ap:
                      e.activation(d, s, AF.Sqrt, bias=c_eps.ap()), inc="sv")
        p.wait("vector", tok_sq)
        p.op("vector", lambda e, d=rinvD.ap(), s=rstD.ap(): e.reciprocal(d, s))
        p.drain("vector")
        tok = None
        for blk in range(2):
            xa = x_tm.ap()[:, blk * 1024:(blk + 1) * 1024]
            da = out_tm.ap()[:, blk * 1024:(blk + 1) * 1024]
            mcol = mvD.ap()[:, blk * 2: blk * 2 + 1]
            rcol = rinvD.ap()[:, blk: blk + 1]
            tok = p.op("vector", lambda e, d=da, s=xa, m=mcol, r=rcol:
                       e.tensor_scalar(d, s, m, r, op0=OP.subtract,
                                       op1=OP.mult), inc="vv")
        return tok

    def ln2(x_tm, out_tm, in_toks):
        ln2_stats(x_tm, in_toks)
        return ln2_norm(x_tm, out_tm)

    y_store_par = [None, None]
    mix_free = [None, None]
    prev_split2 = [None, None]
    th2, tl2 = [None, None], [None, None]
    prev_k2lo = None
    k2_read_done = [None, None]
    tok_y2u = [None, None]
    ghd = 0

    def ysl256(t, ki):
        return t.ap()[:, ki * 256:(ki + 1) * 256]

    fa_ld = [None, None]
    fa_ldx = [None]

    def c2_front_a_load(u):
        par = u % 2
        p.waits("sync", acc_group[(u * 2) // 8])
        if u == 0:
            p.waits("sync", [("vv", p.cnt["vv"]), ("sv", p.cnt["sv"]),
                             ("gv", p.cnt["gv"]), ("tv", p.cnt["tv"])])
        else:
            p.waits("sync", [tok_y2u[1 - par]])
        fa_ldx[0] = p.dma_group("sync", [
            (xin.ap()[:, blk * 1024:(blk + 1) * 1024],
             sx1.ap()[u * 2 + blk]) for blk in range(2)])
        p.wait("sync", y_store_par[par])
        p.waits("sync", tok_kv_dead)
        fa_ld[par] = p.dma_group("sync", [
            (x1c[par].ap()[:, blk * 1024:(blk + 1) * 1024],
             sx1.ap()[u * 2 + blk]) for blk in range(2)])

    def c2_front_a_stats(u):
        par = u % 2
        if u == 0:
            p.waits("vector", tok_kv_dead)
        ln2_stats(xin, [fa_ldx[0]])

    def c2_front_a_norm(u):
        par = u % 2
        tok_y2u[par] = ln2_norm(xin, y2c[par])

    def c2_front_a(u):
        c2_front_a_load(u)
        c2_front_a_stats(u)
        c2_front_a_norm(u)

    def c2_front_b(u):
        par = u % 2
        split_last = []
        ch_split = {}

        def chunk(c):
            yc = y2Tc[c % 2]
            if c == 0:
                p.wait("tensor", tok_y2u[par])
                p.wait("vector", prev_k2lo)
            p.waits("vector", [mix_free[c % 2], tok_sliv])
            p.drain("vector")
            p.op("vector", lambda e, d=yc.ap()[:, 0:1],
                 s=sliv.ap()[:, c:c + 1]: e.tensor_copy(d, s))
            part = c % 2
            p.wait("tensor", psS_free[part])
            tt = None
            for blk in range(2):
                srcy = y2c[par].ap()[:, blk * 1024 + c * 128:
                                     blk * 1024 + (c + 1) * 128]
                tt = p.op("tensor", lambda e,
                          d=psS[part].ap()[:, blk * 128:(blk + 1) * 128],
                          s=srcy: e.transpose(d, s, c_idf.ap()),
                          inc="tv" if blk == 1 else None)
            p.wait("vector", tt)
            psS_free[part] = p.op("vector", lambda e, d=yc.ap()[:, 1:257],
                                  s=psS[part].ap()[:, 0:256]:
                                  e.tensor_copy(d, s), inc="vv")
            p.drain("vector")
            p.op("vector", lambda e, d=sliv.ap()[:, c:c + 1],
                 s=yc.ap()[:, 256:257]: e.tensor_copy(d, s))
            p.drain("vector")
            p.op("vector", lambda e, d=d1c[par].ap(), a=yc.ap()[:, 1:257],
                 b_=yc.ap()[:, 0:256]:
                 e.tensor_tensor(d, a, b_, op=OP.subtract))
            p.drain("vector")
            cp = c % 2
            p.wait("vector", prev_split2[cp])
            p.wait("vector", k2_read_done[par])
            p.op("vector", lambda e, d=xrf[cp].ap(), a=d1c[par].ap(),
                 mu=c_mur.ap()[:, c:c + 1], b_=yc.ap()[:, 0:256]:
                 e.scalar_tensor_tensor(d, a, mu, b_, op0=OP.mult, op1=OP.add))
            tok_mix = p.op("vector", lambda e, d=xkf[cp].ap(),
                           a=d1c[par].ap(),
                           mu=c_muk.ap()[:, c:c + 1],
                           b_=yc.ap()[:, 0:256]:
                           e.scalar_tensor_tensor(d, a, mu, b_, op0=OP.mult,
                                                  op1=OP.add), inc="vv")
            mix_free[c % 2] = tok_mix
            p.drain("vector")
            t1 = p.op("vector", lambda e,
                      d=xr_hi[par].ap()[:, c * 256:(c + 1) * 256],
                      s=xrf[cp].ap(): e.tensor_copy(d, s), inc="vv")
            t2 = p.op("vector", lambda e,
                      d=xk_hi[par].ap()[:, c * 256:(c + 1) * 256],
                      s=xkf[cp].ap(): e.tensor_copy(d, s), inc="vv")
            p.waits("gpsimd", [t1, t2])
            t3 = p.op("gpsimd", lambda e,
                      d=xr_lo[par].ap()[:, c * 256:(c + 1) * 256],
                      s=xrf[cp].ap(),
                      h=xr_hi[par].ap()[:, c * 256:(c + 1) * 256]:
                      e.tensor_tensor(d, s, h, op=OP.subtract), inc="gv")
            t4 = p.op("gpsimd", lambda e,
                      d=xk_lo[par].ap()[:, c * 256:(c + 1) * 256],
                      s=xkf[cp].ap(),
                      h=xk_hi[par].ap()[:, c * 256:(c + 1) * 256]:
                      e.tensor_tensor(d, s, h, op=OP.subtract), inc="gv")
            prev_split2[cp] = t4
            split_last[:] = [t1, t2, t3, t4]
            ch_split[c] = [t1, t2, t3, t4]
        return [lambda c=c: chunk(c) for c in range(NCH)], split_last, ch_split

    def c2_back(u, split_toks, chs, fillers=None):
        nonlocal ghd, prev_k2lo
        par = u % 2
        gidx = [0]

        def fill():
            if fillers is not None and gidx[0] % 3 == 2:
                ci = gidx[0] // 3
                if ci < NCH:
                    fillers[ci]()
            gidx[0] += 1

        p.waits("tensor", wtoks_wr)
        r_t = y2c[par]
        for mo in range(NCH):
            parh = ghd % 4
            p.wait("tensor", psH_free[parh])
            mms = hilo_mms(wr_hi, wr_lo, xr_hi[par], xr_lo[par],
                           lambda t, ki, m=mo: wsl(t, ki, m), ysl256)
            if mo == 0:
                pre = [None] * len(mms)
                for ki in range(NCH):
                    pre[3 * ki] = [chs[ki][0], chs[ki][2]]
                tok_mm = p.mm_group_w(psH[parh][:, 0:256], mms, pre)
            else:
                tok_mm = p.mm_group(psH[parh][:, 0:256], mms)
            p.wait("scalar", tok_mm)
            psH_free[parh] = p.op("scalar", lambda e,
                                  d=r_t.ap()[:, mo * 256:(mo + 1) * 256],
                                  s=psH[parh][:, 0:256]:
                                  e.activation(d, s, AF.Sigmoid), inc="sv")
            ghd += 1
            fill()
        if u == 0:
            p.waits("sync", tok_kv_dead)
            wv2_toks.extend(load_weight(wd["wv2"], wv2_hi, wv2_lo, stgD,
                                        stg_c2, stg_s2))
        p.waits("tensor", wk2_toks)
        k2s = {}
        for mo in range(NCH):
            mp = mo % 2
            parh = ghd % 4
            p.wait("tensor", psH_free[parh])
            mms = hilo_mms(wk2_hi, wk2_lo, xk_hi[par], xk_lo[par],
                           lambda t, ki, m=mo: wsl(t, ki, m), ysl256)
            if mo == 0:
                pre = [None] * len(mms)
                for ki in range(NCH):
                    pre[3 * ki] = [chs[ki][1], chs[ki][3]]
                tok_mm = p.mm_group_w(psH[parh][:, 0:256], mms, pre)
            else:
                tok_mm = p.mm_group(psH[parh][:, 0:256], mms)
            p.wait("scalar", tok_mm)
            p.waits("scalar", [th2[mp], tl2[mp]])
            p.drain("scalar")
            psH_free[parh] = p.op("scalar", lambda e, d=e_sb[mp].ap(),
                                  s=psH[parh][:, 0:256]:
                                  e.activation(d, s, AF.Relu), inc="sv")
            p.drain("scalar")
            tok_t = p.op("scalar", lambda e, d=t_sb[mp].ap(),
                         s=e_sb[mp].ap():
                         e.activation(d, s, AF.Square), inc="sv")
            p.drain("scalar")
            p.wait("vector", tok_t)
            th = p.op("vector", lambda e,
                      d=k2_hi.ap()[:, mo * 256:(mo + 1) * 256],
                      s=t_sb[mp].ap(): e.tensor_copy(d, s), inc="vv")
            p.wait("gpsimd", th)
            tl = p.op("gpsimd", lambda e,
                      d=k2_lo.ap()[:, mo * 256:(mo + 1) * 256],
                      s=t_sb[mp].ap(),
                      h=k2_hi.ap()[:, mo * 256:(mo + 1) * 256]:
                      e.tensor_tensor(d, s, h, op=OP.subtract), inc="gv")
            th2[mp], tl2[mp] = th, tl
            k2s[mo] = [th, tl]
            ghd += 1
            fill()
        prev_k2lo = tl
        k2_read_done[par] = None
        p.waits("tensor", wv2_toks)
        p.waits("vector", [fa_ld[par]])
        ev_last = None
        for mo in range(NCH):
            parh = ghd % 4
            p.wait("tensor", psH_free[parh])
            mms = hilo_mms(wv2_hi, wv2_lo, k2_hi, k2_lo,
                           lambda t, ki, m=mo: wsl(t, ki, m), ysl256)
            if mo == 0:
                pre = [None] * len(mms)
                for ki in range(NCH):
                    pre[3 * ki] = k2s[ki]
                tok_mm = p.mm_group_w(psH[parh][:, 0:256], mms, pre)
            else:
                tok_mm = p.mm_group(psH[parh][:, 0:256], mms)
            p.waits("vector", [tok_mm])
            tok_of = p.op("vector", lambda e, d=of32.ap(),
                          s=psH[parh][:, 0:256],
                          r=r_t.ap()[:, mo * 256:(mo + 1) * 256]:
                          e.tensor_tensor(d, s, r, op=OP.mult), inc="vv")
            psH_free[parh] = tok_of
            p.waits("tensor", [tok_of])
            part = ghd % 2
            p.wait("tensor", psS_free[part])
            tt = None
            for blk in range(2):
                srco2 = of32.ap()[:, blk * 128:(blk + 1) * 128]
                tt = p.op("tensor", lambda e,
                          d=psS[part].ap()[:, blk * 128:(blk + 1) * 128],
                          s=srco2: e.transpose(d, s, c_idf.ap()),
                          inc="tv" if blk == 1 else None)
            p.wait("vector", tt)
            xs = bass.AP(x1c[par], mo * 128, [[2048, 128], [1024, 2], [1, 128]])
            ps3 = bass.AP(psS[part], 0, [[512, 128], [128, 2], [1, 128]])
            ev_last = p.op("vector", lambda e, d=xs, s=ps3:
                           e.tensor_tensor(d, s, d, op=OP.add), inc="vv")
            psS_free[part] = ev_last
            ghd += 1
            fill()
        k2_read_done[par] = ("tv", p.cnt["tv"])
        p.wait("scalar", ev_last)
        y_store_par[par] = p.dma_group("scalar", [
            (yr3[u * 2 + blk], x1c[par].ap()[:, blk * 1024:(blk + 1) * 1024])
            for blk in range(2)])


    r_wo = Region(nc, ov_base, ov_top)
    wo_hi = r_wo.t("wo_hi", [128, 8192], BF16)
    wo_lo = r_wo.t("wo_lo", [128, 8192], BF16)
    stgC = [r_wo.t("stgC0", [128, 1024], F32), r_wo.t("stgC1", [128, 1024], F32)]
    ov_base2 = r_wo.cur
    rb = Region(nc, ov_base2, ov_top)
    q_addr = (rb.cur + 63) & ~63
    qhiT = rb.t("qhiT", [128, 8192], BF16)
    qloT = rb.t("qloT", [128, 8192], BF16)
    o_hi = nc.alloc_sbuf_tensor_at("o_hi", [128, 8192], BF16, offset=q_addr)
    o_lo = nc.alloc_sbuf_tensor_at("o_lo", [128, 8192], BF16,
                                   offset=q_addr + 16384)
    khiT = [rb.t("khiT0", [128, 4096], BF16), rb.t("khiT1", [128, 4096], BF16)]
    kloT = [rb.t("kloT0", [128, 4096], BF16), rb.t("kloT1", [128, 4096], BF16)]
    vhiT = [rb.t("vhiT0", [128, 4096], BF16), rb.t("vhiT1", [128, 4096], BF16)]
    vloT = [rb.t("vloT0", [128, 4096], BF16), rb.t("vloT1", [128, 4096], BF16)]
    o_addr = (rb.cur + 63) & ~63
    o_sb = rb.t("o_sb", [128, 8192], F32)
    oTb_hi = nc.alloc_sbuf_tensor_at("oTb_hi", [128, 8192], BF16,
                                     offset=o_addr)
    oTb_lo = nc.alloc_sbuf_tensor_at("oTb_lo", [128, 8192], BF16,
                                     offset=o_addr + 16384)
    ab_addr = (rb.cur + 63) & ~63
    at_bk = [rb.t("at_bk0", [128, 512], F32), rb.t("at_bk1", [128, 512], F32)]
    at_bk2 = [rb.t("at_bk20", [128, 512], F32), rb.t("at_bk21", [128, 512], F32)]
    a_sb = [rb.t("a_sb0", [128, 512], F32), rb.t("a_sb1", [128, 512], F32),
            rb.t("a_sb2", [128, 512], F32), rb.t("a_sb3", [128, 512], F32)]
    p_sb = [rb.t("p_sb0", [128, 512], F32), rb.t("p_sb1", [128, 512], F32)]
    phi = [rb.t("phi0", [128, 512], BF16), rb.t("phi1", [128, 512], BF16)]
    plo = [rb.t("plo0", [128, 512], BF16), rb.t("plo1", [128, 512], BF16)]
    phiT = [rb.t("phiT0", [128, 512], BF16), rb.t("phiT1", [128, 512], BF16)]
    ploT = [rb.t("ploT0", [128, 512], BF16), rb.t("ploT1", [128, 512], BF16)]
    nmr = rb.t("nmr", [128, 16], F32)
    js = nc.alloc_sbuf_tensor_at("js", [128, 512], F32, offset=ab_addr)
    lr = rb.t("lr", [128, 8], F32)
    nmj = rb.t("nmj", [128, 1], F32)
    nmnew = rb.t("nmnew", [128, 2], F32)
    nm32 = rb.t("nm32", [128, 1], F32)
    alph = [rb.t("alph0", [128, 1], F32), rb.t("alph1", [128, 1], F32)]
    lpart = [rb.t("lpart0", [128, 1], F32), rb.t("lpart1", [128, 1], F32)]
    linv = rb.t("linv", [128, 1], F32)
    assert rb.cur <= ov_top

    p.waits("sync", tok_pa_stores)

    kv_busy = [None, None]
    psT_free = [None, None]
    psO_freeB = [None, None]
    psH_free = [None, None, None, None]
    wtoks_o = []
    at_free2 = [None, None]
    acc_toks = [None, None]
    ghb = 0
    tok_n = None
    alpha_busy = [None, None]

    kv_toks = {}

    def load_q(g):
        return p.dma_group("sync", [
            (qhiT.ap()[:, c * 1024:(c + 1) * 1024],
             sqh.ap()[c][:, g * QG:(g + 1) * QG]) for c in range(NCH)] + [
            (qloT.ap()[:, c * 1024:(c + 1) * 1024],
             sql.ap()[c][:, g * QG:(g + 1) * QG]) for c in range(NCH)])

    def load_kv(g, j, q="sync"):
        par = j % 2
        p.wait(q, kv_busy[par])
        prs = []
        for c in range(NCH):
            prs.append((khiT[par].ap()[:, c * 512:(c + 1) * 512],
                        skh.ap()[c][:, j * 512:(j + 1) * 512]))
            prs.append((kloT[par].ap()[:, c * 512:(c + 1) * 512],
                        skl.ap()[c][:, j * 512:(j + 1) * 512]))
        for kb in range(4):
            prs.append((vhiT[par].ap()[:, kb * 1024:(kb + 1) * 1024],
                        svh.ap()[j * 4 + kb]))
            prs.append((vloT[par].ap()[:, kb * 1024:(kb + 1) * 1024],
                        svl.ap()[j * 4 + kb]))
        kv_toks[(g, j)] = p.dma_group(q, prs)

    tok_q_next = load_q(0)
    p.waits("gpsimd", tok_pa_stores)
    load_kv(0, 0, q="gpsimd")
    prev_c1_free = None
    acc_group = []
    oT_done_last = [None]
    stg_c2, stg_s2 = [None, None], [None, None]
    wtoks_o.extend(load_weight(wd["wo"], wo_hi, wo_lo, stgC,
                               stg_c2, stg_s2))
    c2_hooked = [False]
    wtoks_wr = []

    for g in range(NG):
        nkc = KCH_PER_G * (g + 1)
        tok_q = tok_q_next

        sub_ph = [0] * SUBS
        pairs = []
        for j in range(nkc):
            smin = max(0, 4 * j - 8 * g)
            for s in range(smin, SUBS):
                qi = g * SUBS + s
                n_k = min(512, (qi + 1) * 128 - j * 512)
                pairs.append((j, s, n_k))

        def emit_qk(i):
            j, s, n_k = pairs[i]
            par = i % 2
            kpar = j % 2
            p.waits("tensor", [tok_q, kv_toks[(g, j)], psS_free[par]])
            mms = []
            for ki in range(NCH):
                qh = qhiT.ap()[:, ki * 1024 + s * 128: ki * 1024 + (s + 1) * 128]
                ql = qloT.ap()[:, ki * 1024 + s * 128: ki * 1024 + (s + 1) * 128]
                kh = khiT[kpar].ap()[:, ki * 512: ki * 512 + n_k]
                kl = kloT[kpar].ap()[:, ki * 512: ki * 512 + n_k]
                mms += [(qh, kh), (qh, kl), (ql, kh)]
            return p.mm_group(psS[par].ap()[:, 0:n_k], mms)

        def emit_abs(i, tok_s):
            j, s, n_k = pairs[i]
            par = i % 2
            p.wait("scalar", tok_s)
            return p.op("scalar", lambda e, d=a_sb[i % 4].ap()[:, 0:n_k],
                        s_=psS[par].ap()[:, 0:n_k]:
                        e.activation(d, s_, AF.Abs), inc="sv")

        def emit_rest(i, tok_a):
            j, s, n_k = pairs[i]
            par = i % 2
            kpar = j % 2
            qi = g * SUBS + s
            diag = (qi + 1) * 128 - j * 512 <= 512
            nkb = n_k // 128
            first = j == 0
            p.wait("vector", tok_a)
            if diag:
                p.op("vector", lambda e,
                     d=a_sb[i % 4].ap()[:, n_k - 128:n_k]:
                     e.tensor_tensor(d, d, c_mask.ap(), op=OP.add))
                p.drain("vector")
            ph = sub_ph[s]
            c_old = 2 * s + ph
            c_new = 2 * s + (1 - ph)
            sub_ph[s] = 1 - ph
            if first:
                p.op("vector", lambda e, d=nmr.ap()[:, c_new:c_new + 1],
                     s_=a_sb[i % 4].ap()[:, 0:n_k]:
                     e.tensor_reduce(d, s_, axis=AX.X, op=OP.max, negate=True))
                p.drain("vector")
                tok_stats = p.op("vector", lambda e, d=nm32.ap(),
                                 s_=nmr.ap()[:, c_new:c_new + 1]:
                                 e.tensor_scalar(d, s_, 1.0 / 32, None,
                                                 op0=OP.mult), inc="vv")
                tok_alpha = None
            else:
                p.op("vector", lambda e, j_=js.ap()[:, 0:n_k],
                     s_=a_sb[i % 4].ap()[:, 0:n_k],
                     sd=nmr.ap()[:, c_old:c_old + 1],
                     ac=nmr.ap()[:, c_new:c_new + 1]:
                     e.tensor_scalar(j_, s_, -1.0, sd, op0=OP.mult, op1=OP.min,
                                     accum_out=ac))
                p.drain("vector")
                p.op("vector", lambda e, d=nmnew.ap()[:, 1:2],
                     a=nmr.ap()[:, c_new:c_new + 1],
                     b_=nmr.ap()[:, c_old:c_old + 1]:
                     e.tensor_tensor(d, a, b_, op=OP.subtract))
                tok_stats = p.op("vector", lambda e, d=nm32.ap(),
                                 s_=nmr.ap()[:, c_new:c_new + 1]:
                                 e.tensor_scalar(d, s_, 1.0 / 32, None,
                                                 op0=OP.mult), inc="vv")
            p.wait("scalar", tok_stats)
            p.wait("scalar", alpha_busy[par])
            p.drain("scalar")
            tok_alpha = None
            if not first:
                tok_alpha = p.op("scalar", lambda e, d=alph[par].ap(),
                                 s_=nmnew.ap()[:, 1:2]:
                                 e.activation(d, s_, AF.Exp, scale=1.0 / 32),
                                 inc="sv")
            tok_p = p.op("scalar", lambda e, d=p_sb[par].ap()[:, 0:n_k],
                         s_=a_sb[i % 4].ap()[:, 0:n_k], b_=nm32.ap(),
                         ac=lpart[par].ap():
                         e.activation(d, s_, AF.Exp, scale=1.0 / 32, bias=b_,
                                      accum_out=ac), inc="sv")
            p.wait("vector", tok_p)
            tok_phi = p.op("vector", lambda e, d=phi[par].ap()[:, 0:n_k],
                           s_=p_sb[par].ap()[:, 0:n_k]: e.tensor_copy(d, s_),
                           inc="vv")
            p.wait("gpsimd", tok_phi)
            tok_plo = p.op("gpsimd", lambda e, d=plo[par].ap()[:, 0:n_k],
                           s_=p_sb[par].ap()[:, 0:n_k],
                           h=phi[par].ap()[:, 0:n_k]:
                           e.tensor_tensor(d, s_, h, op=OP.subtract), inc="gv")
            p.waits("tensor", [tok_phi, psT_free[0]])
            tk1 = None
            for kb in range(nkb):
                src = phi[par].ap()[:, kb * 128:(kb + 1) * 128]
                dst = psT[0].ap()[:, kb * 128:(kb + 1) * 128]
                tk1 = p.op("tensor", lambda e, d=dst, s_=src:
                           e.transpose(d, s_, c_idb.ap()),
                           inc="tv" if kb == nkb - 1 else None)
            p.waits("tensor", [tok_plo, psT_free[1]])
            tk2 = None
            for kb in range(nkb):
                src = plo[par].ap()[:, kb * 128:(kb + 1) * 128]
                dst = psT[1].ap()[:, kb * 128:(kb + 1) * 128]
                tk2 = p.op("tensor", lambda e, d=dst, s_=src:
                           e.transpose(d, s_, c_idb.ap()),
                           inc="tv" if kb == nkb - 1 else None)
            p.wait("vector", tk1)
            psT_free[0] = p.op("vector", lambda e,
                               d=phiT[par].ap()[:, 0:n_k],
                               s_=psT[0].ap()[:, 0:n_k]: e.tensor_copy(d, s_),
                               inc="vv")
            p.wait("vector", tk2)
            psT_free[1] = p.op("vector", lambda e,
                               d=ploT[par].ap()[:, 0:n_k],
                               s_=psT[1].ap()[:, 0:n_k]: e.tensor_copy(d, s_),
                               inc="vv")
            opar = i % 2
            p.waits("tensor", [psT_free[0], psT_free[1], kv_toks[(g, j)],
                               psO_freeB[opar]])
            mms = []
            for kb in range(nkb):
                ph = phiT[par].ap()[:, kb * 128:(kb + 1) * 128]
                pl = ploT[par].ap()[:, kb * 128:(kb + 1) * 128]
                for h in range(2):
                    vh = vhiT[kpar].ap()[:, kb * 1024 + h * 512:
                                         kb * 1024 + (h + 1) * 512]
                    vl = vloT[kpar].ap()[:, kb * 1024 + h * 512:
                                         kb * 1024 + (h + 1) * 512]
                    mms.append((ph, vh, h))
                    mms.append((ph, vl, h))
                    mms.append((pl, vh, h))
            halves = [psO[opar].ap()[:, 0:512], psO[opar].ap()[:, 512:1024]]
            tok_pv = p.mm_group_halves(halves, mms)
            p.waits("vector", [tok_pv, tok_alpha, tok_p])
            lsl = lr.ap()[:, s:s + 1]
            dsl = o_sb.ap()[:, s * 1024:(s + 1) * 1024]
            if first:
                if prev_c1_free is not None:
                    p.waits("vector", prev_c1_free)
                psO_freeB[opar] = p.op("vector", lambda e, d=dsl,
                                       ps_=psO[opar].ap():
                                       e.tensor_copy(d, ps_), inc="vv")
                alpha_busy[par] = p.op(
                    "vector", lambda e, d=lsl, lp=lpart[par].ap():
                    e.tensor_copy(d, lp), inc="vv")
            else:
                psO_freeB[opar] = p.op("vector", lambda e, d=dsl,
                                       al=alph[par].ap(), ps_=psO[opar].ap():
                                       e.scalar_tensor_tensor(
                                           d, d, al, ps_, op0=OP.mult,
                                           op1=OP.add), inc="vv")
                alpha_busy[par] = p.op(
                    "vector", lambda e, d=lsl, al=alph[par].ap(),
                    lp=lpart[par].ap():
                    e.scalar_tensor_tensor(d, d, al, lp, op0=OP.mult,
                                           op1=OP.add), inc="vv")

        npairs = len(pairs)
        tokS = [None] * npairs
        tokA = [None] * npairs
        tv_after_qk = None
        tokS[0] = emit_qk(0)
        tokA[0] = emit_abs(0, tokS[0])
        psS_free[0] = tokA[0]
        for ii in (1, 2):
            if ii < npairs:
                jn = pairs[ii][0]
                if (g, jn) not in kv_toks:
                    load_kv(g, jn)
                tokS[ii] = emit_qk(ii)
                if ii == 1:
                    tokA[1] = emit_abs(1, tokS[1])
                    psS_free[1] = tokA[1]
        for i in range(npairs):
            j, s, n_k = pairs[i]
            if i + 3 < npairs:
                jn3 = pairs[i + 3][0]
                if (g, jn3) not in kv_toks:
                    load_kv(g, jn3)
                tokS[i + 3] = emit_qk(i + 3)
            if i + 2 < npairs:
                tokA[i + 2] = emit_abs(i + 2, tokS[i + 2])
                psS_free[i % 2] = tokA[i + 2]
            if i + 3 >= npairs:
                tv_after_qk = ("tv", p.cnt["tv"])
            emit_rest(i, tokA[i])
            qi = g * SUBS + s
            if (qi + 1) * 128 - j * 512 <= 512:
                p.drain("vector")
                p.op("vector", lambda e, d=linv.ap(), s_=lr.ap()[:, s:s + 1]:
                     e.reciprocal(d, s_))
                p.drain("vector")
                tok_n = p.op("vector", lambda e,
                             d=o_sb.ap()[:, s * 1024:(s + 1) * 1024],
                             li=linv.ap():
                             e.tensor_scalar(d, d, li, None, op0=OP.mult),
                             inc="vv")
                p.drain("vector")
            if i + 1 == len(pairs) or pairs[i + 1][0] != j:
                kv_busy[j % 2] = ("tv", p.cnt["tv"])
        if g + 1 < NG:
            load_kv(g + 1, 0)
            load_kv(g + 1, 1)
        for k in range(4):
            psH_free[k] = ("vv", p.cnt["vv"])
        tok_base = p.dma_group("sync", [
            (sx1.ap()[g * 8 + b], xr3[g * 8 + b]) for b in range(8)])
        p.waits("scalar", [tok_n, tv_after_qk])
        p.waits("gpsimd", [tok_n, tv_after_qk])
        chunk_hi = [None] * NCH
        chunk_lo = [None] * NCH
        for c in range(NCH):
            sap3 = bass.AP(o_sb, c * 128, [[8192, 128], [1024, 8], [1, 128]])
            hap3 = bass.AP(o_hi, c * 128, [[8192, 128], [1024, 8], [1, 128]])
            lap3 = bass.AP(o_lo, c * 128, [[8192, 128], [1024, 8], [1, 128]])
            p.drain("scalar")
            chunk_hi[c] = p.op("scalar", lambda e, d=hap3, s_=sap3:
                               e.copy(d, s_), inc="sv")
            p.wait("gpsimd", chunk_hi[c])
            chunk_lo[c] = p.op("gpsimd", lambda e, d=lap3, s_=sap3, h=hap3:
                               e.tensor_tensor(d, s_, h, op=OP.subtract),
                               inc="gv")
        ev_toks = []
        for src_t, toks in ((o_hi, chunk_hi), (o_lo, chunk_lo)):
            dst_t = oTb_hi if src_t is o_hi else oTb_lo
            for c in range(NCH):
                tp = len(ev_toks) % 2
                p.waits("tensor", [toks[c], psT_free[tp]])
                tt = None
                for b in range(8):
                    sap = src_t.ap()[:, b * 1024 + c * 128:
                                     b * 1024 + (c + 1) * 128]
                    tt = p.op("tensor", lambda e,
                              d=psT[tp].ap()[:, b * 128:(b + 1) * 128],
                              s_=sap: e.transpose(d, s_, c_idb.ap()),
                              inc="tv" if b == 7 else None)
                p.wait("vector", tt)
                p.wait("vector", chunk_lo[NCH - 1])
                dap = dst_t.ap()[:, c * 1024:(c + 1) * 1024]
                psT_free[tp] = p.op("vector", lambda e, d=dap,
                                    s_=psT[tp].ap()[:, 0:1024]:
                                    e.tensor_copy(d, s_), inc="vv")
                ev_toks.append(psT_free[tp])
        tok_oT_done = ("tv", p.cnt["tv"])
        p.waits("tensor", wtoks_o + [ev_toks[-1]])
        for h in range(2):
            for mo in range(NCH):
                parh = ghb % 4
                p.wait("tensor", psH_free[parh])
                mms = hilo_mms(wo_hi, wo_lo, oTb_hi, oTb_lo,
                               lambda t, ki, m=mo: wsl(t, ki, m),
                               lambda t, ki, hh=h:
                               t.ap()[:, ki * 1024 + hh * 512:
                                      ki * 1024 + (hh + 1) * 512])
                tok_mm = p.mm_group(psH[parh], mms)
                para = ghb % 2
                p.waits("vector", [tok_mm, at_free2[para]])
                bcol = c_bo.ap()[:, mo:mo + 1]
                tok_at = p.op("vector", lambda e, d=at_bk[para].ap(),
                              s_=psH[parh], bc=bcol:
                              e.tensor_scalar(d, s_, bc, None, op0=OP.add),
                              inc="vv")
                psH_free[parh] = tok_at
                p.waits("tensor", [tok_at])
                part = ghb % 2
                p.wait("tensor", psS_free[part])
                tt = None
                for blk in range(4):
                    sap = at_bk[para].ap()[:, blk * 128:(blk + 1) * 128]
                    tt = p.op("tensor", lambda e,
                              d=psS[part].ap()[:, blk * 128:(blk + 1) * 128],
                              s_=sap: e.transpose(d, s_, c_idf.ap()),
                              inc="tv" if blk == 3 else None)
                at_free2[para] = tt
                p.wait("vector", tt)
                p.wait("vector", acc_toks[para])
                tok_ev = p.op("vector", lambda e, d=at_bk2[para].ap(),
                              s_=psS[part].ap()[:, 0:512]:
                              e.tensor_copy(d, s_), inc="vv")
                psS_free[part] = tok_ev
                p.wait("gpsimd", tok_base)
                p.wait("gpsimd", tok_ev)
                dst3 = bass.AP(sx1, (g * 8 + h * 4) * 128 * 1024 + mo * 128,
                               [[1024, 128], [128 * 1024, 4], [1, 128]])
                src3 = bass.AP(at_bk2[para], 0, [[512, 128], [128, 4], [1, 128]])
                acc_toks[para] = p.dma_group("gpsimd", [(dst3, src3)],
                                             accum=True)
                ghb += 1
                if g == NG - 1 and g >= 1:
                    slot = h * NCH + mo
                    if 1 <= slot <= 8:
                        if slot == 1:
                            p.wait("sync", tok_oT_done)
                        weight_piece(wd["wr"], wr_hi, wr_lo, stgD,
                                     stg_c2, stg_s2, slot - 1)
                        if slot == 8:
                            wtoks_wr = stg_c2[:] + stg_s2[:]
                            tok_kv_dead = [kv_busy[0], kv_busy[1]]
                            c2_front_a_load(0)
                            c2_hooked[0] = True
                    elif slot == 10:
                        c2_front_a_stats(0)
                    elif slot == 12:
                        c2_front_a_norm(0)
        prev_c1_free = [("vv", p.cnt["vv"]), ("tv", p.cnt["tv"])]
        psO_freeB[0] = ("vv", p.cnt["vv"])
        psO_freeB[1] = ("vv", p.cnt["vv"])
        acc_group.append([t for t in acc_toks if t is not None])
        oT_done_last[0] = tok_oT_done
        if g + 1 < NG:
            p.wait("sync", tok_oT_done)
            tok_q_next = load_q(g + 1)

    tok_pb_stores = p.all_dma_toks("scalar") + p.all_dma_toks("gpsimd")
    tok_kv_dead = [kv_busy[0], kv_busy[1]]
    tok_q_dead = oT_done_last[0]
    tok_wo_dead = ("tv", p.cnt["tv"])

    p.wait("vector", tok_wo_dead)
    tok_sliv = p.op("vector", lambda e: e.memset(sliv.ap(), 0.0), inc="vv")
    p.drain("vector")

    wk2_toks = []
    wv2_toks = []
    if not c2_hooked[0]:
        c2_front_a(0)
        p.wait("sync", tok_q_dead)
        wtoks_wr = load_weight(wd["wr"], wr_hi, wr_lo, stgD, stg_c2, stg_s2)
    ch0, s0, cm0 = c2_front_b(0)
    p.waits("gpsimd", acc_group[-1] + [tok_wo_dead])
    for f in ch0:
        f()
    p.wait("sync", tok_wo_dead)
    wk2_toks = load_weight_sg(wd["wk2"], wk2_hi, wk2_lo, stgD,
                              stg_c2, stg_s2)
    splits = {0: list(s0)}
    chmaps = {0: cm0}
    for u in range(1, NU2):
        c2_front_a(u)
        chu, su, cmu = c2_front_b(u)
        c2_back(u - 1, splits[u - 1], chmaps[u - 1], fillers=chu)
        splits[u] = list(su)
        chmaps[u] = cmu
    c2_back(NU2 - 1, splits[NU2 - 1], chmaps[NU2 - 1])

    p.waits("sync", p.all_dma_toks("scalar"))
    p.waits("sync", p.all_dma_toks("sync"))

    with nc.Block() as block:
        @block.sync
        def _(eng):
            for name, fn in p.events:
                if name == "sync":
                    fn(eng)

        @block.tensor
        def _(eng):
            for name, fn in p.events:
                if name == "tensor":
                    fn(eng)

        @block.vector
        def _(eng):
            for name, fn in p.events:
                if name == "vector":
                    fn(eng)

        @block.scalar
        def _(eng):
            for name, fn in p.events:
                if name == "scalar":
                    fn(eng)

        @block.gpsimd
        def _(eng):
            for name, fn in p.events:
                if name == "gpsimd":
                    fn(eng)

    return nc


def _host_prep(inputs, S):
    f32 = np.float32
    w = {}
    w["wqt"] = np.ascontiguousarray(inputs["L0_attn_Wq"].T.astype(f32))
    w["wkt"] = np.ascontiguousarray(inputs["L0_attn_Wk"].T.astype(f32))
    w["wvt"] = np.ascontiguousarray(inputs["L0_attn_Wv"].T.astype(f32))
    w["wot"] = np.ascontiguousarray(inputs["L0_attn_Wo"].T.astype(f32))
    w["wrt"] = np.ascontiguousarray(inputs["L0_ffn_W_r"].T.astype(f32))
    w["wk2t"] = np.ascontiguousarray(inputs["L0_ffn_W_k"].T.astype(f32))
    w["wv2t"] = np.ascontiguousarray(inputs["L0_ffn_W_v"].T.astype(f32))

    def fm(v):
        return np.ascontiguousarray(np.asarray(v, f32).reshape(8, 128).T)

    w["bqf"] = fm(inputs["L0_attn_bq"])
    w["bkf"] = fm(inputs["L0_attn_bk"])
    w["bof"] = fm(inputs["L0_attn_bo"])
    w["murf"] = fm(inputs["L0_ffn_mu_r"])
    w["mukf"] = fm(inputs["L0_ffn_mu_k"])
    w["bvb"] = np.ascontiguousarray(
        np.broadcast_to(np.asarray(inputs["L0_attn_bv"], f32)[None, :],
                        (128, D)))
    w["idb"] = np.eye(128, dtype=ml_dtypes.bfloat16)
    w["epsc"] = np.full((128, 1), 1e-5, f32)
    w["idf"] = np.eye(128, dtype=f32)
    mask = np.zeros((128, 128), f32)
    iu = np.triu_indices(128, 1)
    mask[iu] = -1.0e30
    w["mask"] = mask
    return w


def kernel(**inputs):
    x = np.asarray(inputs["L0_attn_x"], np.float32)
    Bsz, S, _ = x.shape
    assert Bsz == 8
    nc = build_core(S)
    shared = _host_prep(inputs, S)
    in_maps = []
    for b in range(Bsz):
        m = dict(shared)
        m["x"] = np.ascontiguousarray(x[b])
        in_maps.append(m)
    res = run_bass_kernel_spmd(nc, in_maps, core_ids=list(range(8)))
    return np.stack([np.asarray(r["y"], np.float32) for r in res.results],
                    axis=0)


# version 54
# speedup vs baseline: 1.0113x; 1.0113x over previous
import numpy as np
import ml_dtypes
import concourse.bass as bass
import concourse.mybir as mybir
from concourse.bass_utils import run_bass_kernel_spmd

F32 = mybir.dt.float32
BF16 = mybir.dt.bfloat16
AF = mybir.ActivationFunctionType
OP = mybir.AluOpType
AX = mybir.AxisListType

D = 1024
NCH = 8
TU = 512


class Prog:

    NPOOL = 6

    def __init__(self, nc):
        self.nc = nc
        self.events = []
        self.cnt = {}
        self.sems = {}
        self.last_wait = {}
        self._drained = {}
        names = ["tv", "vv", "sv", "gv"]
        self.pool = {"sync": [], "scalar": [], "gpsimd": []}
        for q in ("sync", "scalar", "gpsimd"):
            for i in range(self.NPOOL):
                n = f"dma_{q}_{i}"
                names.append(n)
                self.pool[q].append(n)
        for s in names:
            self.sems[s] = nc.alloc_semaphore(s)
            self.cnt[s] = 0
        self.pool_idx = {"sync": 0, "scalar": 0, "gpsimd": 0}
        self.DMA_SEMS = tuple(n for n in names if n.startswith("dma_"))

    def ev(self, eng, fn):
        self.events.append((eng, fn))
        self._drained[eng] = False

    def op(self, eng, fn, inc=None):
        if inc is None:
            self.ev(eng, fn)
            return None
        n = 16 if inc in self.DMA_SEMS else 1
        self.cnt[inc] += n
        tgt = self.cnt[inc]

        def run(e, f=fn, s=self.sems[inc], k=n):
            f(e).then_inc(s, k)

        self.ev(eng, run)
        return (inc, tgt)

    def dma_group(self, eng, pairs, accum=False):
        sem = self.pool[eng][self.pool_idx[eng] % self.NPOOL]
        self.pool_idx[eng] += 1
        if self.cnt[sem] > 0:
            self.wait(eng, (sem, self.cnt[sem]))
        tok = None
        for out, in_ in pairs:
            if accum:
                tok = self.op(eng, lambda e, o=out, i=in_:
                              e.dma_start(out=o, in_=i, accum_op=OP.add),
                              inc=sem)
            else:
                tok = self.op(eng, lambda e, o=out, i=in_:
                              e.dma_start(out=o, in_=i), inc=sem)
        return tok

    def dma(self, eng, out, in_):
        return self.dma_group(eng, [(out, in_)])

    def all_dma_toks(self, eng):
        return [(s, self.cnt[s]) for s in self.pool[eng] if self.cnt[s] > 0]

    def wait(self, eng, tok):
        if tok is None:
            return
        sem, val = tok
        k = (eng, sem)
        if self.last_wait.get(k, 0) >= val:
            return
        self.last_wait[k] = val
        self.ev(eng, lambda e, s=self.sems[sem], v=val: e.wait_ge(s, v))

    def waits(self, eng, toks):
        for t in toks:
            self.wait(eng, t)

    def drain(self, eng):
        if self._drained.get(eng):
            return
        self.events.append((eng, lambda e: e.drain()))
        self._drained[eng] = True

    def mm_group(self, psum_ap, mms):
        n = len(mms)
        tok = None
        for i, (w, m) in enumerate(mms):
            tok = self.op("tensor", lambda e, o=psum_ap, ww=w, mm=m,
                          s=(i == 0), sp=(i == n - 1):
                          e.matmul(o, ww, mm, start=s, stop=sp),
                          inc="tv" if i == n - 1 else None)
        return tok

    def mm_group_w(self, psum_ap, mms, pre):
        n = len(mms)
        tok = None
        for i, (w, m) in enumerate(mms):
            if pre[i]:
                self.waits("tensor", pre[i])
            tok = self.op("tensor", lambda e, o=psum_ap, ww=w, mm=m,
                          s=(i == 0), sp=(i == n - 1):
                          e.matmul(o, ww, mm, start=s, stop=sp),
                          inc="tv" if i == n - 1 else None)
        return tok

    def mm_group_halves(self, half_aps, mms):
        firsts, lasts = {}, {}
        for i, (_, _, h) in enumerate(mms):
            if h not in firsts:
                firsts[h] = i
            lasts[h] = i
        tok = None
        for i, (w, m, h) in enumerate(mms):
            tok = self.op("tensor", lambda e, o=half_aps[h], ww=w, mm=m,
                          s=(i == firsts[h]), sp=(i == lasts[h]):
                          e.matmul(o, ww, mm, start=s, stop=sp),
                          inc="tv" if i == len(mms) - 1 else None)
        return tok


class Region:
    def __init__(self, nc, base, top):
        self.nc = nc
        self.cur = base
        self.top = top

    def t(self, name, shape, dtype):
        per_part = int(np.prod(shape[1:])) * mybir.dt.size(dtype)
        addr = (self.cur + 63) & ~63
        self.cur = addr + per_part
        assert self.cur <= self.top, (
            f"SBUF overflow: {name} ends at {self.cur} > {self.top}")
        return self.nc.alloc_sbuf_tensor_at(name, shape, dtype, offset=addr)


def hilo_mms(whi_t, wlo_t, xhi_t, xlo_t, wslices, xslice):
    mms = []
    for ki in range(NCH):
        wh, wl = wslices(whi_t, ki), wslices(wlo_t, ki)
        xh, xl = xslice(xhi_t, ki), xslice(xlo_t, ki)
        mms.append((wh, xh))
        mms.append((wh, xl))
        mms.append((wl, xh))
    return mms


def build_core(S):
    nc = bass.Bass("TRN2", target_bir_lowering=False)
    NU = S // TU
    NBLK = S // 128
    QG = 1024 if S % 1024 == 0 else 512
    NG = S // QG
    SUBS = QG // 128
    KCH_PER_G = QG // 512

    xd = nc.dram_tensor("x", [S, D], F32, kind="ExternalInput")
    wd = {w: nc.dram_tensor(w + "t", [D, D], F32, kind="ExternalInput")
          for w in ("wq", "wk", "wv", "wo", "wr", "wk2", "wv2")}
    bqf = nc.dram_tensor("bqf", [128, 8], F32, kind="ExternalInput")
    bkf = nc.dram_tensor("bkf", [128, 8], F32, kind="ExternalInput")
    bof = nc.dram_tensor("bof", [128, 8], F32, kind="ExternalInput")
    murf = nc.dram_tensor("murf", [128, 8], F32, kind="ExternalInput")
    mukf = nc.dram_tensor("mukf", [128, 8], F32, kind="ExternalInput")
    bvbd = nc.dram_tensor("bvb", [128, D], F32, kind="ExternalInput")
    idbd = nc.dram_tensor("idb", [128, 128], BF16, kind="ExternalInput")
    idfd = nc.dram_tensor("idf", [128, 128], F32, kind="ExternalInput")
    maskd = nc.dram_tensor("mask", [128, 128], F32, kind="ExternalInput")
    epsd = nc.dram_tensor("epsc", [128, 1], F32, kind="ExternalInput")
    yd = nc.dram_tensor("y", [S, D], F32, kind="ExternalOutput")

    sqh = nc.dram_tensor("sqh", [NCH, 128, S], BF16, kind="Internal")
    sql = nc.dram_tensor("sql", [NCH, 128, S], BF16, kind="Internal")
    skh = nc.dram_tensor("skh", [NCH, 128, S], BF16, kind="Internal")
    skl = nc.dram_tensor("skl", [NCH, 128, S], BF16, kind="Internal")
    svh = nc.dram_tensor("svh", [NBLK, 128, D], BF16, kind="Internal")
    svl = nc.dram_tensor("svl", [NBLK, 128, D], BF16, kind="Internal")
    sx1 = nc.dram_tensor("sx1", [NBLK, 128, D], F32, kind="Internal")
    xr3 = xd.ap().rearrange("(n p) m -> n p m", p=128)
    yr3 = yd.ap().rearrange("(n p) m -> n p m", p=128)

    ps_a = nc.alloc_psum_tensor("ps_a", [128, 512], F32)
    ps_b = nc.alloc_psum_tensor("ps_b", [128, 512], F32)
    ps_c = nc.alloc_psum_tensor("ps_c", [128, 1024], F32)
    ps_d = nc.alloc_psum_tensor("ps_d", [128, 1024], F32)
    ps_t1 = nc.alloc_psum_tensor("ps_t1", [128, 1024], BF16)
    ps_t2 = nc.alloc_psum_tensor("ps_t2", [128, 1024], BF16)
    psS = [ps_a, ps_b]
    psH = [ps_c.ap()[:, 0:512], ps_c.ap()[:, 512:1024],
           ps_d.ap()[:, 0:512], ps_d.ap()[:, 512:1024]]
    psO = [ps_c, ps_d]
    psT = [ps_t1, ps_t2]

    c_idb = nc.alloc_sbuf_tensor("c_idb", [128, 128], BF16)
    c_idf = nc.alloc_sbuf_tensor("c_idf", [128, 128], F32)
    c_mask = nc.alloc_sbuf_tensor("c_mask", [128, 128], F32)
    c_bq = nc.alloc_sbuf_tensor("c_bq", [128, 8], F32)
    c_bk = nc.alloc_sbuf_tensor("c_bk", [128, 8], F32)
    c_bo = nc.alloc_sbuf_tensor("c_bo", [128, 8], F32)
    c_mur = nc.alloc_sbuf_tensor("c_mur", [128, 8], F32)
    c_muk = nc.alloc_sbuf_tensor("c_muk", [128, 8], F32)
    c_bv = nc.alloc_sbuf_tensor("c_bv", [128, D], F32)
    c_eps = nc.alloc_sbuf_tensor("c_eps", [128, 1], F32)

    ov_base = nc.sbuf_base + 4096
    ov_top = nc.sbuf_top

    p = Prog(nc)

    tok_consts = p.dma_group("sync", [
        (dst.ap(), srcd.ap()) for dst, srcd in
        ((c_idb, idbd), (c_idf, idfd), (c_mask, maskd),
         (c_bq, bqf), (c_bk, bkf), (c_bo, bof),
         (c_mur, murf), (c_muk, mukf), (c_bv, bvbd),
         (c_eps, epsd))])
    for engx in ("tensor", "vector", "scalar", "gpsimd"):
        p.wait(engx, tok_consts)

    def weight_piece(wdram, whi, wlo, stg, copy_toks, sub_toks, ki):
        wr = wdram.ap().rearrange("(c p) m -> c p m", p=128)
        par = ki % len(stg)
        p.waits("sync", [copy_toks[par], sub_toks[par]])
        ld = p.dma("sync", stg[par].ap(), wr[ki])
        p.wait("vector", ld)
        dsth = whi.ap()[:, ki * 1024:(ki + 1) * 1024]
        tok_c = p.op("vector", lambda e, d=dsth, s=stg[par].ap():
                     e.tensor_copy(d, s), inc="vv")
        p.wait("gpsimd", tok_c)
        g1 = p.op("gpsimd", lambda e, d=wlo.ap()[:, ki * 1024:
                  ki * 1024 + 512],
                  s=stg[par].ap()[:, 0:512],
                  h=whi.ap()[:, ki * 1024: ki * 1024 + 512]:
                  e.tensor_tensor(d, s, h, op=OP.subtract), inc="gv")
        p.drain("vector")
        g2 = p.op("vector", lambda e, d=wlo.ap()[:, ki * 1024 + 512:
                  (ki + 1) * 1024],
                  s=stg[par].ap()[:, 512:1024],
                  h=whi.ap()[:, ki * 1024 + 512:(ki + 1) * 1024]:
                  e.tensor_tensor(d, s, h, op=OP.subtract), inc="vv")
        copy_toks[par] = g2
        sub_toks[par] = g1

    def load_weight(wdram, whi, wlo, stg, copy_toks, sub_toks):
        for ki in range(NCH):
            weight_piece(wdram, whi, wlo, stg, copy_toks, sub_toks, ki)
        return copy_toks[:] + sub_toks[:]

    def load_weight_sg(wdram, whi, wlo, stg, copy_toks, sub_toks):
        wr = wdram.ap().rearrange("(c p) m -> c p m", p=128)
        ns = len(stg)
        for ki in range(NCH):
            par = ki % ns
            p.waits("sync", [copy_toks[par], sub_toks[par]])
            ld = p.dma("sync", stg[par].ap(), wr[ki])
            p.wait("scalar", ld)
            p.drain("scalar")
            tok_c = p.op("scalar", lambda e,
                         d=whi.ap()[:, ki * 1024:(ki + 1) * 1024],
                         s=stg[par].ap(): e.copy(d, s), inc="sv")
            p.wait("gpsimd", tok_c)
            g1 = p.op("gpsimd", lambda e, d=wlo.ap()[:, ki * 1024:
                      ki * 1024 + 512],
                      s=stg[par].ap()[:, 0:512],
                      h=whi.ap()[:, ki * 1024: ki * 1024 + 512]:
                      e.tensor_tensor(d, s, h, op=OP.subtract), inc="gv")
            g2 = p.op("gpsimd", lambda e, d=wlo.ap()[:, ki * 1024 + 512:
                      (ki + 1) * 1024],
                      s=stg[par].ap()[:, 512:1024],
                      h=whi.ap()[:, ki * 1024 + 512:(ki + 1) * 1024]:
                      e.tensor_tensor(d, s, h, op=OP.subtract), inc="gv")
            copy_toks[par] = tok_c
            sub_toks[par] = g2
        return copy_toks[:] + sub_toks[:] + [g1]

    def layernorm(x_tm, bn_sb, mv_sb, rst4, rinv4, in_toks, out_tm=None):
        p.waits("vector", in_toks)
        for blk in range(4):
            for h in range(2):
                src = x_tm.ap()[:, blk * 1024 + h * 512: blk * 1024 + (h + 1) * 512]
                dst = bn_sb.ap()[:, (blk * 2 + h) * 6:(blk * 2 + h + 1) * 6]
                p.op("vector", lambda e, d=dst, s=src: e.bn_stats(d, s))
        p.drain("vector")
        tok_mv = None
        for blk in range(4):
            src = bn_sb.ap()[:, blk * 12:(blk + 1) * 12]
            dst = mv_sb.ap()[:, blk * 2:(blk + 1) * 2]
            tok_mv = p.op("vector", lambda e, d=dst, s=src: e.bn_aggr(d, s),
                          inc="vv")
        p.wait("scalar", tok_mv)
        var_ap = bass.AP(mv_sb, 1, [[8, 128], [2, 4]])
        tok_sq = p.op("scalar", lambda e, d=rst4.ap(), s=var_ap:
                      e.activation(d, s, AF.Sqrt, bias=c_eps.ap()), inc="sv")
        p.wait("vector", tok_sq)
        p.op("vector", lambda e, d=rinv4.ap(), s=rst4.ap(): e.reciprocal(d, s))
        p.drain("vector")
        tok = None
        dst_t = x_tm if out_tm is None else out_tm
        for blk in range(4):
            xa = x_tm.ap()[:, blk * 1024:(blk + 1) * 1024]
            da = dst_t.ap()[:, blk * 1024:(blk + 1) * 1024]
            mcol = mv_sb.ap()[:, blk * 2: blk * 2 + 1]
            rcol = rinv4.ap()[:, blk: blk + 1]
            tok = p.op("vector", lambda e, d=da, s=xa, m=mcol, r=rcol:
                       e.tensor_scalar(d, s, m, r, op0=OP.subtract, op1=OP.mult),
                       inc="vv")
        return tok

    ra = Region(nc, ov_base, ov_top)
    wq_hi = ra.t("wq_hi", [128, 8192], BF16)
    wq_lo = ra.t("wq_lo", [128, 8192], BF16)
    wk_hi = ra.t("wk_hi", [128, 8192], BF16)
    wk_lo = ra.t("wk_lo", [128, 8192], BF16)
    wv_hi = ra.t("wv_hi", [128, 8192], BF16)
    wv_lo = ra.t("wv_lo", [128, 8192], BF16)
    stgA = [ra.t(f"stgA{i}", [128, 1024], F32) for i in range(3)]
    xa_tm2 = [ra.t("xa_tm", [128, 4096], F32), ra.t("xa_tmB", [128, 4096], F32)]
    yT = ra.t("yT", [128, 4096], F32)
    yT_hi = ra.t("yT_hi", [128, 4096], BF16)
    yT_lo = ra.t("yT_lo", [128, 4096], BF16)
    bnA = ra.t("bnA", [128, 48], F32)
    mvA = ra.t("mvA", [128, 8], F32)
    rstA = ra.t("rstA", [128, 4], F32)
    rinvA = ra.t("rinvA", [128, 4], F32)
    qk_f = [ra.t("qk_f0", [128, 512], F32), ra.t("qk_f1", [128, 512], F32)]
    qk_h = [ra.t("qk_h0", [128, 512], BF16), ra.t("qk_h1", [128, 512], BF16)]
    qk_l = [ra.t("qk_l0", [128, 512], BF16), ra.t("qk_l1", [128, 512], BF16)]
    v_f = [ra.t("v_f0", [128, 1024], F32), ra.t("v_f1", [128, 1024], F32)]
    v_h = [ra.t("v_h0", [128, 1024], BF16), ra.t("v_h1", [128, 1024], BF16)]
    v_l = [ra.t("v_l0", [128, 1024], BF16), ra.t("v_l1", [128, 1024], BF16)]
    assert ra.cur <= ov_top

    def wsl(t, ki, mo):
        return t.ap()[:, ki * 1024 + mo * 128: ki * 1024 + (mo + 1) * 128]

    def wsl512(t, ki, h):
        return t.ap()[:, ki * 1024 + h * 512: ki * 1024 + (h + 1) * 512]

    def ysl(t, ki):
        return t.ap()[:, ki * 512:(ki + 1) * 512]

    def ysl_blk(t, ki, blk):
        return t.ap()[:, ki * 512 + blk * 128: ki * 512 + (blk + 1) * 128]

    x_free_tok = [None, None]
    psS_free = [None, None]
    psHO_free = [None, None]
    qk_busy = [()] * 2
    v_busy = [()] * 2
    gqk = 0
    gv_i = 0
    prev_ylo = None
    prev_ylo2 = None

    def split_halves(dst_t, src_t, hi_t, n, tok_hi):
        h = (n * 5) // 32 // 128 * 128
        p.wait("gpsimd", tok_hi)
        g1 = p.op("gpsimd", lambda e, d=dst_t.ap()[:, 0:h],
                  s=src_t.ap()[:, 0:h], hh=hi_t.ap()[:, 0:h]:
                  e.tensor_tensor(d, s, hh, op=OP.subtract), inc="gv")
        p.drain("vector")
        g2 = p.op("vector", lambda e, d=dst_t.ap()[:, h:n],
                  s=src_t.ap()[:, h:n], hh=hi_t.ap()[:, h:n]:
                  e.tensor_tensor(d, s, hh, op=OP.subtract), inc="vv")
        return g1, g2

    def pa_ln(u):
        xa_tm = xa_tm2[u % 2]
        p.wait("sync", x_free_tok[u % 2])
        if u == 0:
            tok_ldb = p.dma_group("gpsimd", [
                (xa_tm.ap()[:, blk * 1024:(blk + 1) * 1024],
                 xr3[u * 4 + blk]) for blk in range(2, 4)])
            tok_ldx = p.dma_group("sync", [
                (xa_tm.ap()[:, blk * 1024:(blk + 1) * 1024],
                 xr3[u * 4 + blk]) for blk in range(2)])
            return layernorm(xa_tm, bnA, mvA, rstA, rinvA,
                             [tok_ldx, tok_ldb])
        tok_ldx = p.dma_group("sync", [
            (xa_tm.ap()[:, blk * 1024:(blk + 1) * 1024], xr3[u * 4 + blk])
            for blk in range(4)])
        return layernorm(xa_tm, bnA, mvA, rstA, rinvA, [tok_ldx])

    def pa_front(u, tok_y):
        nonlocal prev_ylo, prev_ylo2
        xa_tm = xa_tm2[u % 2]
        p.wait("tensor", tok_y)
        p.wait("vector", prev_ylo)
        p.wait("vector", prev_ylo2)
        last_t = None
        for c in range(NCH):
            par = c % 2
            p.wait("tensor", psS_free[par])
            for blk in range(4):
                srcx = xa_tm.ap()[:, blk * 1024 + c * 128:
                                  blk * 1024 + (c + 1) * 128]
                last_t = p.op("tensor", lambda e,
                              d=psS[par].ap()[:, blk * 128:(blk + 1) * 128],
                              s=srcx: e.transpose(d, s, c_idf.ap()),
                              inc="tv" if blk == 3 else None)
            p.wait("vector", last_t)
            dst = yT.ap()[:, c * 512:(c + 1) * 512]
            psS_free[par] = p.op("vector", lambda e, d=dst,
                                 s=psS[par].ap()[:, 0:512]:
                                 e.tensor_copy(d, s), inc="vv")
        x_free_tok[u % 2] = last_t

        p.drain("vector")
        tok_yhi = p.op("vector", lambda e: e.tensor_copy(yT_hi.ap(), yT.ap()),
                       inc="vv")
        g1, g2 = split_halves(yT_lo, yT, yT_hi, 4096, tok_yhi)
        prev_ylo, prev_ylo2 = g1, g2
        return [tok_yhi, g1, g2]

    def pa_back(u, ytoks, wtoks):
        nonlocal gqk, gv_i
        wq_toks, wk_toks, wv_toks = wtoks
        p.waits("tensor", ytoks)

        for proj, whi, wlo, bias, sh_d, sl_d in (
                (0, wq_hi, wq_lo, c_bq, sqh, sql),
                (1, wk_hi, wk_lo, c_bk, skh, skl)):
            p.waits("tensor", wq_toks if proj == 0 else wk_toks)
            for mo in range(NCH):
                par = gqk % 2
                p.wait("tensor", psS_free[par])
                mms = hilo_mms(whi, wlo, yT_hi, yT_lo,
                               lambda t, ki, m=mo: wsl(t, ki, m), ysl)
                tok_mm = p.mm_group(psS[par].ap(), mms)
                for t in qk_busy[par]:
                    p.wait("vector", t)
                p.wait("vector", tok_mm)
                bcol = bias.ap()[:, mo:mo + 1]
                p.op("vector", lambda e, d=qk_f[par].ap(), s=psS[par].ap(),
                     bc=bcol: e.tensor_scalar(d, s, bc, None, op0=OP.add))
                p.drain("vector")
                psS_free[par] = p.op("vector", lambda e, d=qk_h[par].ap(),
                                     s=qk_f[par].ap(): e.tensor_copy(d, s),
                                     inc="vv")
                tok_hi = psS_free[par]
                p.wait("gpsimd", tok_hi)
                tok_lo = p.op("gpsimd", lambda e, d=qk_l[par].ap(),
                              s=qk_f[par].ap(), h=qk_h[par].ap():
                              e.tensor_tensor(d, s, h, op=OP.subtract), inc="gv")
                p.wait("scalar", tok_hi)
                st1 = p.dma("scalar", sh_d.ap()[mo][:, u * 512:(u + 1) * 512],
                            qk_h[par].ap())
                p.wait("scalar", tok_lo)
                st2 = p.dma("scalar", sl_d.ap()[mo][:, u * 512:(u + 1) * 512],
                            qk_l[par].ap())
                qk_busy[par] = (st1, st2, tok_lo)
                gqk += 1

        p.waits("tensor", wv_toks)
        for blk in range(4):
            par = gv_i % 2
            pst = psO[par]
            p.wait("tensor", psHO_free[par])
            mms = []
            for ki in range(NCH):
                yh = ysl_blk(yT_hi, ki, blk)
                yl = ysl_blk(yT_lo, ki, blk)
                for h in range(2):
                    mms.append((yh, wsl512(wv_hi, ki, h), h))
                    mms.append((yh, wsl512(wv_lo, ki, h), h))
                    mms.append((yl, wsl512(wv_hi, ki, h), h))
            halves = [pst.ap()[:, 0:512], pst.ap()[:, 512:1024]]
            tok_mm = p.mm_group_halves(halves, mms)
            for t in v_busy[par]:
                p.wait("vector", t)
            p.wait("vector", tok_mm)
            p.op("vector", lambda e, d=v_f[par].ap(), s=pst.ap(),
                 b_=c_bv.ap(): e.tensor_tensor(d, s, b_, op=OP.add))
            p.drain("vector")
            psHO_free[par] = p.op("vector", lambda e, d=v_h[par].ap(),
                                  s=v_f[par].ap(): e.tensor_copy(d, s),
                                  inc="vv")
            tok_hi = psHO_free[par]
            p.wait("gpsimd", tok_hi)
            tok_lo = p.op("gpsimd", lambda e, d=v_l[par].ap(),
                          s=v_f[par].ap(), h=v_h[par].ap():
                          e.tensor_tensor(d, s, h, op=OP.subtract), inc="gv")
            p.wait("scalar", tok_hi)
            st1 = p.dma("scalar", svh.ap()[u * 4 + blk], v_h[par].ap())
            p.wait("scalar", tok_lo)
            st2 = p.dma("scalar", svl.ap()[u * 4 + blk], v_l[par].ap())
            v_busy[par] = (st1, st2, tok_lo)
            gv_i += 1

    ty0 = pa_ln(0)
    ytoks0 = pa_front(0, ty0)
    stg_c, stg_s = [None] * 3, [None] * 3
    wq_toks = load_weight_sg(wd["wq"], wq_hi, wq_lo, stgA, stg_c, stg_s)
    wk_toks = load_weight_sg(wd["wk"], wk_hi, wk_lo, stgA, stg_c, stg_s)
    wv_toks = load_weight_sg(wd["wv"], wv_hi, wv_lo, stgA, stg_c, stg_s)
    wtoks = (wq_toks, wk_toks, wv_toks)
    ty_next = pa_ln(1) if NU > 1 else None
    pa_back(0, ytoks0, wtoks)
    for u in range(1, NU):
        yt = pa_front(u, ty_next)
        ty_next = pa_ln(u + 1) if u + 1 < NU else None
        pa_back(u, yt, wtoks)

    tok_pa_stores = p.all_dma_toks("scalar")

    TUC = 256
    NU2 = S // TUC
    rd = Region(nc, ov_base, ov_top)
    wk2_hi = rd.t("wk2_hi", [128, 8192], BF16)
    wk2_lo = rd.t("wk2_lo", [128, 8192], BF16)
    stgD = [rd.t("stgD0", [128, 1024], F32), rd.t("stgD1", [128, 1024], F32)]
    wr_hi = rd.t("wr_hi", [128, 8192], BF16)
    wr_lo = rd.t("wr_lo", [128, 8192], BF16)
    wv2_hi = rd.t("wv2_hi", [128, 8192], BF16)
    wv2_lo = rd.t("wv2_lo", [128, 8192], BF16)
    x1c = [rd.t("x1c0", [128, 2048], F32), rd.t("x1c1", [128, 2048], F32)]
    y2c = [rd.t("y2c0", [128, 2048], F32), rd.t("y2c1", [128, 2048], F32)]
    y2Tc = [rd.t("y2Tc0", [128, 257], F32), rd.t("y2Tc1", [128, 257], F32)]
    sliv = rd.t("sliv", [128, 8], F32)
    d1c = [rd.t("d1c0", [128, 256], F32), rd.t("d1c1", [128, 256], F32)]
    xrf = [rd.t("xrf0", [128, 256], F32), rd.t("xrf1", [128, 256], F32)]
    xkf = [rd.t("xkf0", [128, 256], F32), rd.t("xkf1", [128, 256], F32)]
    xr_hi = [rd.t("xr_hi0", [128, 2048], BF16), rd.t("xr_hi1", [128, 2048], BF16)]
    xr_lo = [rd.t("xr_lo0", [128, 2048], BF16), rd.t("xr_lo1", [128, 2048], BF16)]
    xk_hi = [rd.t("xk_hi0", [128, 2048], BF16), rd.t("xk_hi1", [128, 2048], BF16)]
    xk_lo = [rd.t("xk_lo0", [128, 2048], BF16), rd.t("xk_lo1", [128, 2048], BF16)]
    bnD = rd.t("bnD", [128, 24], F32)
    mvD = rd.t("mvD", [128, 4], F32)
    rstD = rd.t("rstD", [128, 2], F32)
    rinvD = rd.t("rinvD", [128, 2], F32)
    k2_hi = rd.t("k2_hi", [128, 2048], BF16)
    k2_lo = rd.t("k2_lo", [128, 2048], BF16)
    e_sb = [rd.t("e_sb0", [128, 256], F32), rd.t("e_sb1", [128, 256], F32)]
    t_sb = [rd.t("t_sb0", [128, 256], F32), rd.t("t_sb1", [128, 256], F32)]
    of32 = rd.t("of32", [128, 256], F32)
    xin = rd.t("xin", [128, 2048], F32)
    assert rd.cur <= ov_top

    def ln2_stats(x_tm, in_toks):
        p.waits("vector", in_toks)
        for blk in range(2):
            for h in range(2):
                srcb = x_tm.ap()[:, blk * 1024 + h * 512:
                                 blk * 1024 + (h + 1) * 512]
                dstb = bnD.ap()[:, (blk * 2 + h) * 6:(blk * 2 + h + 1) * 6]
                p.op("vector", lambda e, d=dstb, s=srcb: e.bn_stats(d, s))
        p.drain("vector")

    def ln2_norm(x_tm, out_tm):
        tok_mv = None
        for blk in range(2):
            srcb = bnD.ap()[:, blk * 12:(blk + 1) * 12]
            dstb = mvD.ap()[:, blk * 2:(blk + 1) * 2]
            tok_mv = p.op("vector", lambda e, d=dstb, s=srcb: e.bn_aggr(d, s),
                          inc="vv")
        p.wait("scalar", tok_mv)
        var_ap = bass.AP(mvD, 1, [[4, 128], [2, 2]])
        tok_sq = p.op("scalar", lambda e, d=rstD.ap(), s=var_ap:
                      e.activation(d, s, AF.Sqrt, bias=c_eps.ap()), inc="sv")
        p.wait("vector", tok_sq)
        p.op("vector", lambda e, d=rinvD.ap(), s=rstD.ap(): e.reciprocal(d, s))
        p.drain("vector")
        tok = None
        for blk in range(2):
            xa = x_tm.ap()[:, blk * 1024:(blk + 1) * 1024]
            da = out_tm.ap()[:, blk * 1024:(blk + 1) * 1024]
            mcol = mvD.ap()[:, blk * 2: blk * 2 + 1]
            rcol = rinvD.ap()[:, blk: blk + 1]
            tok = p.op("vector", lambda e, d=da, s=xa, m=mcol, r=rcol:
                       e.tensor_scalar(d, s, m, r, op0=OP.subtract,
                                       op1=OP.mult), inc="vv")
        return tok

    def ln2(x_tm, out_tm, in_toks):
        ln2_stats(x_tm, in_toks)
        return ln2_norm(x_tm, out_tm)

    y_store_par = [None, None]
    mix_free = [None, None]
    prev_split2 = [None, None]
    th2, tl2 = [None, None], [None, None]
    prev_k2lo = None
    k2_read_done = [None, None]
    tok_y2u = [None, None]
    ghd = 0

    def ysl256(t, ki):
        return t.ap()[:, ki * 256:(ki + 1) * 256]

    fa_ld = [None, None]
    fa_ldx = [None]

    def c2_front_a_load(u):
        par = u % 2
        p.waits("sync", acc_group[(u * 2) // 8])
        if u == 0:
            p.waits("sync", [("vv", p.cnt["vv"]), ("sv", p.cnt["sv"]),
                             ("gv", p.cnt["gv"]), ("tv", p.cnt["tv"])])
        else:
            p.waits("sync", [tok_y2u[1 - par]])
        fa_ldx[0] = p.dma_group("sync", [
            (xin.ap()[:, blk * 1024:(blk + 1) * 1024],
             sx1.ap()[u * 2 + blk]) for blk in range(2)])
        p.wait("sync", y_store_par[par])
        p.waits("sync", tok_kv_dead)
        fa_ld[par] = p.dma_group("sync", [
            (x1c[par].ap()[:, blk * 1024:(blk + 1) * 1024],
             sx1.ap()[u * 2 + blk]) for blk in range(2)])

    def c2_front_a_stats(u):
        par = u % 2
        if u == 0:
            p.waits("vector", tok_kv_dead)
        ln2_stats(xin, [fa_ldx[0]])

    def c2_front_a_norm(u):
        par = u % 2
        tok_y2u[par] = ln2_norm(xin, y2c[par])

    def c2_front_a(u):
        c2_front_a_load(u)
        c2_front_a_stats(u)
        c2_front_a_norm(u)

    def c2_front_b(u):
        par = u % 2
        split_last = []
        ch_split = {}

        def chunk(c):
            yc = y2Tc[c % 2]
            if c == 0:
                p.wait("tensor", tok_y2u[par])
                p.wait("vector", prev_k2lo)
            p.waits("vector", [mix_free[c % 2], tok_sliv])
            p.drain("vector")
            p.op("vector", lambda e, d=yc.ap()[:, 0:1],
                 s=sliv.ap()[:, c:c + 1]: e.tensor_copy(d, s))
            part = c % 2
            p.wait("tensor", psS_free[part])
            tt = None
            for blk in range(2):
                srcy = y2c[par].ap()[:, blk * 1024 + c * 128:
                                     blk * 1024 + (c + 1) * 128]
                tt = p.op("tensor", lambda e,
                          d=psS[part].ap()[:, blk * 128:(blk + 1) * 128],
                          s=srcy: e.transpose(d, s, c_idf.ap()),
                          inc="tv" if blk == 1 else None)
            p.wait("vector", tt)
            psS_free[part] = p.op("vector", lambda e, d=yc.ap()[:, 1:257],
                                  s=psS[part].ap()[:, 0:256]:
                                  e.tensor_copy(d, s), inc="vv")
            p.drain("vector")
            p.op("vector", lambda e, d=sliv.ap()[:, c:c + 1],
                 s=yc.ap()[:, 256:257]: e.tensor_copy(d, s))
            p.drain("vector")
            p.op("vector", lambda e, d=d1c[par].ap(), a=yc.ap()[:, 1:257],
                 b_=yc.ap()[:, 0:256]:
                 e.tensor_tensor(d, a, b_, op=OP.subtract))
            p.drain("vector")
            cp = c % 2
            p.wait("vector", prev_split2[cp])
            p.wait("vector", k2_read_done[par])
            p.op("vector", lambda e, d=xrf[cp].ap(), a=d1c[par].ap(),
                 mu=c_mur.ap()[:, c:c + 1], b_=yc.ap()[:, 0:256]:
                 e.scalar_tensor_tensor(d, a, mu, b_, op0=OP.mult, op1=OP.add))
            tok_mix = p.op("vector", lambda e, d=xkf[cp].ap(),
                           a=d1c[par].ap(),
                           mu=c_muk.ap()[:, c:c + 1],
                           b_=yc.ap()[:, 0:256]:
                           e.scalar_tensor_tensor(d, a, mu, b_, op0=OP.mult,
                                                  op1=OP.add), inc="vv")
            mix_free[c % 2] = tok_mix
            p.drain("vector")
            t1 = p.op("vector", lambda e,
                      d=xr_hi[par].ap()[:, c * 256:(c + 1) * 256],
                      s=xrf[cp].ap(): e.tensor_copy(d, s), inc="vv")
            t2 = p.op("vector", lambda e,
                      d=xk_hi[par].ap()[:, c * 256:(c + 1) * 256],
                      s=xkf[cp].ap(): e.tensor_copy(d, s), inc="vv")
            p.waits("gpsimd", [t1, t2])
            t3 = p.op("gpsimd", lambda e,
                      d=xr_lo[par].ap()[:, c * 256:(c + 1) * 256],
                      s=xrf[cp].ap(),
                      h=xr_hi[par].ap()[:, c * 256:(c + 1) * 256]:
                      e.tensor_tensor(d, s, h, op=OP.subtract), inc="gv")
            if c == NCH - 1:
                p.drain("vector")
                t4 = p.op("vector", lambda e,
                          d=xk_lo[par].ap()[:, c * 256:(c + 1) * 256],
                          s=xkf[cp].ap(),
                          h=xk_hi[par].ap()[:, c * 256:(c + 1) * 256]:
                          e.tensor_tensor(d, s, h, op=OP.subtract),
                          inc="vv")
            else:
                t4 = p.op("gpsimd", lambda e,
                          d=xk_lo[par].ap()[:, c * 256:(c + 1) * 256],
                          s=xkf[cp].ap(),
                          h=xk_hi[par].ap()[:, c * 256:(c + 1) * 256]:
                          e.tensor_tensor(d, s, h, op=OP.subtract),
                          inc="gv")
            prev_split2[cp] = t4
            split_last[:] = [t1, t2, t3, t4]
            ch_split[c] = [t1, t2, t3, t4]
        return [lambda c=c: chunk(c) for c in range(NCH)], split_last, ch_split

    def c2_back(u, split_toks, chs, fillers=None):
        nonlocal ghd, prev_k2lo
        par = u % 2
        gidx = [0]

        def fill():
            if fillers is not None and gidx[0] % 3 == 2:
                ci = gidx[0] // 3
                if ci < NCH:
                    fillers[ci]()
            gidx[0] += 1

        p.waits("tensor", wtoks_wr)
        r_t = y2c[par]
        for mo in range(NCH):
            parh = ghd % 4
            p.wait("tensor", psH_free[parh])
            mms = hilo_mms(wr_hi, wr_lo, xr_hi[par], xr_lo[par],
                           lambda t, ki, m=mo: wsl(t, ki, m), ysl256)
            if mo == 0:
                pre = [None] * len(mms)
                for ki in range(NCH):
                    pre[3 * ki] = [chs[ki][0], chs[ki][2]]
                tok_mm = p.mm_group_w(psH[parh][:, 0:256], mms, pre)
            else:
                tok_mm = p.mm_group(psH[parh][:, 0:256], mms)
            p.wait("scalar", tok_mm)
            psH_free[parh] = p.op("scalar", lambda e,
                                  d=r_t.ap()[:, mo * 256:(mo + 1) * 256],
                                  s=psH[parh][:, 0:256]:
                                  e.activation(d, s, AF.Sigmoid), inc="sv")
            ghd += 1
            fill()
        if u == 0:
            p.waits("sync", tok_kv_dead)
            wv2_toks.extend(load_weight(wd["wv2"], wv2_hi, wv2_lo, stgD,
                                        stg_c2, stg_s2))
        p.waits("tensor", wk2_toks)
        k2s = {}
        for mo in range(NCH):
            mp = mo % 2
            parh = ghd % 4
            p.wait("tensor", psH_free[parh])
            mms = hilo_mms(wk2_hi, wk2_lo, xk_hi[par], xk_lo[par],
                           lambda t, ki, m=mo: wsl(t, ki, m), ysl256)
            if mo == 0:
                pre = [None] * len(mms)
                for ki in range(NCH):
                    pre[3 * ki] = [chs[ki][1], chs[ki][3]]
                tok_mm = p.mm_group_w(psH[parh][:, 0:256], mms, pre)
            else:
                tok_mm = p.mm_group(psH[parh][:, 0:256], mms)
            p.wait("scalar", tok_mm)
            p.waits("scalar", [th2[mp], tl2[mp]])
            p.drain("scalar")
            psH_free[parh] = p.op("scalar", lambda e, d=e_sb[mp].ap(),
                                  s=psH[parh][:, 0:256]:
                                  e.activation(d, s, AF.Relu), inc="sv")
            p.drain("scalar")
            tok_t = p.op("scalar", lambda e, d=t_sb[mp].ap(),
                         s=e_sb[mp].ap():
                         e.activation(d, s, AF.Square), inc="sv")
            p.drain("scalar")
            p.wait("vector", tok_t)
            th = p.op("vector", lambda e,
                      d=k2_hi.ap()[:, mo * 256:(mo + 1) * 256],
                      s=t_sb[mp].ap(): e.tensor_copy(d, s), inc="vv")
            p.wait("gpsimd", th)
            tl = p.op("gpsimd", lambda e,
                      d=k2_lo.ap()[:, mo * 256:(mo + 1) * 256],
                      s=t_sb[mp].ap(),
                      h=k2_hi.ap()[:, mo * 256:(mo + 1) * 256]:
                      e.tensor_tensor(d, s, h, op=OP.subtract), inc="gv")
            th2[mp], tl2[mp] = th, tl
            k2s[mo] = [th, tl]
            ghd += 1
            fill()
        prev_k2lo = tl
        k2_read_done[par] = None
        p.waits("tensor", wv2_toks)
        p.waits("vector", [fa_ld[par]])
        ev_last = None
        for mo in range(NCH):
            parh = ghd % 4
            p.wait("tensor", psH_free[parh])
            mms = hilo_mms(wv2_hi, wv2_lo, k2_hi, k2_lo,
                           lambda t, ki, m=mo: wsl(t, ki, m), ysl256)
            if mo == 0:
                pre = [None] * len(mms)
                for ki in range(NCH):
                    pre[3 * ki] = k2s[ki]
                tok_mm = p.mm_group_w(psH[parh][:, 0:256], mms, pre)
            else:
                tok_mm = p.mm_group(psH[parh][:, 0:256], mms)
            p.waits("vector", [tok_mm])
            tok_of = p.op("vector", lambda e, d=of32.ap(),
                          s=psH[parh][:, 0:256],
                          r=r_t.ap()[:, mo * 256:(mo + 1) * 256]:
                          e.tensor_tensor(d, s, r, op=OP.mult), inc="vv")
            psH_free[parh] = tok_of
            p.waits("tensor", [tok_of])
            part = ghd % 2
            p.wait("tensor", psS_free[part])
            tt = None
            for blk in range(2):
                srco2 = of32.ap()[:, blk * 128:(blk + 1) * 128]
                tt = p.op("tensor", lambda e,
                          d=psS[part].ap()[:, blk * 128:(blk + 1) * 128],
                          s=srco2: e.transpose(d, s, c_idf.ap()),
                          inc="tv" if blk == 1 else None)
            p.wait("vector", tt)
            xs = bass.AP(x1c[par], mo * 128, [[2048, 128], [1024, 2], [1, 128]])
            ps3 = bass.AP(psS[part], 0, [[512, 128], [128, 2], [1, 128]])
            ev_last = p.op("vector", lambda e, d=xs, s=ps3:
                           e.tensor_tensor(d, s, d, op=OP.add), inc="vv")
            psS_free[part] = ev_last
            ghd += 1
            fill()
        k2_read_done[par] = ("tv", p.cnt["tv"])
        p.wait("scalar", ev_last)
        y_store_par[par] = p.dma_group("scalar", [
            (yr3[u * 2 + blk], x1c[par].ap()[:, blk * 1024:(blk + 1) * 1024])
            for blk in range(2)])


    r_wo = Region(nc, ov_base, ov_top)
    wo_hi = r_wo.t("wo_hi", [128, 8192], BF16)
    wo_lo = r_wo.t("wo_lo", [128, 8192], BF16)
    stgC = [r_wo.t("stgC0", [128, 1024], F32), r_wo.t("stgC1", [128, 1024], F32)]
    ov_base2 = r_wo.cur
    rb = Region(nc, ov_base2, ov_top)
    q_addr = (rb.cur + 63) & ~63
    qhiT = rb.t("qhiT", [128, 8192], BF16)
    qloT = rb.t("qloT", [128, 8192], BF16)
    o_hi = nc.alloc_sbuf_tensor_at("o_hi", [128, 8192], BF16, offset=q_addr)
    o_lo = nc.alloc_sbuf_tensor_at("o_lo", [128, 8192], BF16,
                                   offset=q_addr + 16384)
    khiT = [rb.t("khiT0", [128, 4096], BF16), rb.t("khiT1", [128, 4096], BF16)]
    kloT = [rb.t("kloT0", [128, 4096], BF16), rb.t("kloT1", [128, 4096], BF16)]
    vhiT = [rb.t("vhiT0", [128, 4096], BF16), rb.t("vhiT1", [128, 4096], BF16)]
    vloT = [rb.t("vloT0", [128, 4096], BF16), rb.t("vloT1", [128, 4096], BF16)]
    o_addr = (rb.cur + 63) & ~63
    o_sb = rb.t("o_sb", [128, 8192], F32)
    oTb_hi = nc.alloc_sbuf_tensor_at("oTb_hi", [128, 8192], BF16,
                                     offset=o_addr)
    oTb_lo = nc.alloc_sbuf_tensor_at("oTb_lo", [128, 8192], BF16,
                                     offset=o_addr + 16384)
    ab_addr = (rb.cur + 63) & ~63
    at_bk = [rb.t("at_bk0", [128, 512], F32), rb.t("at_bk1", [128, 512], F32)]
    at_bk2 = [rb.t("at_bk20", [128, 512], F32), rb.t("at_bk21", [128, 512], F32)]
    a_sb = [rb.t("a_sb0", [128, 512], F32), rb.t("a_sb1", [128, 512], F32),
            rb.t("a_sb2", [128, 512], F32), rb.t("a_sb3", [128, 512], F32)]
    p_sb = [rb.t("p_sb0", [128, 512], F32), rb.t("p_sb1", [128, 512], F32)]
    phi = [rb.t("phi0", [128, 512], BF16), rb.t("phi1", [128, 512], BF16)]
    plo = [rb.t("plo0", [128, 512], BF16), rb.t("plo1", [128, 512], BF16)]
    phiT = [rb.t("phiT0", [128, 512], BF16), rb.t("phiT1", [128, 512], BF16)]
    ploT = [rb.t("ploT0", [128, 512], BF16), rb.t("ploT1", [128, 512], BF16)]
    nmr = rb.t("nmr", [128, 16], F32)
    js = nc.alloc_sbuf_tensor_at("js", [128, 512], F32, offset=ab_addr)
    lr = rb.t("lr", [128, 8], F32)
    nmj = rb.t("nmj", [128, 1], F32)
    nmnew = rb.t("nmnew", [128, 2], F32)
    nm32 = rb.t("nm32", [128, 1], F32)
    alph = [rb.t("alph0", [128, 1], F32), rb.t("alph1", [128, 1], F32)]
    lpart = [rb.t("lpart0", [128, 1], F32), rb.t("lpart1", [128, 1], F32)]
    linv = rb.t("linv", [128, 1], F32)
    assert rb.cur <= ov_top

    p.waits("sync", tok_pa_stores)

    kv_busy = [None, None]
    psT_free = [None, None]
    psO_freeB = [None, None]
    psH_free = [None, None, None, None]
    wtoks_o = []
    at_free2 = [None, None]
    acc_toks = [None, None]
    ghb = 0
    tok_n = None
    alpha_busy = [None, None]

    kv_toks = {}

    def load_q(g):
        return p.dma_group("sync", [
            (qhiT.ap()[:, c * 1024:(c + 1) * 1024],
             sqh.ap()[c][:, g * QG:(g + 1) * QG]) for c in range(NCH)] + [
            (qloT.ap()[:, c * 1024:(c + 1) * 1024],
             sql.ap()[c][:, g * QG:(g + 1) * QG]) for c in range(NCH)])

    def load_kv(g, j, q="sync"):
        par = j % 2
        p.wait(q, kv_busy[par])
        prs = []
        for c in range(NCH):
            prs.append((khiT[par].ap()[:, c * 512:(c + 1) * 512],
                        skh.ap()[c][:, j * 512:(j + 1) * 512]))
            prs.append((kloT[par].ap()[:, c * 512:(c + 1) * 512],
                        skl.ap()[c][:, j * 512:(j + 1) * 512]))
        for kb in range(4):
            prs.append((vhiT[par].ap()[:, kb * 1024:(kb + 1) * 1024],
                        svh.ap()[j * 4 + kb]))
            prs.append((vloT[par].ap()[:, kb * 1024:(kb + 1) * 1024],
                        svl.ap()[j * 4 + kb]))
        kv_toks[(g, j)] = p.dma_group(q, prs)

    tok_q_next = load_q(0)
    p.waits("gpsimd", tok_pa_stores)
    load_kv(0, 0, q="gpsimd")
    prev_c1_free = None
    acc_group = []
    oT_done_last = [None]
    stg_c2, stg_s2 = [None, None], [None, None]
    wtoks_o.extend(load_weight(wd["wo"], wo_hi, wo_lo, stgC,
                               stg_c2, stg_s2))
    c2_hooked = [False]
    wtoks_wr = []

    for g in range(NG):
        nkc = KCH_PER_G * (g + 1)
        tok_q = tok_q_next

        sub_ph = [0] * SUBS
        pairs = []
        for j in range(nkc):
            smin = max(0, 4 * j - 8 * g)
            for s in range(smin, SUBS):
                qi = g * SUBS + s
                n_k = min(512, (qi + 1) * 128 - j * 512)
                pairs.append((j, s, n_k))

        def emit_qk(i):
            j, s, n_k = pairs[i]
            par = i % 2
            kpar = j % 2
            p.waits("tensor", [tok_q, kv_toks[(g, j)], psS_free[par]])
            mms = []
            for ki in range(NCH):
                qh = qhiT.ap()[:, ki * 1024 + s * 128: ki * 1024 + (s + 1) * 128]
                ql = qloT.ap()[:, ki * 1024 + s * 128: ki * 1024 + (s + 1) * 128]
                kh = khiT[kpar].ap()[:, ki * 512: ki * 512 + n_k]
                kl = kloT[kpar].ap()[:, ki * 512: ki * 512 + n_k]
                mms += [(qh, kh), (qh, kl), (ql, kh)]
            return p.mm_group(psS[par].ap()[:, 0:n_k], mms)

        def emit_abs(i, tok_s):
            j, s, n_k = pairs[i]
            par = i % 2
            p.wait("scalar", tok_s)
            return p.op("scalar", lambda e, d=a_sb[i % 4].ap()[:, 0:n_k],
                        s_=psS[par].ap()[:, 0:n_k]:
                        e.activation(d, s_, AF.Abs), inc="sv")

        def emit_rest(i, tok_a):
            j, s, n_k = pairs[i]
            par = i % 2
            kpar = j % 2
            qi = g * SUBS + s
            diag = (qi + 1) * 128 - j * 512 <= 512
            nkb = n_k // 128
            first = j == 0
            p.wait("vector", tok_a)
            if diag:
                p.op("vector", lambda e,
                     d=a_sb[i % 4].ap()[:, n_k - 128:n_k]:
                     e.tensor_tensor(d, d, c_mask.ap(), op=OP.add))
                p.drain("vector")
            ph = sub_ph[s]
            c_old = 2 * s + ph
            c_new = 2 * s + (1 - ph)
            sub_ph[s] = 1 - ph
            if first:
                p.op("vector", lambda e, d=nmr.ap()[:, c_new:c_new + 1],
                     s_=a_sb[i % 4].ap()[:, 0:n_k]:
                     e.tensor_reduce(d, s_, axis=AX.X, op=OP.max, negate=True))
                p.drain("vector")
                tok_stats = p.op("vector", lambda e, d=nm32.ap(),
                                 s_=nmr.ap()[:, c_new:c_new + 1]:
                                 e.tensor_scalar(d, s_, 1.0 / 32, None,
                                                 op0=OP.mult), inc="vv")
                tok_alpha = None
            else:
                p.op("vector", lambda e, j_=js.ap()[:, 0:n_k],
                     s_=a_sb[i % 4].ap()[:, 0:n_k],
                     sd=nmr.ap()[:, c_old:c_old + 1],
                     ac=nmr.ap()[:, c_new:c_new + 1]:
                     e.tensor_scalar(j_, s_, -1.0, sd, op0=OP.mult, op1=OP.min,
                                     accum_out=ac))
                p.drain("vector")
                p.op("vector", lambda e, d=nmnew.ap()[:, 1:2],
                     a=nmr.ap()[:, c_new:c_new + 1],
                     b_=nmr.ap()[:, c_old:c_old + 1]:
                     e.tensor_tensor(d, a, b_, op=OP.subtract))
                tok_stats = p.op("vector", lambda e, d=nm32.ap(),
                                 s_=nmr.ap()[:, c_new:c_new + 1]:
                                 e.tensor_scalar(d, s_, 1.0 / 32, None,
                                                 op0=OP.mult), inc="vv")
            p.wait("scalar", tok_stats)
            p.wait("scalar", alpha_busy[par])
            p.drain("scalar")
            tok_alpha = None
            if not first:
                tok_alpha = p.op("scalar", lambda e, d=alph[par].ap(),
                                 s_=nmnew.ap()[:, 1:2]:
                                 e.activation(d, s_, AF.Exp, scale=1.0 / 32),
                                 inc="sv")
            tok_p = p.op("scalar", lambda e, d=p_sb[par].ap()[:, 0:n_k],
                         s_=a_sb[i % 4].ap()[:, 0:n_k], b_=nm32.ap(),
                         ac=lpart[par].ap():
                         e.activation(d, s_, AF.Exp, scale=1.0 / 32, bias=b_,
                                      accum_out=ac), inc="sv")
            p.wait("vector", tok_p)
            tok_phi = p.op("vector", lambda e, d=phi[par].ap()[:, 0:n_k],
                           s_=p_sb[par].ap()[:, 0:n_k]: e.tensor_copy(d, s_),
                           inc="vv")
            p.wait("gpsimd", tok_phi)
            tok_plo = p.op("gpsimd", lambda e, d=plo[par].ap()[:, 0:n_k],
                           s_=p_sb[par].ap()[:, 0:n_k],
                           h=phi[par].ap()[:, 0:n_k]:
                           e.tensor_tensor(d, s_, h, op=OP.subtract), inc="gv")
            p.waits("tensor", [tok_phi, psT_free[0]])
            tk1 = None
            for kb in range(nkb):
                src = phi[par].ap()[:, kb * 128:(kb + 1) * 128]
                dst = psT[0].ap()[:, kb * 128:(kb + 1) * 128]
                tk1 = p.op("tensor", lambda e, d=dst, s_=src:
                           e.transpose(d, s_, c_idb.ap()),
                           inc="tv" if kb == nkb - 1 else None)
            p.waits("tensor", [tok_plo, psT_free[1]])
            tk2 = None
            for kb in range(nkb):
                src = plo[par].ap()[:, kb * 128:(kb + 1) * 128]
                dst = psT[1].ap()[:, kb * 128:(kb + 1) * 128]
                tk2 = p.op("tensor", lambda e, d=dst, s_=src:
                           e.transpose(d, s_, c_idb.ap()),
                           inc="tv" if kb == nkb - 1 else None)
            p.wait("vector", tk1)
            psT_free[0] = p.op("vector", lambda e,
                               d=phiT[par].ap()[:, 0:n_k],
                               s_=psT[0].ap()[:, 0:n_k]: e.tensor_copy(d, s_),
                               inc="vv")
            p.wait("vector", tk2)
            psT_free[1] = p.op("vector", lambda e,
                               d=ploT[par].ap()[:, 0:n_k],
                               s_=psT[1].ap()[:, 0:n_k]: e.tensor_copy(d, s_),
                               inc="vv")
            opar = i % 2
            p.waits("tensor", [psT_free[0], psT_free[1], kv_toks[(g, j)],
                               psO_freeB[opar]])
            mms = []
            for kb in range(nkb):
                ph = phiT[par].ap()[:, kb * 128:(kb + 1) * 128]
                pl = ploT[par].ap()[:, kb * 128:(kb + 1) * 128]
                for h in range(2):
                    vh = vhiT[kpar].ap()[:, kb * 1024 + h * 512:
                                         kb * 1024 + (h + 1) * 512]
                    vl = vloT[kpar].ap()[:, kb * 1024 + h * 512:
                                         kb * 1024 + (h + 1) * 512]
                    mms.append((ph, vh, h))
                    mms.append((ph, vl, h))
                    mms.append((pl, vh, h))
            halves = [psO[opar].ap()[:, 0:512], psO[opar].ap()[:, 512:1024]]
            tok_pv = p.mm_group_halves(halves, mms)
            p.waits("vector", [tok_pv, tok_alpha, tok_p])
            lsl = lr.ap()[:, s:s + 1]
            dsl = o_sb.ap()[:, s * 1024:(s + 1) * 1024]
            if first:
                if prev_c1_free is not None:
                    p.waits("vector", prev_c1_free)
                psO_freeB[opar] = p.op("vector", lambda e, d=dsl,
                                       ps_=psO[opar].ap():
                                       e.tensor_copy(d, ps_), inc="vv")
                alpha_busy[par] = p.op(
                    "vector", lambda e, d=lsl, lp=lpart[par].ap():
                    e.tensor_copy(d, lp), inc="vv")
            else:
                psO_freeB[opar] = p.op("vector", lambda e, d=dsl,
                                       al=alph[par].ap(), ps_=psO[opar].ap():
                                       e.scalar_tensor_tensor(
                                           d, d, al, ps_, op0=OP.mult,
                                           op1=OP.add), inc="vv")
                alpha_busy[par] = p.op(
                    "vector", lambda e, d=lsl, al=alph[par].ap(),
                    lp=lpart[par].ap():
                    e.scalar_tensor_tensor(d, d, al, lp, op0=OP.mult,
                                           op1=OP.add), inc="vv")

        npairs = len(pairs)
        tokS = [None] * npairs
        tokA = [None] * npairs
        tv_after_qk = None
        tokS[0] = emit_qk(0)
        tokA[0] = emit_abs(0, tokS[0])
        psS_free[0] = tokA[0]
        for ii in (1, 2):
            if ii < npairs:
                jn = pairs[ii][0]
                if (g, jn) not in kv_toks:
                    load_kv(g, jn)
                tokS[ii] = emit_qk(ii)
                if ii == 1:
                    tokA[1] = emit_abs(1, tokS[1])
                    psS_free[1] = tokA[1]
        for i in range(npairs):
            j, s, n_k = pairs[i]
            if i + 3 < npairs:
                jn3 = pairs[i + 3][0]
                if (g, jn3) not in kv_toks:
                    load_kv(g, jn3)
                tokS[i + 3] = emit_qk(i + 3)
            if i + 2 < npairs:
                tokA[i + 2] = emit_abs(i + 2, tokS[i + 2])
                psS_free[i % 2] = tokA[i + 2]
            if i + 3 >= npairs:
                tv_after_qk = ("tv", p.cnt["tv"])
            emit_rest(i, tokA[i])
            qi = g * SUBS + s
            if (qi + 1) * 128 - j * 512 <= 512:
                p.drain("vector")
                p.op("vector", lambda e, d=linv.ap(), s_=lr.ap()[:, s:s + 1]:
                     e.reciprocal(d, s_))
                p.drain("vector")
                tok_n = p.op("vector", lambda e,
                             d=o_sb.ap()[:, s * 1024:(s + 1) * 1024],
                             li=linv.ap():
                             e.tensor_scalar(d, d, li, None, op0=OP.mult),
                             inc="vv")
                p.drain("vector")
            if i + 1 == len(pairs) or pairs[i + 1][0] != j:
                kv_busy[j % 2] = ("tv", p.cnt["tv"])
        if g + 1 < NG:
            load_kv(g + 1, 0)
            load_kv(g + 1, 1)
        for k in range(4):
            psH_free[k] = ("vv", p.cnt["vv"])
        tok_base = p.dma_group("sync", [
            (sx1.ap()[g * 8 + b], xr3[g * 8 + b]) for b in range(8)])
        p.waits("scalar", [tok_n, tv_after_qk])
        p.waits("gpsimd", [tok_n, tv_after_qk])
        chunk_hi = [None] * NCH
        chunk_lo = [None] * NCH
        for c in range(NCH):
            sap3 = bass.AP(o_sb, c * 128, [[8192, 128], [1024, 8], [1, 128]])
            hap3 = bass.AP(o_hi, c * 128, [[8192, 128], [1024, 8], [1, 128]])
            lap3 = bass.AP(o_lo, c * 128, [[8192, 128], [1024, 8], [1, 128]])
            p.drain("scalar")
            chunk_hi[c] = p.op("scalar", lambda e, d=hap3, s_=sap3:
                               e.copy(d, s_), inc="sv")
            p.wait("gpsimd", chunk_hi[c])
            chunk_lo[c] = p.op("gpsimd", lambda e, d=lap3, s_=sap3, h=hap3:
                               e.tensor_tensor(d, s_, h, op=OP.subtract),
                               inc="gv")
        ev_toks = []
        for src_t, toks in ((o_hi, chunk_hi), (o_lo, chunk_lo)):
            dst_t = oTb_hi if src_t is o_hi else oTb_lo
            for c in range(NCH):
                tp = len(ev_toks) % 2
                p.waits("tensor", [toks[c], psT_free[tp]])
                tt = None
                for b in range(8):
                    sap = src_t.ap()[:, b * 1024 + c * 128:
                                     b * 1024 + (c + 1) * 128]
                    tt = p.op("tensor", lambda e,
                              d=psT[tp].ap()[:, b * 128:(b + 1) * 128],
                              s_=sap: e.transpose(d, s_, c_idb.ap()),
                              inc="tv" if b == 7 else None)
                p.wait("vector", tt)
                p.wait("vector", chunk_lo[NCH - 1])
                dap = dst_t.ap()[:, c * 1024:(c + 1) * 1024]
                psT_free[tp] = p.op("vector", lambda e, d=dap,
                                    s_=psT[tp].ap()[:, 0:1024]:
                                    e.tensor_copy(d, s_), inc="vv")
                ev_toks.append(psT_free[tp])
        tok_oT_done = ("tv", p.cnt["tv"])
        p.waits("tensor", wtoks_o + [ev_toks[-1]])
        for h in range(2):
            for mo in range(NCH):
                parh = ghb % 4
                p.wait("tensor", psH_free[parh])
                mms = hilo_mms(wo_hi, wo_lo, oTb_hi, oTb_lo,
                               lambda t, ki, m=mo: wsl(t, ki, m),
                               lambda t, ki, hh=h:
                               t.ap()[:, ki * 1024 + hh * 512:
                                      ki * 1024 + (hh + 1) * 512])
                tok_mm = p.mm_group(psH[parh], mms)
                para = ghb % 2
                p.waits("vector", [tok_mm, at_free2[para]])
                bcol = c_bo.ap()[:, mo:mo + 1]
                tok_at = p.op("vector", lambda e, d=at_bk[para].ap(),
                              s_=psH[parh], bc=bcol:
                              e.tensor_scalar(d, s_, bc, None, op0=OP.add),
                              inc="vv")
                psH_free[parh] = tok_at
                p.waits("tensor", [tok_at])
                part = ghb % 2
                p.wait("tensor", psS_free[part])
                tt = None
                for blk in range(4):
                    sap = at_bk[para].ap()[:, blk * 128:(blk + 1) * 128]
                    tt = p.op("tensor", lambda e,
                              d=psS[part].ap()[:, blk * 128:(blk + 1) * 128],
                              s_=sap: e.transpose(d, s_, c_idf.ap()),
                              inc="tv" if blk == 3 else None)
                at_free2[para] = tt
                p.wait("vector", tt)
                p.wait("vector", acc_toks[para])
                tok_ev = p.op("vector", lambda e, d=at_bk2[para].ap(),
                              s_=psS[part].ap()[:, 0:512]:
                              e.tensor_copy(d, s_), inc="vv")
                psS_free[part] = tok_ev
                p.wait("gpsimd", tok_base)
                p.wait("gpsimd", tok_ev)
                dst3 = bass.AP(sx1, (g * 8 + h * 4) * 128 * 1024 + mo * 128,
                               [[1024, 128], [128 * 1024, 4], [1, 128]])
                src3 = bass.AP(at_bk2[para], 0, [[512, 128], [128, 4], [1, 128]])
                acc_toks[para] = p.dma_group("gpsimd", [(dst3, src3)],
                                             accum=True)
                ghb += 1
                if g == NG - 1 and g >= 1:
                    slot = h * NCH + mo
                    if 1 <= slot <= 8:
                        if slot == 1:
                            p.wait("sync", tok_oT_done)
                        weight_piece(wd["wr"], wr_hi, wr_lo, stgD,
                                     stg_c2, stg_s2, slot - 1)
                        if slot == 8:
                            wtoks_wr = stg_c2[:] + stg_s2[:]
                            tok_kv_dead = [kv_busy[0], kv_busy[1]]
                            c2_front_a_load(0)
                            c2_hooked[0] = True
                    elif slot == 10:
                        c2_front_a_stats(0)
                    elif slot == 12:
                        c2_front_a_norm(0)
        prev_c1_free = [("vv", p.cnt["vv"]), ("tv", p.cnt["tv"])]
        psO_freeB[0] = ("vv", p.cnt["vv"])
        psO_freeB[1] = ("vv", p.cnt["vv"])
        acc_group.append([t for t in acc_toks if t is not None])
        oT_done_last[0] = tok_oT_done
        if g + 1 < NG:
            p.wait("sync", tok_oT_done)
            tok_q_next = load_q(g + 1)

    tok_pb_stores = p.all_dma_toks("scalar") + p.all_dma_toks("gpsimd")
    tok_kv_dead = [kv_busy[0], kv_busy[1]]
    tok_q_dead = oT_done_last[0]
    tok_wo_dead = ("tv", p.cnt["tv"])

    p.wait("vector", tok_wo_dead)
    tok_sliv = p.op("vector", lambda e: e.memset(sliv.ap(), 0.0), inc="vv")
    p.drain("vector")

    wk2_toks = []
    wv2_toks = []
    if not c2_hooked[0]:
        c2_front_a(0)
        p.wait("sync", tok_q_dead)
        wtoks_wr = load_weight(wd["wr"], wr_hi, wr_lo, stgD, stg_c2, stg_s2)
    ch0, s0, cm0 = c2_front_b(0)
    p.waits("gpsimd", acc_group[-1] + [tok_wo_dead])
    for f in ch0:
        f()
    p.wait("sync", tok_wo_dead)
    wk2_toks = load_weight_sg(wd["wk2"], wk2_hi, wk2_lo, stgD,
                              stg_c2, stg_s2)
    splits = {0: list(s0)}
    chmaps = {0: cm0}
    for u in range(1, NU2):
        c2_front_a(u)
        chu, su, cmu = c2_front_b(u)
        c2_back(u - 1, splits[u - 1], chmaps[u - 1], fillers=chu)
        splits[u] = list(su)
        chmaps[u] = cmu
    c2_back(NU2 - 1, splits[NU2 - 1], chmaps[NU2 - 1])

    p.waits("sync", p.all_dma_toks("scalar"))
    p.waits("sync", p.all_dma_toks("sync"))

    with nc.Block() as block:
        @block.sync
        def _(eng):
            for name, fn in p.events:
                if name == "sync":
                    fn(eng)

        @block.tensor
        def _(eng):
            for name, fn in p.events:
                if name == "tensor":
                    fn(eng)

        @block.vector
        def _(eng):
            for name, fn in p.events:
                if name == "vector":
                    fn(eng)

        @block.scalar
        def _(eng):
            for name, fn in p.events:
                if name == "scalar":
                    fn(eng)

        @block.gpsimd
        def _(eng):
            for name, fn in p.events:
                if name == "gpsimd":
                    fn(eng)

    return nc


def _host_prep(inputs, S):
    f32 = np.float32
    w = {}
    w["wqt"] = np.ascontiguousarray(inputs["L0_attn_Wq"].T.astype(f32))
    w["wkt"] = np.ascontiguousarray(inputs["L0_attn_Wk"].T.astype(f32))
    w["wvt"] = np.ascontiguousarray(inputs["L0_attn_Wv"].T.astype(f32))
    w["wot"] = np.ascontiguousarray(inputs["L0_attn_Wo"].T.astype(f32))
    w["wrt"] = np.ascontiguousarray(inputs["L0_ffn_W_r"].T.astype(f32))
    w["wk2t"] = np.ascontiguousarray(inputs["L0_ffn_W_k"].T.astype(f32))
    w["wv2t"] = np.ascontiguousarray(inputs["L0_ffn_W_v"].T.astype(f32))

    def fm(v):
        return np.ascontiguousarray(np.asarray(v, f32).reshape(8, 128).T)

    w["bqf"] = fm(inputs["L0_attn_bq"])
    w["bkf"] = fm(inputs["L0_attn_bk"])
    w["bof"] = fm(inputs["L0_attn_bo"])
    w["murf"] = fm(inputs["L0_ffn_mu_r"])
    w["mukf"] = fm(inputs["L0_ffn_mu_k"])
    w["bvb"] = np.ascontiguousarray(
        np.broadcast_to(np.asarray(inputs["L0_attn_bv"], f32)[None, :],
                        (128, D)))
    w["idb"] = np.eye(128, dtype=ml_dtypes.bfloat16)
    w["epsc"] = np.full((128, 1), 1e-5, f32)
    w["idf"] = np.eye(128, dtype=f32)
    mask = np.zeros((128, 128), f32)
    iu = np.triu_indices(128, 1)
    mask[iu] = -1.0e30
    w["mask"] = mask
    return w


def kernel(**inputs):
    x = np.asarray(inputs["L0_attn_x"], np.float32)
    Bsz, S, _ = x.shape
    assert Bsz == 8
    nc = build_core(S)
    shared = _host_prep(inputs, S)
    in_maps = []
    for b in range(Bsz):
        m = dict(shared)
        m["x"] = np.ascontiguousarray(x[b])
        in_maps.append(m)
    res = run_bass_kernel_spmd(nc, in_maps, core_ids=list(range(8)))
    return np.stack([np.asarray(r["y"], np.float32) for r in res.results],
                    axis=0)


# version 55
# speedup vs baseline: 1.0144x; 1.0030x over previous
import numpy as np
import ml_dtypes
import concourse.bass as bass
import concourse.mybir as mybir
from concourse.bass_utils import run_bass_kernel_spmd

F32 = mybir.dt.float32
BF16 = mybir.dt.bfloat16
AF = mybir.ActivationFunctionType
OP = mybir.AluOpType
AX = mybir.AxisListType

D = 1024
NCH = 8
TU = 512


class Prog:

    NPOOL = 6

    def __init__(self, nc):
        self.nc = nc
        self.events = []
        self.cnt = {}
        self.sems = {}
        self.last_wait = {}
        self._drained = {}
        names = ["tv", "vv", "sv", "gv"]
        self.pool = {"sync": [], "scalar": [], "gpsimd": []}
        for q in ("sync", "scalar", "gpsimd"):
            for i in range(self.NPOOL):
                n = f"dma_{q}_{i}"
                names.append(n)
                self.pool[q].append(n)
        for s in names:
            self.sems[s] = nc.alloc_semaphore(s)
            self.cnt[s] = 0
        self.pool_idx = {"sync": 0, "scalar": 0, "gpsimd": 0}
        self.DMA_SEMS = tuple(n for n in names if n.startswith("dma_"))

    def ev(self, eng, fn):
        self.events.append((eng, fn))
        self._drained[eng] = False

    def op(self, eng, fn, inc=None):
        if inc is None:
            self.ev(eng, fn)
            return None
        n = 16 if inc in self.DMA_SEMS else 1
        self.cnt[inc] += n
        tgt = self.cnt[inc]

        def run(e, f=fn, s=self.sems[inc], k=n):
            f(e).then_inc(s, k)

        self.ev(eng, run)
        return (inc, tgt)

    def dma_group(self, eng, pairs, accum=False):
        sem = self.pool[eng][self.pool_idx[eng] % self.NPOOL]
        self.pool_idx[eng] += 1
        if self.cnt[sem] > 0:
            self.wait(eng, (sem, self.cnt[sem]))
        tok = None
        for out, in_ in pairs:
            if accum:
                tok = self.op(eng, lambda e, o=out, i=in_:
                              e.dma_start(out=o, in_=i, accum_op=OP.add),
                              inc=sem)
            else:
                tok = self.op(eng, lambda e, o=out, i=in_:
                              e.dma_start(out=o, in_=i), inc=sem)
        return tok

    def dma(self, eng, out, in_):
        return self.dma_group(eng, [(out, in_)])

    def all_dma_toks(self, eng):
        return [(s, self.cnt[s]) for s in self.pool[eng] if self.cnt[s] > 0]

    def wait(self, eng, tok):
        if tok is None:
            return
        sem, val = tok
        k = (eng, sem)
        if self.last_wait.get(k, 0) >= val:
            return
        self.last_wait[k] = val
        self.ev(eng, lambda e, s=self.sems[sem], v=val: e.wait_ge(s, v))

    def waits(self, eng, toks):
        for t in toks:
            self.wait(eng, t)

    def drain(self, eng):
        if self._drained.get(eng):
            return
        self.events.append((eng, lambda e: e.drain()))
        self._drained[eng] = True

    def mm_group(self, psum_ap, mms):
        n = len(mms)
        tok = None
        for i, (w, m) in enumerate(mms):
            tok = self.op("tensor", lambda e, o=psum_ap, ww=w, mm=m,
                          s=(i == 0), sp=(i == n - 1):
                          e.matmul(o, ww, mm, start=s, stop=sp),
                          inc="tv" if i == n - 1 else None)
        return tok

    def mm_group_w(self, psum_ap, mms, pre):
        n = len(mms)
        tok = None
        for i, (w, m) in enumerate(mms):
            if pre[i]:
                self.waits("tensor", pre[i])
            tok = self.op("tensor", lambda e, o=psum_ap, ww=w, mm=m,
                          s=(i == 0), sp=(i == n - 1):
                          e.matmul(o, ww, mm, start=s, stop=sp),
                          inc="tv" if i == n - 1 else None)
        return tok

    def mm_group_halves(self, half_aps, mms):
        firsts, lasts = {}, {}
        for i, (_, _, h) in enumerate(mms):
            if h not in firsts:
                firsts[h] = i
            lasts[h] = i
        tok = None
        for i, (w, m, h) in enumerate(mms):
            tok = self.op("tensor", lambda e, o=half_aps[h], ww=w, mm=m,
                          s=(i == firsts[h]), sp=(i == lasts[h]):
                          e.matmul(o, ww, mm, start=s, stop=sp),
                          inc="tv" if i == len(mms) - 1 else None)
        return tok


class Region:
    def __init__(self, nc, base, top):
        self.nc = nc
        self.cur = base
        self.top = top

    def t(self, name, shape, dtype):
        per_part = int(np.prod(shape[1:])) * mybir.dt.size(dtype)
        addr = (self.cur + 63) & ~63
        self.cur = addr + per_part
        assert self.cur <= self.top, (
            f"SBUF overflow: {name} ends at {self.cur} > {self.top}")
        return self.nc.alloc_sbuf_tensor_at(name, shape, dtype, offset=addr)


def hilo_mms(whi_t, wlo_t, xhi_t, xlo_t, wslices, xslice):
    mms = []
    for ki in range(NCH):
        wh, wl = wslices(whi_t, ki), wslices(wlo_t, ki)
        xh, xl = xslice(xhi_t, ki), xslice(xlo_t, ki)
        mms.append((wh, xh))
        mms.append((wh, xl))
        mms.append((wl, xh))
    return mms


def build_core(S):
    nc = bass.Bass("TRN2", target_bir_lowering=False)
    NU = S // TU
    NBLK = S // 128
    QG = 1024 if S % 1024 == 0 else 512
    NG = S // QG
    SUBS = QG // 128
    KCH_PER_G = QG // 512

    xd = nc.dram_tensor("x", [S, D], F32, kind="ExternalInput")
    wd = {w: nc.dram_tensor(w + "t", [D, D], F32, kind="ExternalInput")
          for w in ("wq", "wk", "wv", "wo", "wr", "wk2", "wv2")}
    bqf = nc.dram_tensor("bqf", [128, 8], F32, kind="ExternalInput")
    bkf = nc.dram_tensor("bkf", [128, 8], F32, kind="ExternalInput")
    bof = nc.dram_tensor("bof", [128, 8], F32, kind="ExternalInput")
    murf = nc.dram_tensor("murf", [128, 8], F32, kind="ExternalInput")
    mukf = nc.dram_tensor("mukf", [128, 8], F32, kind="ExternalInput")
    bvbd = nc.dram_tensor("bvb", [128, D], F32, kind="ExternalInput")
    idbd = nc.dram_tensor("idb", [128, 128], BF16, kind="ExternalInput")
    idfd = nc.dram_tensor("idf", [128, 128], F32, kind="ExternalInput")
    maskd = nc.dram_tensor("mask", [128, 128], F32, kind="ExternalInput")
    epsd = nc.dram_tensor("epsc", [128, 1], F32, kind="ExternalInput")
    yd = nc.dram_tensor("y", [S, D], F32, kind="ExternalOutput")

    sqh = nc.dram_tensor("sqh", [NCH, 128, S], BF16, kind="Internal")
    sql = nc.dram_tensor("sql", [NCH, 128, S], BF16, kind="Internal")
    skh = nc.dram_tensor("skh", [NCH, 128, S], BF16, kind="Internal")
    skl = nc.dram_tensor("skl", [NCH, 128, S], BF16, kind="Internal")
    svh = nc.dram_tensor("svh", [NBLK, 128, D], BF16, kind="Internal")
    svl = nc.dram_tensor("svl", [NBLK, 128, D], BF16, kind="Internal")
    sx1 = nc.dram_tensor("sx1", [NBLK, 128, D], F32, kind="Internal")
    xr3 = xd.ap().rearrange("(n p) m -> n p m", p=128)
    yr3 = yd.ap().rearrange("(n p) m -> n p m", p=128)

    ps_a = nc.alloc_psum_tensor("ps_a", [128, 512], F32)
    ps_b = nc.alloc_psum_tensor("ps_b", [128, 512], F32)
    ps_c = nc.alloc_psum_tensor("ps_c", [128, 1024], F32)
    ps_d = nc.alloc_psum_tensor("ps_d", [128, 1024], F32)
    ps_t1 = nc.alloc_psum_tensor("ps_t1", [128, 1024], BF16)
    ps_t2 = nc.alloc_psum_tensor("ps_t2", [128, 1024], BF16)
    psS = [ps_a, ps_b]
    psH = [ps_c.ap()[:, 0:512], ps_c.ap()[:, 512:1024],
           ps_d.ap()[:, 0:512], ps_d.ap()[:, 512:1024]]
    psO = [ps_c, ps_d]
    psT = [ps_t1, ps_t2]

    c_idb = nc.alloc_sbuf_tensor("c_idb", [128, 128], BF16)
    c_idf = nc.alloc_sbuf_tensor("c_idf", [128, 128], F32)
    c_mask = nc.alloc_sbuf_tensor("c_mask", [128, 128], F32)
    c_bq = nc.alloc_sbuf_tensor("c_bq", [128, 8], F32)
    c_bk = nc.alloc_sbuf_tensor("c_bk", [128, 8], F32)
    c_bo = nc.alloc_sbuf_tensor("c_bo", [128, 8], F32)
    c_mur = nc.alloc_sbuf_tensor("c_mur", [128, 8], F32)
    c_muk = nc.alloc_sbuf_tensor("c_muk", [128, 8], F32)
    c_bv = nc.alloc_sbuf_tensor("c_bv", [128, D], F32)
    c_eps = nc.alloc_sbuf_tensor("c_eps", [128, 1], F32)

    ov_base = nc.sbuf_base + 4096
    ov_top = nc.sbuf_top

    p = Prog(nc)

    tok_consts = p.dma_group("sync", [
        (dst.ap(), srcd.ap()) for dst, srcd in
        ((c_idb, idbd), (c_idf, idfd), (c_mask, maskd),
         (c_bq, bqf), (c_bk, bkf), (c_bo, bof),
         (c_mur, murf), (c_muk, mukf), (c_bv, bvbd),
         (c_eps, epsd))])
    for engx in ("tensor", "vector", "scalar", "gpsimd"):
        p.wait(engx, tok_consts)

    def weight_piece(wdram, whi, wlo, stg, copy_toks, sub_toks, ki):
        wr = wdram.ap().rearrange("(c p) m -> c p m", p=128)
        par = ki % len(stg)
        p.waits("sync", [copy_toks[par], sub_toks[par]])
        ld = p.dma("sync", stg[par].ap(), wr[ki])
        p.wait("vector", ld)
        dsth = whi.ap()[:, ki * 1024:(ki + 1) * 1024]
        tok_c = p.op("vector", lambda e, d=dsth, s=stg[par].ap():
                     e.tensor_copy(d, s), inc="vv")
        p.wait("gpsimd", tok_c)
        g1 = p.op("gpsimd", lambda e, d=wlo.ap()[:, ki * 1024:
                  ki * 1024 + 512],
                  s=stg[par].ap()[:, 0:512],
                  h=whi.ap()[:, ki * 1024: ki * 1024 + 512]:
                  e.tensor_tensor(d, s, h, op=OP.subtract), inc="gv")
        p.drain("vector")
        g2 = p.op("vector", lambda e, d=wlo.ap()[:, ki * 1024 + 512:
                  (ki + 1) * 1024],
                  s=stg[par].ap()[:, 512:1024],
                  h=whi.ap()[:, ki * 1024 + 512:(ki + 1) * 1024]:
                  e.tensor_tensor(d, s, h, op=OP.subtract), inc="vv")
        copy_toks[par] = g2
        sub_toks[par] = g1

    def load_weight(wdram, whi, wlo, stg, copy_toks, sub_toks):
        for ki in range(NCH):
            weight_piece(wdram, whi, wlo, stg, copy_toks, sub_toks, ki)
        return copy_toks[:] + sub_toks[:]

    def load_weight_sg(wdram, whi, wlo, stg, copy_toks, sub_toks):
        wr = wdram.ap().rearrange("(c p) m -> c p m", p=128)
        ns = len(stg)
        for ki in range(NCH):
            par = ki % ns
            p.waits("sync", [copy_toks[par], sub_toks[par]])
            ld = p.dma("sync", stg[par].ap(), wr[ki])
            p.wait("scalar", ld)
            p.drain("scalar")
            tok_c = p.op("scalar", lambda e,
                         d=whi.ap()[:, ki * 1024:(ki + 1) * 1024],
                         s=stg[par].ap(): e.copy(d, s), inc="sv")
            p.wait("gpsimd", tok_c)
            g1 = p.op("gpsimd", lambda e, d=wlo.ap()[:, ki * 1024:
                      ki * 1024 + 512],
                      s=stg[par].ap()[:, 0:512],
                      h=whi.ap()[:, ki * 1024: ki * 1024 + 512]:
                      e.tensor_tensor(d, s, h, op=OP.subtract), inc="gv")
            g2 = p.op("gpsimd", lambda e, d=wlo.ap()[:, ki * 1024 + 512:
                      (ki + 1) * 1024],
                      s=stg[par].ap()[:, 512:1024],
                      h=whi.ap()[:, ki * 1024 + 512:(ki + 1) * 1024]:
                      e.tensor_tensor(d, s, h, op=OP.subtract), inc="gv")
            copy_toks[par] = tok_c
            sub_toks[par] = g2
        return copy_toks[:] + sub_toks[:] + [g1]

    def layernorm(x_tm, bn_sb, mv_sb, rst4, rinv4, in_toks, out_tm=None):
        p.waits("vector", in_toks)
        for blk in range(4):
            for h in range(2):
                src = x_tm.ap()[:, blk * 1024 + h * 512: blk * 1024 + (h + 1) * 512]
                dst = bn_sb.ap()[:, (blk * 2 + h) * 6:(blk * 2 + h + 1) * 6]
                p.op("vector", lambda e, d=dst, s=src: e.bn_stats(d, s))
        p.drain("vector")
        tok_mv = None
        for blk in range(4):
            src = bn_sb.ap()[:, blk * 12:(blk + 1) * 12]
            dst = mv_sb.ap()[:, blk * 2:(blk + 1) * 2]
            tok_mv = p.op("vector", lambda e, d=dst, s=src: e.bn_aggr(d, s),
                          inc="vv")
        p.wait("scalar", tok_mv)
        var_ap = bass.AP(mv_sb, 1, [[8, 128], [2, 4]])
        tok_sq = p.op("scalar", lambda e, d=rst4.ap(), s=var_ap:
                      e.activation(d, s, AF.Sqrt, bias=c_eps.ap()), inc="sv")
        p.wait("vector", tok_sq)
        p.op("vector", lambda e, d=rinv4.ap(), s=rst4.ap(): e.reciprocal(d, s))
        p.drain("vector")
        tok = None
        dst_t = x_tm if out_tm is None else out_tm
        for blk in range(4):
            xa = x_tm.ap()[:, blk * 1024:(blk + 1) * 1024]
            da = dst_t.ap()[:, blk * 1024:(blk + 1) * 1024]
            mcol = mv_sb.ap()[:, blk * 2: blk * 2 + 1]
            rcol = rinv4.ap()[:, blk: blk + 1]
            tok = p.op("vector", lambda e, d=da, s=xa, m=mcol, r=rcol:
                       e.tensor_scalar(d, s, m, r, op0=OP.subtract, op1=OP.mult),
                       inc="vv")
        return tok

    ra = Region(nc, ov_base, ov_top)
    wq_hi = ra.t("wq_hi", [128, 8192], BF16)
    wq_lo = ra.t("wq_lo", [128, 8192], BF16)
    wk_hi = ra.t("wk_hi", [128, 8192], BF16)
    wk_lo = ra.t("wk_lo", [128, 8192], BF16)
    wv_hi = ra.t("wv_hi", [128, 8192], BF16)
    wv_lo = ra.t("wv_lo", [128, 8192], BF16)
    stgA = [ra.t(f"stgA{i}", [128, 1024], F32) for i in range(3)]
    xa_tm2 = [ra.t("xa_tm", [128, 4096], F32), ra.t("xa_tmB", [128, 4096], F32)]
    yT = ra.t("yT", [128, 4096], F32)
    yT_hi = ra.t("yT_hi", [128, 4096], BF16)
    yT_lo = ra.t("yT_lo", [128, 4096], BF16)
    bnA = ra.t("bnA", [128, 48], F32)
    mvA = ra.t("mvA", [128, 8], F32)
    rstA = ra.t("rstA", [128, 4], F32)
    rinvA = ra.t("rinvA", [128, 4], F32)
    qk_f = [ra.t("qk_f0", [128, 512], F32), ra.t("qk_f1", [128, 512], F32)]
    qk_h = [ra.t("qk_h0", [128, 512], BF16), ra.t("qk_h1", [128, 512], BF16)]
    qk_l = [ra.t("qk_l0", [128, 512], BF16), ra.t("qk_l1", [128, 512], BF16)]
    v_f = [ra.t("v_f0", [128, 1024], F32), ra.t("v_f1", [128, 1024], F32)]
    v_h = [ra.t("v_h0", [128, 1024], BF16), ra.t("v_h1", [128, 1024], BF16)]
    v_l = [ra.t("v_l0", [128, 1024], BF16), ra.t("v_l1", [128, 1024], BF16)]
    assert ra.cur <= ov_top

    def wsl(t, ki, mo):
        return t.ap()[:, ki * 1024 + mo * 128: ki * 1024 + (mo + 1) * 128]

    def wsl512(t, ki, h):
        return t.ap()[:, ki * 1024 + h * 512: ki * 1024 + (h + 1) * 512]

    def ysl(t, ki):
        return t.ap()[:, ki * 512:(ki + 1) * 512]

    def ysl_blk(t, ki, blk):
        return t.ap()[:, ki * 512 + blk * 128: ki * 512 + (blk + 1) * 128]

    x_free_tok = [None, None]
    psS_free = [None, None]
    psHO_free = [None, None]
    qk_busy = [()] * 2
    v_busy = [()] * 2
    gqk = 0
    gv_i = 0
    prev_ylo = None
    prev_ylo2 = None

    def split_halves(dst_t, src_t, hi_t, n, tok_hi):
        h = (n * 5) // 32 // 128 * 128
        p.wait("gpsimd", tok_hi)
        g1 = p.op("gpsimd", lambda e, d=dst_t.ap()[:, 0:h],
                  s=src_t.ap()[:, 0:h], hh=hi_t.ap()[:, 0:h]:
                  e.tensor_tensor(d, s, hh, op=OP.subtract), inc="gv")
        p.drain("vector")
        g2 = p.op("vector", lambda e, d=dst_t.ap()[:, h:n],
                  s=src_t.ap()[:, h:n], hh=hi_t.ap()[:, h:n]:
                  e.tensor_tensor(d, s, hh, op=OP.subtract), inc="vv")
        return g1, g2

    def pa_ln(u):
        xa_tm = xa_tm2[u % 2]
        p.wait("sync", x_free_tok[u % 2])
        tok_ldx = p.dma_group("sync", [
            (xa_tm.ap()[:, blk * 1024:(blk + 1) * 1024], xr3[u * 4 + blk])
            for blk in range(4)])
        return layernorm(xa_tm, bnA, mvA, rstA, rinvA, [tok_ldx])

    def pa_front(u, tok_y):
        nonlocal prev_ylo, prev_ylo2
        xa_tm = xa_tm2[u % 2]
        p.wait("tensor", tok_y)
        p.wait("vector", prev_ylo)
        p.wait("vector", prev_ylo2)
        last_t = None
        for c in range(NCH):
            par = c % 2
            p.wait("tensor", psS_free[par])
            for blk in range(4):
                srcx = xa_tm.ap()[:, blk * 1024 + c * 128:
                                  blk * 1024 + (c + 1) * 128]
                last_t = p.op("tensor", lambda e,
                              d=psS[par].ap()[:, blk * 128:(blk + 1) * 128],
                              s=srcx: e.transpose(d, s, c_idf.ap()),
                              inc="tv" if blk == 3 else None)
            p.wait("vector", last_t)
            dst = yT.ap()[:, c * 512:(c + 1) * 512]
            psS_free[par] = p.op("vector", lambda e, d=dst,
                                 s=psS[par].ap()[:, 0:512]:
                                 e.tensor_copy(d, s), inc="vv")
        x_free_tok[u % 2] = last_t

        p.drain("vector")
        tok_yhi = p.op("vector", lambda e: e.tensor_copy(yT_hi.ap(), yT.ap()),
                       inc="vv")
        g1, g2 = split_halves(yT_lo, yT, yT_hi, 4096, tok_yhi)
        prev_ylo, prev_ylo2 = g1, g2
        return [tok_yhi, g1, g2]

    def pa_back(u, ytoks, wtoks):
        nonlocal gqk, gv_i
        wq_toks, wk_toks, wv_toks = wtoks
        p.waits("tensor", ytoks)

        for proj, whi, wlo, bias, sh_d, sl_d in (
                (0, wq_hi, wq_lo, c_bq, sqh, sql),
                (1, wk_hi, wk_lo, c_bk, skh, skl)):
            p.waits("tensor", wq_toks if proj == 0 else wk_toks)
            for mo in range(NCH):
                par = gqk % 2
                p.wait("tensor", psS_free[par])
                mms = hilo_mms(whi, wlo, yT_hi, yT_lo,
                               lambda t, ki, m=mo: wsl(t, ki, m), ysl)
                tok_mm = p.mm_group(psS[par].ap(), mms)
                for t in qk_busy[par]:
                    p.wait("vector", t)
                p.wait("vector", tok_mm)
                bcol = bias.ap()[:, mo:mo + 1]
                p.op("vector", lambda e, d=qk_f[par].ap(), s=psS[par].ap(),
                     bc=bcol: e.tensor_scalar(d, s, bc, None, op0=OP.add))
                p.drain("vector")
                psS_free[par] = p.op("vector", lambda e, d=qk_h[par].ap(),
                                     s=qk_f[par].ap(): e.tensor_copy(d, s),
                                     inc="vv")
                tok_hi = psS_free[par]
                p.wait("gpsimd", tok_hi)
                tok_lo = p.op("gpsimd", lambda e, d=qk_l[par].ap(),
                              s=qk_f[par].ap(), h=qk_h[par].ap():
                              e.tensor_tensor(d, s, h, op=OP.subtract), inc="gv")
                p.wait("scalar", tok_hi)
                st1 = p.dma("scalar", sh_d.ap()[mo][:, u * 512:(u + 1) * 512],
                            qk_h[par].ap())
                p.wait("scalar", tok_lo)
                st2 = p.dma("scalar", sl_d.ap()[mo][:, u * 512:(u + 1) * 512],
                            qk_l[par].ap())
                qk_busy[par] = (st1, st2, tok_lo)
                gqk += 1

        p.waits("tensor", wv_toks)
        for blk in range(4):
            par = gv_i % 2
            pst = psO[par]
            p.wait("tensor", psHO_free[par])
            mms = []
            for ki in range(NCH):
                yh = ysl_blk(yT_hi, ki, blk)
                yl = ysl_blk(yT_lo, ki, blk)
                for h in range(2):
                    mms.append((yh, wsl512(wv_hi, ki, h), h))
                    mms.append((yh, wsl512(wv_lo, ki, h), h))
                    mms.append((yl, wsl512(wv_hi, ki, h), h))
            halves = [pst.ap()[:, 0:512], pst.ap()[:, 512:1024]]
            tok_mm = p.mm_group_halves(halves, mms)
            for t in v_busy[par]:
                p.wait("vector", t)
            p.wait("vector", tok_mm)
            p.op("vector", lambda e, d=v_f[par].ap(), s=pst.ap(),
                 b_=c_bv.ap(): e.tensor_tensor(d, s, b_, op=OP.add))
            p.drain("vector")
            psHO_free[par] = p.op("vector", lambda e, d=v_h[par].ap(),
                                  s=v_f[par].ap(): e.tensor_copy(d, s),
                                  inc="vv")
            tok_hi = psHO_free[par]
            p.wait("gpsimd", tok_hi)
            tok_lo = p.op("gpsimd", lambda e, d=v_l[par].ap(),
                          s=v_f[par].ap(), h=v_h[par].ap():
                          e.tensor_tensor(d, s, h, op=OP.subtract), inc="gv")
            p.wait("scalar", tok_hi)
            st1 = p.dma("scalar", svh.ap()[u * 4 + blk], v_h[par].ap())
            p.wait("scalar", tok_lo)
            st2 = p.dma("scalar", svl.ap()[u * 4 + blk], v_l[par].ap())
            v_busy[par] = (st1, st2, tok_lo)
            gv_i += 1

    ty0 = pa_ln(0)
    ytoks0 = pa_front(0, ty0)
    stg_c, stg_s = [None] * 3, [None] * 3
    wq_toks = load_weight_sg(wd["wq"], wq_hi, wq_lo, stgA, stg_c, stg_s)
    wk_toks = load_weight_sg(wd["wk"], wk_hi, wk_lo, stgA, stg_c, stg_s)
    wv_toks = load_weight_sg(wd["wv"], wv_hi, wv_lo, stgA, stg_c, stg_s)
    wtoks = (wq_toks, wk_toks, wv_toks)
    ty_next = pa_ln(1) if NU > 1 else None
    pa_back(0, ytoks0, wtoks)
    for u in range(1, NU):
        yt = pa_front(u, ty_next)
        ty_next = pa_ln(u + 1) if u + 1 < NU else None
        pa_back(u, yt, wtoks)

    tok_pa_stores = p.all_dma_toks("scalar")

    TUC = 256
    NU2 = S // TUC
    rd = Region(nc, ov_base, ov_top)
    wk2_hi = rd.t("wk2_hi", [128, 8192], BF16)
    wk2_lo = rd.t("wk2_lo", [128, 8192], BF16)
    stgD = [rd.t("stgD0", [128, 1024], F32), rd.t("stgD1", [128, 1024], F32)]
    wr_hi = rd.t("wr_hi", [128, 8192], BF16)
    wr_lo = rd.t("wr_lo", [128, 8192], BF16)
    wv2_hi = rd.t("wv2_hi", [128, 8192], BF16)
    wv2_lo = rd.t("wv2_lo", [128, 8192], BF16)
    x1c = [rd.t("x1c0", [128, 2048], F32), rd.t("x1c1", [128, 2048], F32)]
    y2c = [rd.t("y2c0", [128, 2048], F32), rd.t("y2c1", [128, 2048], F32)]
    y2Tc = [rd.t("y2Tc0", [128, 257], F32), rd.t("y2Tc1", [128, 257], F32)]
    sliv = rd.t("sliv", [128, 8], F32)
    d1c = [rd.t("d1c0", [128, 256], F32), rd.t("d1c1", [128, 256], F32)]
    xrf = [rd.t("xrf0", [128, 256], F32), rd.t("xrf1", [128, 256], F32)]
    xkf = [rd.t("xkf0", [128, 256], F32), rd.t("xkf1", [128, 256], F32)]
    xr_hi = [rd.t("xr_hi0", [128, 2048], BF16), rd.t("xr_hi1", [128, 2048], BF16)]
    xr_lo = [rd.t("xr_lo0", [128, 2048], BF16), rd.t("xr_lo1", [128, 2048], BF16)]
    xk_hi = [rd.t("xk_hi0", [128, 2048], BF16), rd.t("xk_hi1", [128, 2048], BF16)]
    xk_lo = [rd.t("xk_lo0", [128, 2048], BF16), rd.t("xk_lo1", [128, 2048], BF16)]
    bnD = rd.t("bnD", [128, 24], F32)
    mvD = rd.t("mvD", [128, 4], F32)
    rstD = rd.t("rstD", [128, 2], F32)
    rinvD = rd.t("rinvD", [128, 2], F32)
    k2_hi = rd.t("k2_hi", [128, 2048], BF16)
    k2_lo = rd.t("k2_lo", [128, 2048], BF16)
    e_sb = [rd.t("e_sb0", [128, 256], F32), rd.t("e_sb1", [128, 256], F32)]
    t_sb = [rd.t("t_sb0", [128, 256], F32), rd.t("t_sb1", [128, 256], F32)]
    of32 = rd.t("of32", [128, 256], F32)
    xin = rd.t("xin", [128, 2048], F32)
    assert rd.cur <= ov_top

    def ln2_stats(x_tm, in_toks):
        p.waits("vector", in_toks)
        for blk in range(2):
            for h in range(2):
                srcb = x_tm.ap()[:, blk * 1024 + h * 512:
                                 blk * 1024 + (h + 1) * 512]
                dstb = bnD.ap()[:, (blk * 2 + h) * 6:(blk * 2 + h + 1) * 6]
                p.op("vector", lambda e, d=dstb, s=srcb: e.bn_stats(d, s))
        p.drain("vector")

    def ln2_norm(x_tm, out_tm):
        tok_mv = None
        for blk in range(2):
            srcb = bnD.ap()[:, blk * 12:(blk + 1) * 12]
            dstb = mvD.ap()[:, blk * 2:(blk + 1) * 2]
            tok_mv = p.op("vector", lambda e, d=dstb, s=srcb: e.bn_aggr(d, s),
                          inc="vv")
        p.wait("scalar", tok_mv)
        var_ap = bass.AP(mvD, 1, [[4, 128], [2, 2]])
        tok_sq = p.op("scalar", lambda e, d=rstD.ap(), s=var_ap:
                      e.activation(d, s, AF.Sqrt, bias=c_eps.ap()), inc="sv")
        p.wait("vector", tok_sq)
        p.op("vector", lambda e, d=rinvD.ap(), s=rstD.ap(): e.reciprocal(d, s))
        p.drain("vector")
        tok = None
        for blk in range(2):
            xa = x_tm.ap()[:, blk * 1024:(blk + 1) * 1024]
            da = out_tm.ap()[:, blk * 1024:(blk + 1) * 1024]
            mcol = mvD.ap()[:, blk * 2: blk * 2 + 1]
            rcol = rinvD.ap()[:, blk: blk + 1]
            tok = p.op("vector", lambda e, d=da, s=xa, m=mcol, r=rcol:
                       e.tensor_scalar(d, s, m, r, op0=OP.subtract,
                                       op1=OP.mult), inc="vv")
        return tok

    def ln2(x_tm, out_tm, in_toks):
        ln2_stats(x_tm, in_toks)
        return ln2_norm(x_tm, out_tm)

    y_store_par = [None, None]
    mix_free = [None, None]
    prev_split2 = [None, None]
    th2, tl2 = [None, None], [None, None]
    prev_k2lo = None
    k2_read_done = [None, None]
    tok_y2u = [None, None]
    ghd = 0

    def ysl256(t, ki):
        return t.ap()[:, ki * 256:(ki + 1) * 256]

    fa_ld = [None, None]
    fa_ldx = [None]

    def c2_front_a_load(u):
        par = u % 2
        p.waits("sync", acc_group[(u * 2) // 8])
        if u == 0:
            p.waits("sync", [("vv", p.cnt["vv"]), ("sv", p.cnt["sv"]),
                             ("gv", p.cnt["gv"]), ("tv", p.cnt["tv"])])
        else:
            p.waits("sync", [tok_y2u[1 - par]])
        fa_ldx[0] = p.dma_group("sync", [
            (xin.ap(), bass.AP(sx1, u * 2 * 131072,
                               [[1024, 128], [131072, 2], [1, 1024]]))])
        p.wait("sync", y_store_par[par])
        p.waits("sync", tok_kv_dead)
        fa_ld[par] = p.dma_group("sync", [
            (x1c[par].ap(), bass.AP(sx1, u * 2 * 131072,
                                    [[1024, 128], [131072, 2], [1, 1024]]))])

    def c2_front_a_stats(u):
        par = u % 2
        if u == 0:
            p.waits("vector", tok_kv_dead)
        ln2_stats(xin, [fa_ldx[0]])

    def c2_front_a_norm(u):
        par = u % 2
        tok_y2u[par] = ln2_norm(xin, y2c[par])

    def c2_front_a(u):
        c2_front_a_load(u)
        c2_front_a_stats(u)
        c2_front_a_norm(u)

    def c2_front_b(u):
        par = u % 2
        split_last = []
        ch_split = {}

        def chunk(c):
            yc = y2Tc[c % 2]
            if c == 0:
                p.wait("tensor", tok_y2u[par])
                p.wait("vector", prev_k2lo)
            p.waits("vector", [mix_free[c % 2], tok_sliv])
            p.drain("vector")
            p.op("vector", lambda e, d=yc.ap()[:, 0:1],
                 s=sliv.ap()[:, c:c + 1]: e.tensor_copy(d, s))
            part = c % 2
            p.wait("tensor", psS_free[part])
            tt = None
            for blk in range(2):
                srcy = y2c[par].ap()[:, blk * 1024 + c * 128:
                                     blk * 1024 + (c + 1) * 128]
                tt = p.op("tensor", lambda e,
                          d=psS[part].ap()[:, blk * 128:(blk + 1) * 128],
                          s=srcy: e.transpose(d, s, c_idf.ap()),
                          inc="tv" if blk == 1 else None)
            p.wait("vector", tt)
            psS_free[part] = p.op("vector", lambda e, d=yc.ap()[:, 1:257],
                                  s=psS[part].ap()[:, 0:256]:
                                  e.tensor_copy(d, s), inc="vv")
            p.drain("vector")
            p.op("vector", lambda e, d=sliv.ap()[:, c:c + 1],
                 s=yc.ap()[:, 256:257]: e.tensor_copy(d, s))
            p.drain("vector")
            p.op("vector", lambda e, d=d1c[par].ap(), a=yc.ap()[:, 1:257],
                 b_=yc.ap()[:, 0:256]:
                 e.tensor_tensor(d, a, b_, op=OP.subtract))
            p.drain("vector")
            cp = c % 2
            p.wait("vector", prev_split2[cp])
            p.wait("vector", k2_read_done[par])
            p.op("vector", lambda e, d=xrf[cp].ap(), a=d1c[par].ap(),
                 mu=c_mur.ap()[:, c:c + 1], b_=yc.ap()[:, 0:256]:
                 e.scalar_tensor_tensor(d, a, mu, b_, op0=OP.mult, op1=OP.add))
            tok_mix = p.op("vector", lambda e, d=xkf[cp].ap(),
                           a=d1c[par].ap(),
                           mu=c_muk.ap()[:, c:c + 1],
                           b_=yc.ap()[:, 0:256]:
                           e.scalar_tensor_tensor(d, a, mu, b_, op0=OP.mult,
                                                  op1=OP.add), inc="vv")
            mix_free[c % 2] = tok_mix
            p.drain("vector")
            t1 = p.op("vector", lambda e,
                      d=xr_hi[par].ap()[:, c * 256:(c + 1) * 256],
                      s=xrf[cp].ap(): e.tensor_copy(d, s), inc="vv")
            t2 = p.op("vector", lambda e,
                      d=xk_hi[par].ap()[:, c * 256:(c + 1) * 256],
                      s=xkf[cp].ap(): e.tensor_copy(d, s), inc="vv")
            p.waits("gpsimd", [t1, t2])
            t3 = p.op("gpsimd", lambda e,
                      d=xr_lo[par].ap()[:, c * 256:(c + 1) * 256],
                      s=xrf[cp].ap(),
                      h=xr_hi[par].ap()[:, c * 256:(c + 1) * 256]:
                      e.tensor_tensor(d, s, h, op=OP.subtract), inc="gv")
            t4 = p.op("gpsimd", lambda e,
                      d=xk_lo[par].ap()[:, c * 256:(c + 1) * 256],
                      s=xkf[cp].ap(),
                      h=xk_hi[par].ap()[:, c * 256:(c + 1) * 256]:
                      e.tensor_tensor(d, s, h, op=OP.subtract), inc="gv")
            prev_split2[cp] = t4
            split_last[:] = [t1, t2, t3, t4]
            ch_split[c] = [t1, t2, t3, t4]
        return [lambda c=c: chunk(c) for c in range(NCH)], split_last, ch_split

    def c2_back(u, split_toks, chs, fillers=None):
        nonlocal ghd, prev_k2lo
        par = u % 2
        gidx = [0]

        def fill():
            if fillers is not None and gidx[0] % 3 == 2:
                ci = gidx[0] // 3
                if ci < NCH:
                    fillers[ci]()
            gidx[0] += 1

        p.waits("tensor", wtoks_wr)
        r_t = y2c[par]
        for mo in range(NCH):
            parh = ghd % 4
            p.wait("tensor", psH_free[parh])
            mms = hilo_mms(wr_hi, wr_lo, xr_hi[par], xr_lo[par],
                           lambda t, ki, m=mo: wsl(t, ki, m), ysl256)
            if mo == 0:
                pre = [None] * len(mms)
                for ki in range(NCH):
                    pre[3 * ki] = [chs[ki][0], chs[ki][2]]
                tok_mm = p.mm_group_w(psH[parh][:, 0:256], mms, pre)
            else:
                tok_mm = p.mm_group(psH[parh][:, 0:256], mms)
            p.wait("scalar", tok_mm)
            psH_free[parh] = p.op("scalar", lambda e,
                                  d=r_t.ap()[:, mo * 256:(mo + 1) * 256],
                                  s=psH[parh][:, 0:256]:
                                  e.activation(d, s, AF.Sigmoid), inc="sv")
            ghd += 1
            fill()
        if u == 0:
            p.waits("sync", tok_kv_dead)
            wv2_toks.extend(load_weight(wd["wv2"], wv2_hi, wv2_lo, stgD,
                                        stg_c2, stg_s2))
        p.waits("tensor", wk2_toks)
        k2s = {}
        for mo in range(NCH):
            mp = mo % 2
            parh = ghd % 4
            p.wait("tensor", psH_free[parh])
            mms = hilo_mms(wk2_hi, wk2_lo, xk_hi[par], xk_lo[par],
                           lambda t, ki, m=mo: wsl(t, ki, m), ysl256)
            if mo == 0:
                pre = [None] * len(mms)
                for ki in range(NCH):
                    pre[3 * ki] = [chs[ki][1], chs[ki][3]]
                tok_mm = p.mm_group_w(psH[parh][:, 0:256], mms, pre)
            else:
                tok_mm = p.mm_group(psH[parh][:, 0:256], mms)
            p.wait("scalar", tok_mm)
            p.waits("scalar", [th2[mp], tl2[mp]])
            p.drain("scalar")
            psH_free[parh] = p.op("scalar", lambda e, d=e_sb[mp].ap(),
                                  s=psH[parh][:, 0:256]:
                                  e.activation(d, s, AF.Relu), inc="sv")
            p.drain("scalar")
            tok_t = p.op("scalar", lambda e, d=t_sb[mp].ap(),
                         s=e_sb[mp].ap():
                         e.activation(d, s, AF.Square), inc="sv")
            p.drain("scalar")
            p.wait("vector", tok_t)
            th = p.op("vector", lambda e,
                      d=k2_hi.ap()[:, mo * 256:(mo + 1) * 256],
                      s=t_sb[mp].ap(): e.tensor_copy(d, s), inc="vv")
            p.wait("gpsimd", th)
            tl = p.op("gpsimd", lambda e,
                      d=k2_lo.ap()[:, mo * 256:(mo + 1) * 256],
                      s=t_sb[mp].ap(),
                      h=k2_hi.ap()[:, mo * 256:(mo + 1) * 256]:
                      e.tensor_tensor(d, s, h, op=OP.subtract), inc="gv")
            th2[mp], tl2[mp] = th, tl
            k2s[mo] = [th, tl]
            ghd += 1
            fill()
        prev_k2lo = tl
        k2_read_done[par] = None
        p.waits("tensor", wv2_toks)
        p.waits("vector", [fa_ld[par]])
        ev_last = None
        for mo in range(NCH):
            parh = ghd % 4
            p.wait("tensor", psH_free[parh])
            mms = hilo_mms(wv2_hi, wv2_lo, k2_hi, k2_lo,
                           lambda t, ki, m=mo: wsl(t, ki, m), ysl256)
            if mo == 0:
                pre = [None] * len(mms)
                for ki in range(NCH):
                    pre[3 * ki] = k2s[ki]
                tok_mm = p.mm_group_w(psH[parh][:, 0:256], mms, pre)
            else:
                tok_mm = p.mm_group(psH[parh][:, 0:256], mms)
            p.waits("vector", [tok_mm])
            tok_of = p.op("vector", lambda e, d=of32.ap(),
                          s=psH[parh][:, 0:256],
                          r=r_t.ap()[:, mo * 256:(mo + 1) * 256]:
                          e.tensor_tensor(d, s, r, op=OP.mult), inc="vv")
            psH_free[parh] = tok_of
            p.waits("tensor", [tok_of])
            part = ghd % 2
            p.wait("tensor", psS_free[part])
            tt = None
            for blk in range(2):
                srco2 = of32.ap()[:, blk * 128:(blk + 1) * 128]
                tt = p.op("tensor", lambda e,
                          d=psS[part].ap()[:, blk * 128:(blk + 1) * 128],
                          s=srco2: e.transpose(d, s, c_idf.ap()),
                          inc="tv" if blk == 1 else None)
            p.wait("vector", tt)
            xs = bass.AP(x1c[par], mo * 128, [[2048, 128], [1024, 2], [1, 128]])
            ps3 = bass.AP(psS[part], 0, [[512, 128], [128, 2], [1, 128]])
            ev_last = p.op("vector", lambda e, d=xs, s=ps3:
                           e.tensor_tensor(d, s, d, op=OP.add), inc="vv")
            psS_free[part] = ev_last
            ghd += 1
            fill()
        k2_read_done[par] = ("tv", p.cnt["tv"])
        p.wait("scalar", ev_last)
        y_store_par[par] = p.dma_group("scalar", [
            (bass.AP(yd, u * 2 * 131072,
                     [[1024, 128], [131072, 2], [1, 1024]]),
             x1c[par].ap())])


    r_wo = Region(nc, ov_base, ov_top)
    wo_hi = r_wo.t("wo_hi", [128, 8192], BF16)
    wo_lo = r_wo.t("wo_lo", [128, 8192], BF16)
    stgC = [r_wo.t("stgC0", [128, 1024], F32), r_wo.t("stgC1", [128, 1024], F32)]
    ov_base2 = r_wo.cur
    rb = Region(nc, ov_base2, ov_top)
    q_addr = (rb.cur + 63) & ~63
    qhiT = rb.t("qhiT", [128, 8192], BF16)
    qloT = rb.t("qloT", [128, 8192], BF16)
    o_hi = nc.alloc_sbuf_tensor_at("o_hi", [128, 8192], BF16, offset=q_addr)
    o_lo = nc.alloc_sbuf_tensor_at("o_lo", [128, 8192], BF16,
                                   offset=q_addr + 16384)
    khiT = [rb.t("khiT0", [128, 4096], BF16), rb.t("khiT1", [128, 4096], BF16)]
    kloT = [rb.t("kloT0", [128, 4096], BF16), rb.t("kloT1", [128, 4096], BF16)]
    vhiT = [rb.t("vhiT0", [128, 4096], BF16), rb.t("vhiT1", [128, 4096], BF16)]
    vloT = [rb.t("vloT0", [128, 4096], BF16), rb.t("vloT1", [128, 4096], BF16)]
    o_addr = (rb.cur + 63) & ~63
    o_sb = rb.t("o_sb", [128, 8192], F32)
    oTb_hi = nc.alloc_sbuf_tensor_at("oTb_hi", [128, 8192], BF16,
                                     offset=o_addr)
    oTb_lo = nc.alloc_sbuf_tensor_at("oTb_lo", [128, 8192], BF16,
                                     offset=o_addr + 16384)
    ab_addr = (rb.cur + 63) & ~63
    at_bk = [rb.t("at_bk0", [128, 512], F32), rb.t("at_bk1", [128, 512], F32)]
    at_bk2 = [rb.t("at_bk20", [128, 512], F32), rb.t("at_bk21", [128, 512], F32)]
    a_sb = [rb.t("a_sb0", [128, 512], F32), rb.t("a_sb1", [128, 512], F32),
            rb.t("a_sb2", [128, 512], F32), rb.t("a_sb3", [128, 512], F32)]
    p_sb = [rb.t("p_sb0", [128, 512], F32), rb.t("p_sb1", [128, 512], F32)]
    phi = [rb.t("phi0", [128, 512], BF16), rb.t("phi1", [128, 512], BF16)]
    plo = [rb.t("plo0", [128, 512], BF16), rb.t("plo1", [128, 512], BF16)]
    phiT = [rb.t("phiT0", [128, 512], BF16), rb.t("phiT1", [128, 512], BF16)]
    ploT = [rb.t("ploT0", [128, 512], BF16), rb.t("ploT1", [128, 512], BF16)]
    nmr = rb.t("nmr", [128, 16], F32)
    js = nc.alloc_sbuf_tensor_at("js", [128, 512], F32, offset=ab_addr)
    lr = rb.t("lr", [128, 8], F32)
    nmj = rb.t("nmj", [128, 1], F32)
    nmnew = rb.t("nmnew", [128, 2], F32)
    nm32 = rb.t("nm32", [128, 1], F32)
    alph = [rb.t("alph0", [128, 1], F32), rb.t("alph1", [128, 1], F32)]
    lpart = [rb.t("lpart0", [128, 1], F32), rb.t("lpart1", [128, 1], F32)]
    linv = rb.t("linv", [128, 1], F32)
    assert rb.cur <= ov_top

    p.waits("sync", tok_pa_stores)

    kv_busy = [None, None]
    psT_free = [None, None]
    psO_freeB = [None, None]
    psH_free = [None, None, None, None]
    wtoks_o = []
    at_free2 = [None, None]
    acc_toks = [None, None]
    ghb = 0
    tok_n = None
    alpha_busy = [None, None]

    kv_toks = {}

    def load_q(g):
        return p.dma_group("sync", [
            (qhiT.ap()[:, c * 1024:(c + 1) * 1024],
             sqh.ap()[c][:, g * QG:(g + 1) * QG]) for c in range(NCH)] + [
            (qloT.ap()[:, c * 1024:(c + 1) * 1024],
             sql.ap()[c][:, g * QG:(g + 1) * QG]) for c in range(NCH)])

    def load_kv(g, j, q="sync"):
        par = j % 2
        p.wait(q, kv_busy[par])
        prs = []
        for c in range(NCH):
            prs.append((khiT[par].ap()[:, c * 512:(c + 1) * 512],
                        skh.ap()[c][:, j * 512:(j + 1) * 512]))
            prs.append((kloT[par].ap()[:, c * 512:(c + 1) * 512],
                        skl.ap()[c][:, j * 512:(j + 1) * 512]))
        for kb in range(4):
            prs.append((vhiT[par].ap()[:, kb * 1024:(kb + 1) * 1024],
                        svh.ap()[j * 4 + kb]))
            prs.append((vloT[par].ap()[:, kb * 1024:(kb + 1) * 1024],
                        svl.ap()[j * 4 + kb]))
        kv_toks[(g, j)] = p.dma_group(q, prs)

    tok_q_next = load_q(0)
    p.waits("gpsimd", tok_pa_stores)
    load_kv(0, 0, q="gpsimd")
    prev_c1_free = None
    acc_group = []
    oT_done_last = [None]
    stg_c2, stg_s2 = [None, None], [None, None]
    wtoks_o.extend(load_weight(wd["wo"], wo_hi, wo_lo, stgC,
                               stg_c2, stg_s2))
    c2_hooked = [False]
    wtoks_wr = []

    for g in range(NG):
        nkc = KCH_PER_G * (g + 1)
        tok_q = tok_q_next

        sub_ph = [0] * SUBS
        pairs = []
        for j in range(nkc):
            smin = max(0, 4 * j - 8 * g)
            for s in range(smin, SUBS):
                qi = g * SUBS + s
                n_k = min(512, (qi + 1) * 128 - j * 512)
                pairs.append((j, s, n_k))

        def emit_qk(i):
            j, s, n_k = pairs[i]
            par = i % 2
            kpar = j % 2
            p.waits("tensor", [tok_q, kv_toks[(g, j)], psS_free[par]])
            mms = []
            for ki in range(NCH):
                qh = qhiT.ap()[:, ki * 1024 + s * 128: ki * 1024 + (s + 1) * 128]
                ql = qloT.ap()[:, ki * 1024 + s * 128: ki * 1024 + (s + 1) * 128]
                kh = khiT[kpar].ap()[:, ki * 512: ki * 512 + n_k]
                kl = kloT[kpar].ap()[:, ki * 512: ki * 512 + n_k]
                mms += [(qh, kh), (qh, kl), (ql, kh)]
            return p.mm_group(psS[par].ap()[:, 0:n_k], mms)

        def emit_abs(i, tok_s):
            j, s, n_k = pairs[i]
            par = i % 2
            p.wait("scalar", tok_s)
            return p.op("scalar", lambda e, d=a_sb[i % 4].ap()[:, 0:n_k],
                        s_=psS[par].ap()[:, 0:n_k]:
                        e.activation(d, s_, AF.Abs), inc="sv")

        def emit_rest(i, tok_a):
            j, s, n_k = pairs[i]
            par = i % 2
            kpar = j % 2
            qi = g * SUBS + s
            diag = (qi + 1) * 128 - j * 512 <= 512
            nkb = n_k // 128
            first = j == 0
            p.wait("vector", tok_a)
            if diag:
                p.op("vector", lambda e,
                     d=a_sb[i % 4].ap()[:, n_k - 128:n_k]:
                     e.tensor_tensor(d, d, c_mask.ap(), op=OP.add))
                p.drain("vector")
            ph = sub_ph[s]
            c_old = 2 * s + ph
            c_new = 2 * s + (1 - ph)
            sub_ph[s] = 1 - ph
            if first:
                p.op("vector", lambda e, d=nmr.ap()[:, c_new:c_new + 1],
                     s_=a_sb[i % 4].ap()[:, 0:n_k]:
                     e.tensor_reduce(d, s_, axis=AX.X, op=OP.max, negate=True))
                p.drain("vector")
                tok_stats = p.op("vector", lambda e, d=nm32.ap(),
                                 s_=nmr.ap()[:, c_new:c_new + 1]:
                                 e.tensor_scalar(d, s_, 1.0 / 32, None,
                                                 op0=OP.mult), inc="vv")
                tok_alpha = None
            else:
                p.op("vector", lambda e, j_=js.ap()[:, 0:n_k],
                     s_=a_sb[i % 4].ap()[:, 0:n_k],
                     sd=nmr.ap()[:, c_old:c_old + 1],
                     ac=nmr.ap()[:, c_new:c_new + 1]:
                     e.tensor_scalar(j_, s_, -1.0, sd, op0=OP.mult, op1=OP.min,
                                     accum_out=ac))
                p.drain("vector")
                p.op("vector", lambda e, d=nmnew.ap()[:, 1:2],
                     a=nmr.ap()[:, c_new:c_new + 1],
                     b_=nmr.ap()[:, c_old:c_old + 1]:
                     e.tensor_tensor(d, a, b_, op=OP.subtract))
                tok_stats = p.op("vector", lambda e, d=nm32.ap(),
                                 s_=nmr.ap()[:, c_new:c_new + 1]:
                                 e.tensor_scalar(d, s_, 1.0 / 32, None,
                                                 op0=OP.mult), inc="vv")
            p.wait("scalar", tok_stats)
            p.wait("scalar", alpha_busy[par])
            p.drain("scalar")
            tok_alpha = None
            if not first:
                tok_alpha = p.op("scalar", lambda e, d=alph[par].ap(),
                                 s_=nmnew.ap()[:, 1:2]:
                                 e.activation(d, s_, AF.Exp, scale=1.0 / 32),
                                 inc="sv")
            tok_p = p.op("scalar", lambda e, d=p_sb[par].ap()[:, 0:n_k],
                         s_=a_sb[i % 4].ap()[:, 0:n_k], b_=nm32.ap(),
                         ac=lpart[par].ap():
                         e.activation(d, s_, AF.Exp, scale=1.0 / 32, bias=b_,
                                      accum_out=ac), inc="sv")
            p.wait("vector", tok_p)
            tok_phi = p.op("vector", lambda e, d=phi[par].ap()[:, 0:n_k],
                           s_=p_sb[par].ap()[:, 0:n_k]: e.tensor_copy(d, s_),
                           inc="vv")
            p.wait("gpsimd", tok_phi)
            tok_plo = p.op("gpsimd", lambda e, d=plo[par].ap()[:, 0:n_k],
                           s_=p_sb[par].ap()[:, 0:n_k],
                           h=phi[par].ap()[:, 0:n_k]:
                           e.tensor_tensor(d, s_, h, op=OP.subtract), inc="gv")
            p.waits("tensor", [tok_phi, psT_free[0]])
            tk1 = None
            for kb in range(nkb):
                src = phi[par].ap()[:, kb * 128:(kb + 1) * 128]
                dst = psT[0].ap()[:, kb * 128:(kb + 1) * 128]
                tk1 = p.op("tensor", lambda e, d=dst, s_=src:
                           e.transpose(d, s_, c_idb.ap()),
                           inc="tv" if kb == nkb - 1 else None)
            p.waits("tensor", [tok_plo, psT_free[1]])
            tk2 = None
            for kb in range(nkb):
                src = plo[par].ap()[:, kb * 128:(kb + 1) * 128]
                dst = psT[1].ap()[:, kb * 128:(kb + 1) * 128]
                tk2 = p.op("tensor", lambda e, d=dst, s_=src:
                           e.transpose(d, s_, c_idb.ap()),
                           inc="tv" if kb == nkb - 1 else None)
            p.wait("vector", tk1)
            psT_free[0] = p.op("vector", lambda e,
                               d=phiT[par].ap()[:, 0:n_k],
                               s_=psT[0].ap()[:, 0:n_k]: e.tensor_copy(d, s_),
                               inc="vv")
            p.wait("vector", tk2)
            psT_free[1] = p.op("vector", lambda e,
                               d=ploT[par].ap()[:, 0:n_k],
                               s_=psT[1].ap()[:, 0:n_k]: e.tensor_copy(d, s_),
                               inc="vv")
            opar = i % 2
            p.waits("tensor", [psT_free[0], psT_free[1], kv_toks[(g, j)],
                               psO_freeB[opar]])
            mms = []
            for kb in range(nkb):
                ph = phiT[par].ap()[:, kb * 128:(kb + 1) * 128]
                pl = ploT[par].ap()[:, kb * 128:(kb + 1) * 128]
                for h in range(2):
                    vh = vhiT[kpar].ap()[:, kb * 1024 + h * 512:
                                         kb * 1024 + (h + 1) * 512]
                    vl = vloT[kpar].ap()[:, kb * 1024 + h * 512:
                                         kb * 1024 + (h + 1) * 512]
                    mms.append((ph, vh, h))
                    mms.append((ph, vl, h))
                    mms.append((pl, vh, h))
            halves = [psO[opar].ap()[:, 0:512], psO[opar].ap()[:, 512:1024]]
            tok_pv = p.mm_group_halves(halves, mms)
            p.waits("vector", [tok_pv, tok_alpha, tok_p])
            lsl = lr.ap()[:, s:s + 1]
            dsl = o_sb.ap()[:, s * 1024:(s + 1) * 1024]
            if first:
                if prev_c1_free is not None:
                    p.waits("vector", prev_c1_free)
                psO_freeB[opar] = p.op("vector", lambda e, d=dsl,
                                       ps_=psO[opar].ap():
                                       e.tensor_copy(d, ps_), inc="vv")
                alpha_busy[par] = p.op(
                    "vector", lambda e, d=lsl, lp=lpart[par].ap():
                    e.tensor_copy(d, lp), inc="vv")
            else:
                psO_freeB[opar] = p.op("vector", lambda e, d=dsl,
                                       al=alph[par].ap(), ps_=psO[opar].ap():
                                       e.scalar_tensor_tensor(
                                           d, d, al, ps_, op0=OP.mult,
                                           op1=OP.add), inc="vv")
                alpha_busy[par] = p.op(
                    "vector", lambda e, d=lsl, al=alph[par].ap(),
                    lp=lpart[par].ap():
                    e.scalar_tensor_tensor(d, d, al, lp, op0=OP.mult,
                                           op1=OP.add), inc="vv")

        npairs = len(pairs)
        tokS = [None] * npairs
        tokA = [None] * npairs
        tv_after_qk = None
        tokS[0] = emit_qk(0)
        tokA[0] = emit_abs(0, tokS[0])
        psS_free[0] = tokA[0]
        for ii in (1, 2):
            if ii < npairs:
                jn = pairs[ii][0]
                if (g, jn) not in kv_toks:
                    load_kv(g, jn)
                tokS[ii] = emit_qk(ii)
                if ii == 1:
                    tokA[1] = emit_abs(1, tokS[1])
                    psS_free[1] = tokA[1]
        for i in range(npairs):
            j, s, n_k = pairs[i]
            if i + 3 < npairs:
                jn3 = pairs[i + 3][0]
                if (g, jn3) not in kv_toks:
                    load_kv(g, jn3)
                tokS[i + 3] = emit_qk(i + 3)
            if i + 2 < npairs:
                tokA[i + 2] = emit_abs(i + 2, tokS[i + 2])
                psS_free[i % 2] = tokA[i + 2]
            if i + 3 >= npairs:
                tv_after_qk = ("tv", p.cnt["tv"])
            emit_rest(i, tokA[i])
            qi = g * SUBS + s
            if (qi + 1) * 128 - j * 512 <= 512:
                p.drain("vector")
                p.op("vector", lambda e, d=linv.ap(), s_=lr.ap()[:, s:s + 1]:
                     e.reciprocal(d, s_))
                p.drain("vector")
                tok_n = p.op("vector", lambda e,
                             d=o_sb.ap()[:, s * 1024:(s + 1) * 1024],
                             li=linv.ap():
                             e.tensor_scalar(d, d, li, None, op0=OP.mult),
                             inc="vv")
                p.drain("vector")
            if i + 1 == len(pairs) or pairs[i + 1][0] != j:
                kv_busy[j % 2] = ("tv", p.cnt["tv"])
        if g + 1 < NG:
            load_kv(g + 1, 0)
            load_kv(g + 1, 1)
        for k in range(4):
            psH_free[k] = ("vv", p.cnt["vv"])
        tok_base = p.dma_group("sync", [
            (sx1.ap()[g * 8 + b], xr3[g * 8 + b]) for b in range(8)])
        p.waits("scalar", [tok_n, tv_after_qk])
        p.waits("gpsimd", [tok_n, tv_after_qk])
        chunk_hi = [None] * NCH
        chunk_lo = [None] * NCH
        for c in range(NCH):
            sap3 = bass.AP(o_sb, c * 128, [[8192, 128], [1024, 8], [1, 128]])
            hap3 = bass.AP(o_hi, c * 128, [[8192, 128], [1024, 8], [1, 128]])
            lap3 = bass.AP(o_lo, c * 128, [[8192, 128], [1024, 8], [1, 128]])
            p.drain("scalar")
            chunk_hi[c] = p.op("scalar", lambda e, d=hap3, s_=sap3:
                               e.copy(d, s_), inc="sv")
            p.wait("gpsimd", chunk_hi[c])
            chunk_lo[c] = p.op("gpsimd", lambda e, d=lap3, s_=sap3, h=hap3:
                               e.tensor_tensor(d, s_, h, op=OP.subtract),
                               inc="gv")
        ev_toks = []
        for src_t, toks in ((o_hi, chunk_hi), (o_lo, chunk_lo)):
            dst_t = oTb_hi if src_t is o_hi else oTb_lo
            for c in range(NCH):
                tp = len(ev_toks) % 2
                p.waits("tensor", [toks[c], psT_free[tp]])
                tt = None
                for b in range(8):
                    sap = src_t.ap()[:, b * 1024 + c * 128:
                                     b * 1024 + (c + 1) * 128]
                    tt = p.op("tensor", lambda e,
                              d=psT[tp].ap()[:, b * 128:(b + 1) * 128],
                              s_=sap: e.transpose(d, s_, c_idb.ap()),
                              inc="tv" if b == 7 else None)
                p.wait("vector", tt)
                p.wait("vector", chunk_lo[NCH - 1])
                dap = dst_t.ap()[:, c * 1024:(c + 1) * 1024]
                psT_free[tp] = p.op("vector", lambda e, d=dap,
                                    s_=psT[tp].ap()[:, 0:1024]:
                                    e.tensor_copy(d, s_), inc="vv")
                ev_toks.append(psT_free[tp])
        tok_oT_done = ("tv", p.cnt["tv"])
        p.waits("tensor", wtoks_o + [ev_toks[-1]])
        for h in range(2):
            for mo in range(NCH):
                parh = ghb % 4
                p.wait("tensor", psH_free[parh])
                mms = hilo_mms(wo_hi, wo_lo, oTb_hi, oTb_lo,
                               lambda t, ki, m=mo: wsl(t, ki, m),
                               lambda t, ki, hh=h:
                               t.ap()[:, ki * 1024 + hh * 512:
                                      ki * 1024 + (hh + 1) * 512])
                tok_mm = p.mm_group(psH[parh], mms)
                para = ghb % 2
                p.waits("vector", [tok_mm, at_free2[para]])
                bcol = c_bo.ap()[:, mo:mo + 1]
                tok_at = p.op("vector", lambda e, d=at_bk[para].ap(),
                              s_=psH[parh], bc=bcol:
                              e.tensor_scalar(d, s_, bc, None, op0=OP.add),
                              inc="vv")
                psH_free[parh] = tok_at
                p.waits("tensor", [tok_at])
                part = ghb % 2
                p.wait("tensor", psS_free[part])
                tt = None
                for blk in range(4):
                    sap = at_bk[para].ap()[:, blk * 128:(blk + 1) * 128]
                    tt = p.op("tensor", lambda e,
                              d=psS[part].ap()[:, blk * 128:(blk + 1) * 128],
                              s_=sap: e.transpose(d, s_, c_idf.ap()),
                              inc="tv" if blk == 3 else None)
                at_free2[para] = tt
                p.wait("vector", tt)
                p.wait("vector", acc_toks[para])
                tok_ev = p.op("vector", lambda e, d=at_bk2[para].ap(),
                              s_=psS[part].ap()[:, 0:512]:
                              e.tensor_copy(d, s_), inc="vv")
                psS_free[part] = tok_ev
                p.wait("gpsimd", tok_base)
                p.wait("gpsimd", tok_ev)
                dst3 = bass.AP(sx1, (g * 8 + h * 4) * 128 * 1024 + mo * 128,
                               [[1024, 128], [128 * 1024, 4], [1, 128]])
                src3 = bass.AP(at_bk2[para], 0, [[512, 128], [128, 4], [1, 128]])
                acc_toks[para] = p.dma_group("gpsimd", [(dst3, src3)],
                                             accum=True)
                ghb += 1
                if g == NG - 1 and g >= 1:
                    slot = h * NCH + mo
                    if 1 <= slot <= 8:
                        if slot == 1:
                            p.wait("sync", tok_oT_done)
                        weight_piece(wd["wr"], wr_hi, wr_lo, stgD,
                                     stg_c2, stg_s2, slot - 1)
                        if slot == 8:
                            wtoks_wr = stg_c2[:] + stg_s2[:]
                            tok_kv_dead = [kv_busy[0], kv_busy[1]]
                            c2_front_a_load(0)
                            c2_hooked[0] = True
                    elif slot == 10:
                        c2_front_a_stats(0)
                    elif slot == 12:
                        c2_front_a_norm(0)
        prev_c1_free = [("vv", p.cnt["vv"]), ("tv", p.cnt["tv"])]
        psO_freeB[0] = ("vv", p.cnt["vv"])
        psO_freeB[1] = ("vv", p.cnt["vv"])
        acc_group.append([t for t in acc_toks if t is not None])
        oT_done_last[0] = tok_oT_done
        if g + 1 < NG:
            p.wait("sync", tok_oT_done)
            tok_q_next = load_q(g + 1)

    tok_pb_stores = p.all_dma_toks("scalar") + p.all_dma_toks("gpsimd")
    tok_kv_dead = [kv_busy[0], kv_busy[1]]
    tok_q_dead = oT_done_last[0]
    tok_wo_dead = ("tv", p.cnt["tv"])

    p.wait("vector", tok_wo_dead)
    tok_sliv = p.op("vector", lambda e: e.memset(sliv.ap(), 0.0), inc="vv")
    p.drain("vector")

    wk2_toks = []
    wv2_toks = []
    if not c2_hooked[0]:
        c2_front_a(0)
        p.wait("sync", tok_q_dead)
        wtoks_wr = load_weight(wd["wr"], wr_hi, wr_lo, stgD, stg_c2, stg_s2)
    ch0, s0, cm0 = c2_front_b(0)
    p.waits("gpsimd", acc_group[-1] + [tok_wo_dead])
    for f in ch0:
        f()
    p.wait("sync", tok_wo_dead)
    wk2_toks = load_weight_sg(wd["wk2"], wk2_hi, wk2_lo, stgD,
                              stg_c2, stg_s2)
    splits = {0: list(s0)}
    chmaps = {0: cm0}
    for u in range(1, NU2):
        c2_front_a(u)
        chu, su, cmu = c2_front_b(u)
        c2_back(u - 1, splits[u - 1], chmaps[u - 1], fillers=chu)
        splits[u] = list(su)
        chmaps[u] = cmu
    c2_back(NU2 - 1, splits[NU2 - 1], chmaps[NU2 - 1])

    p.waits("sync", p.all_dma_toks("scalar"))
    p.waits("sync", p.all_dma_toks("sync"))

    with nc.Block() as block:
        @block.sync
        def _(eng):
            for name, fn in p.events:
                if name == "sync":
                    fn(eng)

        @block.tensor
        def _(eng):
            for name, fn in p.events:
                if name == "tensor":
                    fn(eng)

        @block.vector
        def _(eng):
            for name, fn in p.events:
                if name == "vector":
                    fn(eng)

        @block.scalar
        def _(eng):
            for name, fn in p.events:
                if name == "scalar":
                    fn(eng)

        @block.gpsimd
        def _(eng):
            for name, fn in p.events:
                if name == "gpsimd":
                    fn(eng)

    return nc


def _host_prep(inputs, S):
    f32 = np.float32
    w = {}
    w["wqt"] = np.ascontiguousarray(inputs["L0_attn_Wq"].T.astype(f32))
    w["wkt"] = np.ascontiguousarray(inputs["L0_attn_Wk"].T.astype(f32))
    w["wvt"] = np.ascontiguousarray(inputs["L0_attn_Wv"].T.astype(f32))
    w["wot"] = np.ascontiguousarray(inputs["L0_attn_Wo"].T.astype(f32))
    w["wrt"] = np.ascontiguousarray(inputs["L0_ffn_W_r"].T.astype(f32))
    w["wk2t"] = np.ascontiguousarray(inputs["L0_ffn_W_k"].T.astype(f32))
    w["wv2t"] = np.ascontiguousarray(inputs["L0_ffn_W_v"].T.astype(f32))

    def fm(v):
        return np.ascontiguousarray(np.asarray(v, f32).reshape(8, 128).T)

    w["bqf"] = fm(inputs["L0_attn_bq"])
    w["bkf"] = fm(inputs["L0_attn_bk"])
    w["bof"] = fm(inputs["L0_attn_bo"])
    w["murf"] = fm(inputs["L0_ffn_mu_r"])
    w["mukf"] = fm(inputs["L0_ffn_mu_k"])
    w["bvb"] = np.ascontiguousarray(
        np.broadcast_to(np.asarray(inputs["L0_attn_bv"], f32)[None, :],
                        (128, D)))
    w["idb"] = np.eye(128, dtype=ml_dtypes.bfloat16)
    w["epsc"] = np.full((128, 1), 1e-5, f32)
    w["idf"] = np.eye(128, dtype=f32)
    mask = np.zeros((128, 128), f32)
    iu = np.triu_indices(128, 1)
    mask[iu] = -1.0e30
    w["mask"] = mask
    return w


def kernel(**inputs):
    x = np.asarray(inputs["L0_attn_x"], np.float32)
    Bsz, S, _ = x.shape
    assert Bsz == 8
    nc = build_core(S)
    shared = _host_prep(inputs, S)
    in_maps = []
    for b in range(Bsz):
        m = dict(shared)
        m["x"] = np.ascontiguousarray(x[b])
        in_maps.append(m)
    res = run_bass_kernel_spmd(nc, in_maps, core_ids=list(range(8)))
    return np.stack([np.asarray(r["y"], np.float32) for r in res.results],
                    axis=0)


# version 56
# speedup vs baseline: 1.0194x; 1.0050x over previous
import numpy as np
import ml_dtypes
import concourse.bass as bass
import concourse.mybir as mybir
from concourse.bass_utils import run_bass_kernel_spmd

F32 = mybir.dt.float32
BF16 = mybir.dt.bfloat16
AF = mybir.ActivationFunctionType
OP = mybir.AluOpType
AX = mybir.AxisListType

D = 1024
NCH = 8
TU = 512


class Prog:

    NPOOL = 6

    def __init__(self, nc):
        self.nc = nc
        self.events = []
        self.cnt = {}
        self.sems = {}
        self.last_wait = {}
        self._drained = {}
        names = ["tv", "vv", "sv", "gv"]
        self.pool = {"sync": [], "scalar": [], "gpsimd": []}
        for q in ("sync", "scalar", "gpsimd"):
            for i in range(self.NPOOL):
                n = f"dma_{q}_{i}"
                names.append(n)
                self.pool[q].append(n)
        for s in names:
            self.sems[s] = nc.alloc_semaphore(s)
            self.cnt[s] = 0
        self.pool_idx = {"sync": 0, "scalar": 0, "gpsimd": 0}
        self.DMA_SEMS = tuple(n for n in names if n.startswith("dma_"))

    def ev(self, eng, fn):
        self.events.append((eng, fn))
        self._drained[eng] = False

    def op(self, eng, fn, inc=None):
        if inc is None:
            self.ev(eng, fn)
            return None
        n = 16 if inc in self.DMA_SEMS else 1
        self.cnt[inc] += n
        tgt = self.cnt[inc]

        def run(e, f=fn, s=self.sems[inc], k=n):
            f(e).then_inc(s, k)

        self.ev(eng, run)
        return (inc, tgt)

    def dma_group(self, eng, pairs, accum=False):
        sem = self.pool[eng][self.pool_idx[eng] % self.NPOOL]
        self.pool_idx[eng] += 1
        if self.cnt[sem] > 0:
            self.wait(eng, (sem, self.cnt[sem]))
        tok = None
        for out, in_ in pairs:
            if accum:
                tok = self.op(eng, lambda e, o=out, i=in_:
                              e.dma_start(out=o, in_=i, accum_op=OP.add),
                              inc=sem)
            else:
                tok = self.op(eng, lambda e, o=out, i=in_:
                              e.dma_start(out=o, in_=i), inc=sem)
        return tok

    def dma(self, eng, out, in_):
        return self.dma_group(eng, [(out, in_)])

    def all_dma_toks(self, eng):
        return [(s, self.cnt[s]) for s in self.pool[eng] if self.cnt[s] > 0]

    def wait(self, eng, tok):
        if tok is None:
            return
        sem, val = tok
        k = (eng, sem)
        if self.last_wait.get(k, 0) >= val:
            return
        self.last_wait[k] = val
        self.ev(eng, lambda e, s=self.sems[sem], v=val: e.wait_ge(s, v))

    def waits(self, eng, toks):
        for t in toks:
            self.wait(eng, t)

    def drain(self, eng):
        if self._drained.get(eng):
            return
        self.events.append((eng, lambda e: e.drain()))
        self._drained[eng] = True

    def mm_group(self, psum_ap, mms):
        n = len(mms)
        tok = None
        for i, (w, m) in enumerate(mms):
            tok = self.op("tensor", lambda e, o=psum_ap, ww=w, mm=m,
                          s=(i == 0), sp=(i == n - 1):
                          e.matmul(o, ww, mm, start=s, stop=sp),
                          inc="tv" if i == n - 1 else None)
        return tok

    def mm_group_w(self, psum_ap, mms, pre):
        n = len(mms)
        tok = None
        for i, (w, m) in enumerate(mms):
            if pre[i]:
                self.waits("tensor", pre[i])
            tok = self.op("tensor", lambda e, o=psum_ap, ww=w, mm=m,
                          s=(i == 0), sp=(i == n - 1):
                          e.matmul(o, ww, mm, start=s, stop=sp),
                          inc="tv" if i == n - 1 else None)
        return tok

    def mm_group_halves(self, half_aps, mms):
        firsts, lasts = {}, {}
        for i, (_, _, h) in enumerate(mms):
            if h not in firsts:
                firsts[h] = i
            lasts[h] = i
        tok = None
        for i, (w, m, h) in enumerate(mms):
            tok = self.op("tensor", lambda e, o=half_aps[h], ww=w, mm=m,
                          s=(i == firsts[h]), sp=(i == lasts[h]):
                          e.matmul(o, ww, mm, start=s, stop=sp),
                          inc="tv" if i == len(mms) - 1 else None)
        return tok


class Region:
    def __init__(self, nc, base, top):
        self.nc = nc
        self.cur = base
        self.top = top

    def t(self, name, shape, dtype):
        per_part = int(np.prod(shape[1:])) * mybir.dt.size(dtype)
        addr = (self.cur + 63) & ~63
        self.cur = addr + per_part
        assert self.cur <= self.top, (
            f"SBUF overflow: {name} ends at {self.cur} > {self.top}")
        return self.nc.alloc_sbuf_tensor_at(name, shape, dtype, offset=addr)


def hilo_mms(whi_t, wlo_t, xhi_t, xlo_t, wslices, xslice):
    mms = []
    for ki in range(NCH):
        wh, wl = wslices(whi_t, ki), wslices(wlo_t, ki)
        xh, xl = xslice(xhi_t, ki), xslice(xlo_t, ki)
        mms.append((wh, xh))
        mms.append((wh, xl))
        mms.append((wl, xh))
    return mms


def build_core(S):
    nc = bass.Bass("TRN2", target_bir_lowering=False)
    NU = S // TU
    NBLK = S // 128
    QG = 1024 if S % 1024 == 0 else 512
    NG = S // QG
    SUBS = QG // 128
    KCH_PER_G = QG // 512

    xd = nc.dram_tensor("x", [S, D], F32, kind="ExternalInput")
    wd = {w: nc.dram_tensor(w + "t", [D, D], F32, kind="ExternalInput")
          for w in ("wq", "wk", "wv", "wo", "wr", "wk2", "wv2")}
    bqf = nc.dram_tensor("bqf", [128, 8], F32, kind="ExternalInput")
    bkf = nc.dram_tensor("bkf", [128, 8], F32, kind="ExternalInput")
    bof = nc.dram_tensor("bof", [128, 8], F32, kind="ExternalInput")
    murf = nc.dram_tensor("murf", [128, 8], F32, kind="ExternalInput")
    mukf = nc.dram_tensor("mukf", [128, 8], F32, kind="ExternalInput")
    bvbd = nc.dram_tensor("bvb", [128, D], F32, kind="ExternalInput")
    idbd = nc.dram_tensor("idb", [128, 128], BF16, kind="ExternalInput")
    idfd = nc.dram_tensor("idf", [128, 128], F32, kind="ExternalInput")
    maskd = nc.dram_tensor("mask", [128, 128], F32, kind="ExternalInput")
    epsd = nc.dram_tensor("epsc", [128, 1], F32, kind="ExternalInput")
    yd = nc.dram_tensor("y", [S, D], F32, kind="ExternalOutput")

    sqh = nc.dram_tensor("sqh", [NCH, 128, S], BF16, kind="Internal")
    sql = nc.dram_tensor("sql", [NCH, 128, S], BF16, kind="Internal")
    skh = nc.dram_tensor("skh", [NCH, 128, S], BF16, kind="Internal")
    skl = nc.dram_tensor("skl", [NCH, 128, S], BF16, kind="Internal")
    svh = nc.dram_tensor("svh", [NBLK, 128, D], BF16, kind="Internal")
    svl = nc.dram_tensor("svl", [NBLK, 128, D], BF16, kind="Internal")
    sx1 = nc.dram_tensor("sx1", [NBLK, 128, D], F32, kind="Internal")
    xr3 = xd.ap().rearrange("(n p) m -> n p m", p=128)
    yr3 = yd.ap().rearrange("(n p) m -> n p m", p=128)

    ps_a = nc.alloc_psum_tensor("ps_a", [128, 512], F32)
    ps_b = nc.alloc_psum_tensor("ps_b", [128, 512], F32)
    ps_c = nc.alloc_psum_tensor("ps_c", [128, 1024], F32)
    ps_d = nc.alloc_psum_tensor("ps_d", [128, 1024], F32)
    ps_t1 = nc.alloc_psum_tensor("ps_t1", [128, 1024], BF16)
    ps_t2 = nc.alloc_psum_tensor("ps_t2", [128, 1024], BF16)
    psS = [ps_a, ps_b]
    psH = [ps_c.ap()[:, 0:512], ps_c.ap()[:, 512:1024],
           ps_d.ap()[:, 0:512], ps_d.ap()[:, 512:1024]]
    psO = [ps_c, ps_d]
    psT = [ps_t1, ps_t2]

    c_idb = nc.alloc_sbuf_tensor("c_idb", [128, 128], BF16)
    c_idf = nc.alloc_sbuf_tensor("c_idf", [128, 128], F32)
    c_mask = nc.alloc_sbuf_tensor("c_mask", [128, 128], F32)
    c_bq = nc.alloc_sbuf_tensor("c_bq", [128, 8], F32)
    c_bk = nc.alloc_sbuf_tensor("c_bk", [128, 8], F32)
    c_bo = nc.alloc_sbuf_tensor("c_bo", [128, 8], F32)
    c_mur = nc.alloc_sbuf_tensor("c_mur", [128, 8], F32)
    c_muk = nc.alloc_sbuf_tensor("c_muk", [128, 8], F32)
    c_bv = nc.alloc_sbuf_tensor("c_bv", [128, D], F32)
    c_eps = nc.alloc_sbuf_tensor("c_eps", [128, 1], F32)

    ov_base = nc.sbuf_base + 4096
    ov_top = nc.sbuf_top

    p = Prog(nc)

    tok_consts = p.dma_group("sync", [
        (dst.ap(), srcd.ap()) for dst, srcd in
        ((c_idb, idbd), (c_idf, idfd), (c_mask, maskd),
         (c_bq, bqf), (c_bk, bkf), (c_bo, bof),
         (c_mur, murf), (c_muk, mukf), (c_bv, bvbd),
         (c_eps, epsd))])
    for engx in ("tensor", "vector", "scalar", "gpsimd"):
        p.wait(engx, tok_consts)

    def weight_piece(wdram, whi, wlo, stg, copy_toks, sub_toks, ki):
        wr = wdram.ap().rearrange("(c p) m -> c p m", p=128)
        par = ki % len(stg)
        p.waits("sync", [copy_toks[par], sub_toks[par]])
        ld = p.dma("sync", stg[par].ap(), wr[ki])
        p.wait("vector", ld)
        dsth = whi.ap()[:, ki * 1024:(ki + 1) * 1024]
        tok_c = p.op("vector", lambda e, d=dsth, s=stg[par].ap():
                     e.tensor_copy(d, s), inc="vv")
        p.wait("gpsimd", tok_c)
        g1 = p.op("gpsimd", lambda e, d=wlo.ap()[:, ki * 1024:
                  ki * 1024 + 512],
                  s=stg[par].ap()[:, 0:512],
                  h=whi.ap()[:, ki * 1024: ki * 1024 + 512]:
                  e.tensor_tensor(d, s, h, op=OP.subtract), inc="gv")
        p.drain("vector")
        g2 = p.op("vector", lambda e, d=wlo.ap()[:, ki * 1024 + 512:
                  (ki + 1) * 1024],
                  s=stg[par].ap()[:, 512:1024],
                  h=whi.ap()[:, ki * 1024 + 512:(ki + 1) * 1024]:
                  e.tensor_tensor(d, s, h, op=OP.subtract), inc="vv")
        copy_toks[par] = g2
        sub_toks[par] = g1

    def load_weight(wdram, whi, wlo, stg, copy_toks, sub_toks):
        for ki in range(NCH):
            weight_piece(wdram, whi, wlo, stg, copy_toks, sub_toks, ki)
        return copy_toks[:] + sub_toks[:]

    def load_weight_sg(wdram, whi, wlo, stg, copy_toks, sub_toks):
        wr = wdram.ap().rearrange("(c p) m -> c p m", p=128)
        ns = len(stg)
        for ki in range(NCH):
            par = ki % ns
            p.waits("sync", [copy_toks[par], sub_toks[par]])
            ld = p.dma("sync", stg[par].ap(), wr[ki])
            p.wait("scalar", ld)
            p.drain("scalar")
            tok_c = p.op("scalar", lambda e,
                         d=whi.ap()[:, ki * 1024:(ki + 1) * 1024],
                         s=stg[par].ap(): e.copy(d, s), inc="sv")
            p.wait("gpsimd", tok_c)
            g1 = p.op("gpsimd", lambda e, d=wlo.ap()[:, ki * 1024:
                      ki * 1024 + 512],
                      s=stg[par].ap()[:, 0:512],
                      h=whi.ap()[:, ki * 1024: ki * 1024 + 512]:
                      e.tensor_tensor(d, s, h, op=OP.subtract), inc="gv")
            g2 = p.op("gpsimd", lambda e, d=wlo.ap()[:, ki * 1024 + 512:
                      (ki + 1) * 1024],
                      s=stg[par].ap()[:, 512:1024],
                      h=whi.ap()[:, ki * 1024 + 512:(ki + 1) * 1024]:
                      e.tensor_tensor(d, s, h, op=OP.subtract), inc="gv")
            copy_toks[par] = tok_c
            sub_toks[par] = g2
        return copy_toks[:] + sub_toks[:] + [g1]

    def layernorm(x_tm, bn_sb, mv_sb, rst4, rinv4, in_toks, out_tm=None):
        p.waits("vector", in_toks)
        for blk in range(4):
            for h in range(2):
                src = x_tm.ap()[:, blk * 1024 + h * 512: blk * 1024 + (h + 1) * 512]
                dst = bn_sb.ap()[:, (blk * 2 + h) * 6:(blk * 2 + h + 1) * 6]
                p.op("vector", lambda e, d=dst, s=src: e.bn_stats(d, s))
        p.drain("vector")
        tok_mv = None
        for blk in range(4):
            src = bn_sb.ap()[:, blk * 12:(blk + 1) * 12]
            dst = mv_sb.ap()[:, blk * 2:(blk + 1) * 2]
            tok_mv = p.op("vector", lambda e, d=dst, s=src: e.bn_aggr(d, s),
                          inc="vv")
        p.wait("scalar", tok_mv)
        var_ap = bass.AP(mv_sb, 1, [[8, 128], [2, 4]])
        tok_sq = p.op("scalar", lambda e, d=rst4.ap(), s=var_ap:
                      e.activation(d, s, AF.Sqrt, bias=c_eps.ap()), inc="sv")
        p.wait("vector", tok_sq)
        p.op("vector", lambda e, d=rinv4.ap(), s=rst4.ap(): e.reciprocal(d, s))
        p.drain("vector")
        tok = None
        dst_t = x_tm if out_tm is None else out_tm
        for blk in range(4):
            xa = x_tm.ap()[:, blk * 1024:(blk + 1) * 1024]
            da = dst_t.ap()[:, blk * 1024:(blk + 1) * 1024]
            mcol = mv_sb.ap()[:, blk * 2: blk * 2 + 1]
            rcol = rinv4.ap()[:, blk: blk + 1]
            tok = p.op("vector", lambda e, d=da, s=xa, m=mcol, r=rcol:
                       e.tensor_scalar(d, s, m, r, op0=OP.subtract, op1=OP.mult),
                       inc="vv")
        return tok

    ra = Region(nc, ov_base, ov_top)
    wq_hi = ra.t("wq_hi", [128, 8192], BF16)
    wq_lo = ra.t("wq_lo", [128, 8192], BF16)
    wk_hi = ra.t("wk_hi", [128, 8192], BF16)
    wk_lo = ra.t("wk_lo", [128, 8192], BF16)
    wv_hi = ra.t("wv_hi", [128, 8192], BF16)
    wv_lo = ra.t("wv_lo", [128, 8192], BF16)
    stgA = [ra.t(f"stgA{i}", [128, 1024], F32) for i in range(3)]
    xa_tm2 = [ra.t("xa_tm", [128, 4096], F32), ra.t("xa_tmB", [128, 4096], F32)]
    yT = ra.t("yT", [128, 4096], F32)
    yT_hi = ra.t("yT_hi", [128, 4096], BF16)
    yT_lo = ra.t("yT_lo", [128, 4096], BF16)
    bnA = ra.t("bnA", [128, 48], F32)
    mvA = ra.t("mvA", [128, 8], F32)
    rstA = ra.t("rstA", [128, 4], F32)
    rinvA = ra.t("rinvA", [128, 4], F32)
    qk_f = [ra.t("qk_f0", [128, 512], F32), ra.t("qk_f1", [128, 512], F32)]
    qk_h = [ra.t("qk_h0", [128, 512], BF16), ra.t("qk_h1", [128, 512], BF16)]
    qk_l = [ra.t("qk_l0", [128, 512], BF16), ra.t("qk_l1", [128, 512], BF16)]
    v_f = [ra.t("v_f0", [128, 1024], F32), ra.t("v_f1", [128, 1024], F32)]
    v_h = [ra.t("v_h0", [128, 1024], BF16), ra.t("v_h1", [128, 1024], BF16)]
    v_l = [ra.t("v_l0", [128, 1024], BF16), ra.t("v_l1", [128, 1024], BF16)]
    assert ra.cur <= ov_top

    def wsl(t, ki, mo):
        return t.ap()[:, ki * 1024 + mo * 128: ki * 1024 + (mo + 1) * 128]

    def wsl512(t, ki, h):
        return t.ap()[:, ki * 1024 + h * 512: ki * 1024 + (h + 1) * 512]

    def ysl(t, ki):
        return t.ap()[:, ki * 512:(ki + 1) * 512]

    def ysl_blk(t, ki, blk):
        return t.ap()[:, ki * 512 + blk * 128: ki * 512 + (blk + 1) * 128]

    x_free_tok = [None, None]
    psS_free = [None, None]
    psHO_free = [None, None]
    qk_busy = [()] * 2
    v_busy = [()] * 2
    gqk = 0
    gv_i = 0
    prev_ylo = None
    prev_ylo2 = None

    def split_halves(dst_t, src_t, hi_t, n, tok_hi):
        h = (n * 5) // 32 // 128 * 128
        p.wait("gpsimd", tok_hi)
        g1 = p.op("gpsimd", lambda e, d=dst_t.ap()[:, 0:h],
                  s=src_t.ap()[:, 0:h], hh=hi_t.ap()[:, 0:h]:
                  e.tensor_tensor(d, s, hh, op=OP.subtract), inc="gv")
        p.drain("vector")
        g2 = p.op("vector", lambda e, d=dst_t.ap()[:, h:n],
                  s=src_t.ap()[:, h:n], hh=hi_t.ap()[:, h:n]:
                  e.tensor_tensor(d, s, hh, op=OP.subtract), inc="vv")
        return g1, g2

    def pa_ln(u):
        xa_tm = xa_tm2[u % 2]
        p.wait("sync", x_free_tok[u % 2])
        tok_ldx = p.dma_group("sync", [
            (xa_tm.ap()[:, blk * 1024:(blk + 1) * 1024], xr3[u * 4 + blk])
            for blk in range(4)])
        return layernorm(xa_tm, bnA, mvA, rstA, rinvA, [tok_ldx])

    def pa_front(u, tok_y):
        nonlocal prev_ylo, prev_ylo2
        xa_tm = xa_tm2[u % 2]
        p.wait("tensor", tok_y)
        p.wait("vector", prev_ylo)
        p.wait("vector", prev_ylo2)
        last_t = None
        for c in range(NCH):
            par = c % 2
            p.wait("tensor", psS_free[par])
            for blk in range(4):
                srcx = xa_tm.ap()[:, blk * 1024 + c * 128:
                                  blk * 1024 + (c + 1) * 128]
                last_t = p.op("tensor", lambda e,
                              d=psS[par].ap()[:, blk * 128:(blk + 1) * 128],
                              s=srcx: e.transpose(d, s, c_idf.ap()),
                              inc="tv" if blk == 3 else None)
            p.wait("vector", last_t)
            dst = yT.ap()[:, c * 512:(c + 1) * 512]
            psS_free[par] = p.op("vector", lambda e, d=dst,
                                 s=psS[par].ap()[:, 0:512]:
                                 e.tensor_copy(d, s), inc="vv")
        x_free_tok[u % 2] = last_t

        p.drain("vector")
        tok_yhi = p.op("vector", lambda e: e.tensor_copy(yT_hi.ap(), yT.ap()),
                       inc="vv")
        g1, g2 = split_halves(yT_lo, yT, yT_hi, 4096, tok_yhi)
        prev_ylo, prev_ylo2 = g1, g2
        return [tok_yhi, g1, g2]

    def pa_back(u, ytoks, wtoks):
        nonlocal gqk, gv_i
        wq_toks, wk_toks, wv_toks = wtoks
        p.waits("tensor", ytoks)

        for proj, whi, wlo, bias, sh_d, sl_d in (
                (0, wq_hi, wq_lo, c_bq, sqh, sql),
                (1, wk_hi, wk_lo, c_bk, skh, skl)):
            p.waits("tensor", wq_toks if proj == 0 else wk_toks)
            for mo in range(NCH):
                par = gqk % 2
                p.wait("tensor", psS_free[par])
                mms = hilo_mms(whi, wlo, yT_hi, yT_lo,
                               lambda t, ki, m=mo: wsl(t, ki, m), ysl)
                tok_mm = p.mm_group(psS[par].ap(), mms)
                for t in qk_busy[par]:
                    p.wait("vector", t)
                p.wait("vector", tok_mm)
                bcol = bias.ap()[:, mo:mo + 1]
                p.op("vector", lambda e, d=qk_f[par].ap(), s=psS[par].ap(),
                     bc=bcol: e.tensor_scalar(d, s, bc, None, op0=OP.add))
                p.drain("vector")
                psS_free[par] = p.op("vector", lambda e, d=qk_h[par].ap(),
                                     s=qk_f[par].ap(): e.tensor_copy(d, s),
                                     inc="vv")
                tok_hi = psS_free[par]
                p.wait("gpsimd", tok_hi)
                tok_lo = p.op("gpsimd", lambda e, d=qk_l[par].ap(),
                              s=qk_f[par].ap(), h=qk_h[par].ap():
                              e.tensor_tensor(d, s, h, op=OP.subtract), inc="gv")
                p.wait("scalar", tok_hi)
                st1 = p.dma("scalar", sh_d.ap()[mo][:, u * 512:(u + 1) * 512],
                            qk_h[par].ap())
                p.wait("scalar", tok_lo)
                st2 = p.dma("scalar", sl_d.ap()[mo][:, u * 512:(u + 1) * 512],
                            qk_l[par].ap())
                qk_busy[par] = (st1, st2, tok_lo)
                gqk += 1

        p.waits("tensor", wv_toks)
        for blk in range(4):
            par = gv_i % 2
            pst = psO[par]
            p.wait("tensor", psHO_free[par])
            mms = []
            for ki in range(NCH):
                yh = ysl_blk(yT_hi, ki, blk)
                yl = ysl_blk(yT_lo, ki, blk)
                for h in range(2):
                    mms.append((yh, wsl512(wv_hi, ki, h), h))
                    mms.append((yh, wsl512(wv_lo, ki, h), h))
                    mms.append((yl, wsl512(wv_hi, ki, h), h))
            halves = [pst.ap()[:, 0:512], pst.ap()[:, 512:1024]]
            tok_mm = p.mm_group_halves(halves, mms)
            for t in v_busy[par]:
                p.wait("vector", t)
            p.wait("vector", tok_mm)
            p.op("vector", lambda e, d=v_f[par].ap(), s=pst.ap(),
                 b_=c_bv.ap(): e.tensor_tensor(d, s, b_, op=OP.add))
            p.drain("vector")
            psHO_free[par] = p.op("vector", lambda e, d=v_h[par].ap(),
                                  s=v_f[par].ap(): e.tensor_copy(d, s),
                                  inc="vv")
            tok_hi = psHO_free[par]
            p.wait("gpsimd", tok_hi)
            tok_lo = p.op("gpsimd", lambda e, d=v_l[par].ap(),
                          s=v_f[par].ap(), h=v_h[par].ap():
                          e.tensor_tensor(d, s, h, op=OP.subtract), inc="gv")
            p.wait("scalar", tok_hi)
            st1 = p.dma("scalar", svh.ap()[u * 4 + blk], v_h[par].ap())
            p.wait("scalar", tok_lo)
            st2 = p.dma("scalar", svl.ap()[u * 4 + blk], v_l[par].ap())
            v_busy[par] = (st1, st2, tok_lo)
            gv_i += 1

    ty0 = pa_ln(0)
    ytoks0 = pa_front(0, ty0)
    stg_c, stg_s = [None] * 3, [None] * 3
    wq_toks = load_weight_sg(wd["wq"], wq_hi, wq_lo, stgA, stg_c, stg_s)
    wk_toks = load_weight_sg(wd["wk"], wk_hi, wk_lo, stgA, stg_c, stg_s)
    wv_toks = load_weight_sg(wd["wv"], wv_hi, wv_lo, stgA, stg_c, stg_s)
    wtoks = (wq_toks, wk_toks, wv_toks)
    ty_next = pa_ln(1) if NU > 1 else None
    pa_back(0, ytoks0, wtoks)
    for u in range(1, NU):
        yt = pa_front(u, ty_next)
        ty_next = pa_ln(u + 1) if u + 1 < NU else None
        pa_back(u, yt, wtoks)

    tok_pa_stores = p.all_dma_toks("scalar")

    TUC = 256
    NU2 = S // TUC
    rd = Region(nc, ov_base, ov_top)
    wk2_hi = rd.t("wk2_hi", [128, 8192], BF16)
    wk2_lo = rd.t("wk2_lo", [128, 8192], BF16)
    stgD = [rd.t("stgD0", [128, 1024], F32), rd.t("stgD1", [128, 1024], F32)]
    wr_hi = rd.t("wr_hi", [128, 8192], BF16)
    wr_lo = rd.t("wr_lo", [128, 8192], BF16)
    wv2_hi = rd.t("wv2_hi", [128, 8192], BF16)
    wv2_lo = rd.t("wv2_lo", [128, 8192], BF16)
    x1c = [rd.t("x1c0", [128, 2048], F32), rd.t("x1c1", [128, 2048], F32)]
    y2c = [rd.t("y2c0", [128, 2048], F32), rd.t("y2c1", [128, 2048], F32)]
    y2Tc = [rd.t("y2Tc0", [128, 257], F32), rd.t("y2Tc1", [128, 257], F32)]
    sliv = rd.t("sliv", [128, 8], F32)
    d1c = [rd.t("d1c0", [128, 256], F32), rd.t("d1c1", [128, 256], F32)]
    xrf = [rd.t("xrf0", [128, 256], F32), rd.t("xrf1", [128, 256], F32)]
    xkf = [rd.t("xkf0", [128, 256], F32), rd.t("xkf1", [128, 256], F32)]
    xr_hi = [rd.t("xr_hi0", [128, 2048], BF16), rd.t("xr_hi1", [128, 2048], BF16)]
    xr_lo = [rd.t("xr_lo0", [128, 2048], BF16), rd.t("xr_lo1", [128, 2048], BF16)]
    xk_hi = [rd.t("xk_hi0", [128, 2048], BF16), rd.t("xk_hi1", [128, 2048], BF16)]
    xk_lo = [rd.t("xk_lo0", [128, 2048], BF16), rd.t("xk_lo1", [128, 2048], BF16)]
    bnD = rd.t("bnD", [128, 24], F32)
    mvD = rd.t("mvD", [128, 4], F32)
    rstD = rd.t("rstD", [128, 2], F32)
    rinvD = rd.t("rinvD", [128, 2], F32)
    k2_hi = rd.t("k2_hi", [128, 2048], BF16)
    k2_lo = rd.t("k2_lo", [128, 2048], BF16)
    e_sb = [rd.t("e_sb0", [128, 256], F32), rd.t("e_sb1", [128, 256], F32)]
    t_sb = [rd.t("t_sb0", [128, 256], F32), rd.t("t_sb1", [128, 256], F32)]
    of32 = rd.t("of32", [128, 256], F32)
    xin = rd.t("xin", [128, 2048], F32)
    assert rd.cur <= ov_top

    def ln2_stats(x_tm, in_toks):
        p.waits("vector", in_toks)
        for blk in range(2):
            for h in range(2):
                srcb = x_tm.ap()[:, blk * 1024 + h * 512:
                                 blk * 1024 + (h + 1) * 512]
                dstb = bnD.ap()[:, (blk * 2 + h) * 6:(blk * 2 + h + 1) * 6]
                p.op("vector", lambda e, d=dstb, s=srcb: e.bn_stats(d, s))
        p.drain("vector")

    def ln2_norm(x_tm, out_tm):
        tok_mv = None
        for blk in range(2):
            srcb = bnD.ap()[:, blk * 12:(blk + 1) * 12]
            dstb = mvD.ap()[:, blk * 2:(blk + 1) * 2]
            tok_mv = p.op("vector", lambda e, d=dstb, s=srcb: e.bn_aggr(d, s),
                          inc="vv")
        p.wait("scalar", tok_mv)
        var_ap = bass.AP(mvD, 1, [[4, 128], [2, 2]])
        tok_sq = p.op("scalar", lambda e, d=rstD.ap(), s=var_ap:
                      e.activation(d, s, AF.Sqrt, bias=c_eps.ap()), inc="sv")
        p.wait("vector", tok_sq)
        p.op("vector", lambda e, d=rinvD.ap(), s=rstD.ap(): e.reciprocal(d, s))
        p.drain("vector")
        tok = None
        for blk in range(2):
            xa = x_tm.ap()[:, blk * 1024:(blk + 1) * 1024]
            da = out_tm.ap()[:, blk * 1024:(blk + 1) * 1024]
            mcol = mvD.ap()[:, blk * 2: blk * 2 + 1]
            rcol = rinvD.ap()[:, blk: blk + 1]
            tok = p.op("vector", lambda e, d=da, s=xa, m=mcol, r=rcol:
                       e.tensor_scalar(d, s, m, r, op0=OP.subtract,
                                       op1=OP.mult), inc="vv")
        return tok

    def ln2(x_tm, out_tm, in_toks):
        ln2_stats(x_tm, in_toks)
        return ln2_norm(x_tm, out_tm)

    y_store_par = [None, None]
    mix_free = [None, None]
    prev_split2 = [None, None]
    th2, tl2 = [None, None], [None, None]
    prev_k2lo = None
    k2_read_done = [None, None]
    tok_y2u = [None, None]
    ghd = 0

    def ysl256(t, ki):
        return t.ap()[:, ki * 256:(ki + 1) * 256]

    fa_ld = [None, None]
    fa_ldx = [None]

    def c2_front_a_load(u):
        par = u % 2
        p.waits("sync", acc_group[(u * 2) // 8])
        if u == 0:
            p.waits("sync", [("vv", p.cnt["vv"]), ("sv", p.cnt["sv"]),
                             ("gv", p.cnt["gv"]), ("tv", p.cnt["tv"])])
        else:
            p.waits("sync", [tok_y2u[1 - par]])
        fa_ldx[0] = p.dma_group("sync", [
            (xin.ap()[:, blk * 1024:(blk + 1) * 1024],
             sx1.ap()[u * 2 + blk]) for blk in range(2)])
        p.wait("sync", y_store_par[par])
        p.waits("sync", tok_kv_dead)
        fa_ld[par] = p.dma_group("sync", [
            (x1c[par].ap()[:, blk * 1024:(blk + 1) * 1024],
             sx1.ap()[u * 2 + blk]) for blk in range(2)])

    def c2_front_a_stats(u):
        par = u % 2
        if u == 0:
            p.waits("vector", tok_kv_dead)
        ln2_stats(xin, [fa_ldx[0]])

    def c2_front_a_norm(u):
        par = u % 2
        tok_y2u[par] = ln2_norm(xin, y2c[par])

    def c2_front_a(u):
        c2_front_a_load(u)
        c2_front_a_stats(u)
        c2_front_a_norm(u)

    def c2_front_b(u):
        par = u % 2
        split_last = []
        ch_split = {}

        def chunk(c):
            yc = y2Tc[c % 2]
            if c == 0:
                p.wait("tensor", tok_y2u[par])
                p.wait("vector", prev_k2lo)
            p.waits("vector", [mix_free[c % 2], tok_sliv])
            p.drain("vector")
            p.op("vector", lambda e, d=yc.ap()[:, 0:1],
                 s=sliv.ap()[:, c:c + 1]: e.tensor_copy(d, s))
            part = c % 2
            p.wait("tensor", psS_free[part])
            tt = None
            for blk in range(2):
                srcy = y2c[par].ap()[:, blk * 1024 + c * 128:
                                     blk * 1024 + (c + 1) * 128]
                tt = p.op("tensor", lambda e,
                          d=psS[part].ap()[:, blk * 128:(blk + 1) * 128],
                          s=srcy: e.transpose(d, s, c_idf.ap()),
                          inc="tv" if blk == 1 else None)
            p.wait("vector", tt)
            psS_free[part] = p.op("vector", lambda e, d=yc.ap()[:, 1:257],
                                  s=psS[part].ap()[:, 0:256]:
                                  e.tensor_copy(d, s), inc="vv")
            p.drain("vector")
            p.op("vector", lambda e, d=sliv.ap()[:, c:c + 1],
                 s=yc.ap()[:, 256:257]: e.tensor_copy(d, s))
            p.op("vector", lambda e, d=d1c[par].ap(), a=yc.ap()[:, 1:257],
                 b_=yc.ap()[:, 0:256]:
                 e.tensor_tensor(d, a, b_, op=OP.subtract))
            p.drain("vector")
            cp = c % 2
            p.wait("vector", prev_split2[cp])
            p.wait("vector", k2_read_done[par])
            p.op("vector", lambda e, d=xrf[cp].ap(), a=d1c[par].ap(),
                 mu=c_mur.ap()[:, c:c + 1], b_=yc.ap()[:, 0:256]:
                 e.scalar_tensor_tensor(d, a, mu, b_, op0=OP.mult, op1=OP.add))
            tok_mix = p.op("vector", lambda e, d=xkf[cp].ap(),
                           a=d1c[par].ap(),
                           mu=c_muk.ap()[:, c:c + 1],
                           b_=yc.ap()[:, 0:256]:
                           e.scalar_tensor_tensor(d, a, mu, b_, op0=OP.mult,
                                                  op1=OP.add), inc="vv")
            mix_free[c % 2] = tok_mix
            p.drain("vector")
            t1 = p.op("vector", lambda e,
                      d=xr_hi[par].ap()[:, c * 256:(c + 1) * 256],
                      s=xrf[cp].ap(): e.tensor_copy(d, s), inc="vv")
            t2 = p.op("vector", lambda e,
                      d=xk_hi[par].ap()[:, c * 256:(c + 1) * 256],
                      s=xkf[cp].ap(): e.tensor_copy(d, s), inc="vv")
            p.waits("gpsimd", [t1, t2])
            t3 = p.op("gpsimd", lambda e,
                      d=xr_lo[par].ap()[:, c * 256:(c + 1) * 256],
                      s=xrf[cp].ap(),
                      h=xr_hi[par].ap()[:, c * 256:(c + 1) * 256]:
                      e.tensor_tensor(d, s, h, op=OP.subtract), inc="gv")
            t4 = p.op("gpsimd", lambda e,
                      d=xk_lo[par].ap()[:, c * 256:(c + 1) * 256],
                      s=xkf[cp].ap(),
                      h=xk_hi[par].ap()[:, c * 256:(c + 1) * 256]:
                      e.tensor_tensor(d, s, h, op=OP.subtract), inc="gv")
            prev_split2[cp] = t4
            split_last[:] = [t1, t2, t3, t4]
            ch_split[c] = [t1, t2, t3, t4]
        return [lambda c=c: chunk(c) for c in range(NCH)], split_last, ch_split

    def c2_back(u, split_toks, chs, fillers=None):
        nonlocal ghd, prev_k2lo
        par = u % 2
        gidx = [0]

        def fill():
            if fillers is not None and gidx[0] % 3 == 2:
                ci = gidx[0] // 3
                if ci < NCH:
                    fillers[ci]()
            gidx[0] += 1

        p.waits("tensor", wtoks_wr)
        r_t = y2c[par]
        for mo in range(NCH):
            parh = ghd % 4
            p.wait("tensor", psH_free[parh])
            mms = hilo_mms(wr_hi, wr_lo, xr_hi[par], xr_lo[par],
                           lambda t, ki, m=mo: wsl(t, ki, m), ysl256)
            if mo == 0:
                pre = [None] * len(mms)
                for ki in range(NCH):
                    pre[3 * ki] = [chs[ki][0], chs[ki][2]]
                tok_mm = p.mm_group_w(psH[parh][:, 0:256], mms, pre)
            else:
                tok_mm = p.mm_group(psH[parh][:, 0:256], mms)
            p.wait("scalar", tok_mm)
            psH_free[parh] = p.op("scalar", lambda e,
                                  d=r_t.ap()[:, mo * 256:(mo + 1) * 256],
                                  s=psH[parh][:, 0:256]:
                                  e.activation(d, s, AF.Sigmoid), inc="sv")
            ghd += 1
            fill()
        if u == 0:
            p.waits("sync", tok_kv_dead)
            wv2_toks.extend(load_weight(wd["wv2"], wv2_hi, wv2_lo, stgD,
                                        stg_c2, stg_s2))
        p.waits("tensor", wk2_toks)
        k2s = {}
        for mo in range(NCH):
            mp = mo % 2
            parh = ghd % 4
            p.wait("tensor", psH_free[parh])
            mms = hilo_mms(wk2_hi, wk2_lo, xk_hi[par], xk_lo[par],
                           lambda t, ki, m=mo: wsl(t, ki, m), ysl256)
            if mo == 0:
                pre = [None] * len(mms)
                for ki in range(NCH):
                    pre[3 * ki] = [chs[ki][1], chs[ki][3]]
                tok_mm = p.mm_group_w(psH[parh][:, 0:256], mms, pre)
            else:
                tok_mm = p.mm_group(psH[parh][:, 0:256], mms)
            p.wait("scalar", tok_mm)
            p.waits("scalar", [th2[mp], tl2[mp]])
            p.drain("scalar")
            psH_free[parh] = p.op("scalar", lambda e, d=e_sb[mp].ap(),
                                  s=psH[parh][:, 0:256]:
                                  e.activation(d, s, AF.Relu), inc="sv")
            p.drain("scalar")
            tok_t = p.op("scalar", lambda e, d=t_sb[mp].ap(),
                         s=e_sb[mp].ap():
                         e.activation(d, s, AF.Square), inc="sv")
            p.drain("scalar")
            p.wait("vector", tok_t)
            th = p.op("vector", lambda e,
                      d=k2_hi.ap()[:, mo * 256:(mo + 1) * 256],
                      s=t_sb[mp].ap(): e.tensor_copy(d, s), inc="vv")
            p.wait("gpsimd", th)
            tl = p.op("gpsimd", lambda e,
                      d=k2_lo.ap()[:, mo * 256:(mo + 1) * 256],
                      s=t_sb[mp].ap(),
                      h=k2_hi.ap()[:, mo * 256:(mo + 1) * 256]:
                      e.tensor_tensor(d, s, h, op=OP.subtract), inc="gv")
            th2[mp], tl2[mp] = th, tl
            k2s[mo] = [th, tl]
            ghd += 1
            fill()
        prev_k2lo = tl
        k2_read_done[par] = None
        p.waits("tensor", wv2_toks)
        p.waits("vector", [fa_ld[par]])
        ev_last = None
        for mo in range(NCH):
            parh = ghd % 4
            p.wait("tensor", psH_free[parh])
            mms = hilo_mms(wv2_hi, wv2_lo, k2_hi, k2_lo,
                           lambda t, ki, m=mo: wsl(t, ki, m), ysl256)
            if mo == 0:
                pre = [None] * len(mms)
                for ki in range(NCH):
                    pre[3 * ki] = k2s[ki]
                tok_mm = p.mm_group_w(psH[parh][:, 0:256], mms, pre)
            else:
                tok_mm = p.mm_group(psH[parh][:, 0:256], mms)
            p.waits("vector", [tok_mm])
            tok_of = p.op("vector", lambda e, d=of32.ap(),
                          s=psH[parh][:, 0:256],
                          r=r_t.ap()[:, mo * 256:(mo + 1) * 256]:
                          e.tensor_tensor(d, s, r, op=OP.mult), inc="vv")
            psH_free[parh] = tok_of
            p.waits("tensor", [tok_of])
            part = ghd % 2
            p.wait("tensor", psS_free[part])
            tt = None
            for blk in range(2):
                srco2 = of32.ap()[:, blk * 128:(blk + 1) * 128]
                tt = p.op("tensor", lambda e,
                          d=psS[part].ap()[:, blk * 128:(blk + 1) * 128],
                          s=srco2: e.transpose(d, s, c_idf.ap()),
                          inc="tv" if blk == 1 else None)
            p.wait("vector", tt)
            xs = bass.AP(x1c[par], mo * 128, [[2048, 128], [1024, 2], [1, 128]])
            ps3 = bass.AP(psS[part], 0, [[512, 128], [128, 2], [1, 128]])
            ev_last = p.op("vector", lambda e, d=xs, s=ps3:
                           e.tensor_tensor(d, s, d, op=OP.add), inc="vv")
            psS_free[part] = ev_last
            ghd += 1
            fill()
        k2_read_done[par] = ("tv", p.cnt["tv"])
        p.wait("scalar", ev_last)
        y_store_par[par] = p.dma_group("scalar", [
            (yr3[u * 2 + blk], x1c[par].ap()[:, blk * 1024:(blk + 1) * 1024])
            for blk in range(2)])


    r_wo = Region(nc, ov_base, ov_top)
    wo_hi = r_wo.t("wo_hi", [128, 8192], BF16)
    wo_lo = r_wo.t("wo_lo", [128, 8192], BF16)
    stgC = [r_wo.t("stgC0", [128, 1024], F32), r_wo.t("stgC1", [128, 1024], F32)]
    ov_base2 = r_wo.cur
    rb = Region(nc, ov_base2, ov_top)
    q_addr = (rb.cur + 63) & ~63
    qhiT = rb.t("qhiT", [128, 8192], BF16)
    qloT = rb.t("qloT", [128, 8192], BF16)
    o_hi = nc.alloc_sbuf_tensor_at("o_hi", [128, 8192], BF16, offset=q_addr)
    o_lo = nc.alloc_sbuf_tensor_at("o_lo", [128, 8192], BF16,
                                   offset=q_addr + 16384)
    khiT = [rb.t("khiT0", [128, 4096], BF16), rb.t("khiT1", [128, 4096], BF16)]
    kloT = [rb.t("kloT0", [128, 4096], BF16), rb.t("kloT1", [128, 4096], BF16)]
    vhiT = [rb.t("vhiT0", [128, 4096], BF16), rb.t("vhiT1", [128, 4096], BF16)]
    vloT = [rb.t("vloT0", [128, 4096], BF16), rb.t("vloT1", [128, 4096], BF16)]
    o_addr = (rb.cur + 63) & ~63
    o_sb = rb.t("o_sb", [128, 8192], F32)
    oTb_hi = nc.alloc_sbuf_tensor_at("oTb_hi", [128, 8192], BF16,
                                     offset=o_addr)
    oTb_lo = nc.alloc_sbuf_tensor_at("oTb_lo", [128, 8192], BF16,
                                     offset=o_addr + 16384)
    ab_addr = (rb.cur + 63) & ~63
    at_bk = [rb.t("at_bk0", [128, 512], F32), rb.t("at_bk1", [128, 512], F32)]
    at_bk2 = [rb.t("at_bk20", [128, 512], F32), rb.t("at_bk21", [128, 512], F32)]
    a_sb = [rb.t("a_sb0", [128, 512], F32), rb.t("a_sb1", [128, 512], F32),
            rb.t("a_sb2", [128, 512], F32), rb.t("a_sb3", [128, 512], F32)]
    p_sb = [rb.t("p_sb0", [128, 512], F32), rb.t("p_sb1", [128, 512], F32)]
    phi = [rb.t("phi0", [128, 512], BF16), rb.t("phi1", [128, 512], BF16)]
    plo = [rb.t("plo0", [128, 512], BF16), rb.t("plo1", [128, 512], BF16)]
    phiT = [rb.t("phiT0", [128, 512], BF16), rb.t("phiT1", [128, 512], BF16)]
    ploT = [rb.t("ploT0", [128, 512], BF16), rb.t("ploT1", [128, 512], BF16)]
    nmr = rb.t("nmr", [128, 16], F32)
    js = nc.alloc_sbuf_tensor_at("js", [128, 512], F32, offset=ab_addr)
    lr = rb.t("lr", [128, 8], F32)
    nmj = rb.t("nmj", [128, 1], F32)
    nmnew = rb.t("nmnew", [128, 2], F32)
    nm32 = rb.t("nm32", [128, 1], F32)
    alph = [rb.t("alph0", [128, 1], F32), rb.t("alph1", [128, 1], F32)]
    lpart = [rb.t("lpart0", [128, 1], F32), rb.t("lpart1", [128, 1], F32)]
    linv = rb.t("linv", [128, 1], F32)
    assert rb.cur <= ov_top

    p.waits("sync", tok_pa_stores)

    kv_busy = [None, None]
    psT_free = [None, None]
    psO_freeB = [None, None]
    psH_free = [None, None, None, None]
    wtoks_o = []
    at_free2 = [None, None]
    acc_toks = [None, None]
    ghb = 0
    tok_n = None
    alpha_busy = [None, None]

    kv_toks = {}

    def load_q(g):
        return p.dma_group("sync", [
            (qhiT.ap()[:, c * 1024:(c + 1) * 1024],
             sqh.ap()[c][:, g * QG:(g + 1) * QG]) for c in range(NCH)] + [
            (qloT.ap()[:, c * 1024:(c + 1) * 1024],
             sql.ap()[c][:, g * QG:(g + 1) * QG]) for c in range(NCH)])

    def load_kv(g, j, q="sync"):
        par = j % 2
        p.wait(q, kv_busy[par])
        prs = []
        for c in range(NCH):
            prs.append((khiT[par].ap()[:, c * 512:(c + 1) * 512],
                        skh.ap()[c][:, j * 512:(j + 1) * 512]))
            prs.append((kloT[par].ap()[:, c * 512:(c + 1) * 512],
                        skl.ap()[c][:, j * 512:(j + 1) * 512]))
        for kb in range(4):
            prs.append((vhiT[par].ap()[:, kb * 1024:(kb + 1) * 1024],
                        svh.ap()[j * 4 + kb]))
            prs.append((vloT[par].ap()[:, kb * 1024:(kb + 1) * 1024],
                        svl.ap()[j * 4 + kb]))
        kv_toks[(g, j)] = p.dma_group(q, prs)

    tok_q_next = load_q(0)
    p.waits("gpsimd", tok_pa_stores)
    load_kv(0, 0, q="gpsimd")
    prev_c1_free = None
    acc_group = []
    oT_done_last = [None]
    stg_c2, stg_s2 = [None, None], [None, None]
    wtoks_o.extend(load_weight(wd["wo"], wo_hi, wo_lo, stgC,
                               stg_c2, stg_s2))
    c2_hooked = [False]
    wtoks_wr = []

    for g in range(NG):
        nkc = KCH_PER_G * (g + 1)
        tok_q = tok_q_next

        sub_ph = [0] * SUBS
        pairs = []
        for j in range(nkc):
            smin = max(0, 4 * j - 8 * g)
            for s in range(smin, SUBS):
                qi = g * SUBS + s
                n_k = min(512, (qi + 1) * 128 - j * 512)
                pairs.append((j, s, n_k))

        def emit_qk(i):
            j, s, n_k = pairs[i]
            par = i % 2
            kpar = j % 2
            p.waits("tensor", [tok_q, kv_toks[(g, j)], psS_free[par]])
            mms = []
            for ki in range(NCH):
                qh = qhiT.ap()[:, ki * 1024 + s * 128: ki * 1024 + (s + 1) * 128]
                ql = qloT.ap()[:, ki * 1024 + s * 128: ki * 1024 + (s + 1) * 128]
                kh = khiT[kpar].ap()[:, ki * 512: ki * 512 + n_k]
                kl = kloT[kpar].ap()[:, ki * 512: ki * 512 + n_k]
                mms += [(qh, kh), (qh, kl), (ql, kh)]
            return p.mm_group(psS[par].ap()[:, 0:n_k], mms)

        def emit_abs(i, tok_s):
            j, s, n_k = pairs[i]
            par = i % 2
            p.wait("scalar", tok_s)
            return p.op("scalar", lambda e, d=a_sb[i % 4].ap()[:, 0:n_k],
                        s_=psS[par].ap()[:, 0:n_k]:
                        e.activation(d, s_, AF.Abs), inc="sv")

        def emit_rest(i, tok_a):
            j, s, n_k = pairs[i]
            par = i % 2
            kpar = j % 2
            qi = g * SUBS + s
            diag = (qi + 1) * 128 - j * 512 <= 512
            nkb = n_k // 128
            first = j == 0
            p.wait("vector", tok_a)
            if diag:
                p.op("vector", lambda e,
                     d=a_sb[i % 4].ap()[:, n_k - 128:n_k]:
                     e.tensor_tensor(d, d, c_mask.ap(), op=OP.add))
                p.drain("vector")
            ph = sub_ph[s]
            c_old = 2 * s + ph
            c_new = 2 * s + (1 - ph)
            sub_ph[s] = 1 - ph
            if first:
                p.op("vector", lambda e, d=nmr.ap()[:, c_new:c_new + 1],
                     s_=a_sb[i % 4].ap()[:, 0:n_k]:
                     e.tensor_reduce(d, s_, axis=AX.X, op=OP.max, negate=True))
                p.drain("vector")
                tok_stats = p.op("vector", lambda e, d=nm32.ap(),
                                 s_=nmr.ap()[:, c_new:c_new + 1]:
                                 e.tensor_scalar(d, s_, 1.0 / 32, None,
                                                 op0=OP.mult), inc="vv")
                tok_alpha = None
            else:
                p.op("vector", lambda e, j_=js.ap()[:, 0:n_k],
                     s_=a_sb[i % 4].ap()[:, 0:n_k],
                     sd=nmr.ap()[:, c_old:c_old + 1],
                     ac=nmr.ap()[:, c_new:c_new + 1]:
                     e.tensor_scalar(j_, s_, -1.0, sd, op0=OP.mult, op1=OP.min,
                                     accum_out=ac))
                p.drain("vector")
                p.op("vector", lambda e, d=nmnew.ap()[:, 1:2],
                     a=nmr.ap()[:, c_new:c_new + 1],
                     b_=nmr.ap()[:, c_old:c_old + 1]:
                     e.tensor_tensor(d, a, b_, op=OP.subtract))
                tok_stats = p.op("vector", lambda e, d=nm32.ap(),
                                 s_=nmr.ap()[:, c_new:c_new + 1]:
                                 e.tensor_scalar(d, s_, 1.0 / 32, None,
                                                 op0=OP.mult), inc="vv")
            p.wait("scalar", tok_stats)
            p.wait("scalar", alpha_busy[par])
            p.drain("scalar")
            tok_alpha = None
            if not first:
                tok_alpha = p.op("scalar", lambda e, d=alph[par].ap(),
                                 s_=nmnew.ap()[:, 1:2]:
                                 e.activation(d, s_, AF.Exp, scale=1.0 / 32),
                                 inc="sv")
            tok_p = p.op("scalar", lambda e, d=p_sb[par].ap()[:, 0:n_k],
                         s_=a_sb[i % 4].ap()[:, 0:n_k], b_=nm32.ap(),
                         ac=lpart[par].ap():
                         e.activation(d, s_, AF.Exp, scale=1.0 / 32, bias=b_,
                                      accum_out=ac), inc="sv")
            p.wait("vector", tok_p)
            tok_phi = p.op("vector", lambda e, d=phi[par].ap()[:, 0:n_k],
                           s_=p_sb[par].ap()[:, 0:n_k]: e.tensor_copy(d, s_),
                           inc="vv")
            p.wait("gpsimd", tok_phi)
            tok_plo = p.op("gpsimd", lambda e, d=plo[par].ap()[:, 0:n_k],
                           s_=p_sb[par].ap()[:, 0:n_k],
                           h=phi[par].ap()[:, 0:n_k]:
                           e.tensor_tensor(d, s_, h, op=OP.subtract), inc="gv")
            p.waits("tensor", [tok_phi, psT_free[0]])
            tk1 = None
            for kb in range(nkb):
                src = phi[par].ap()[:, kb * 128:(kb + 1) * 128]
                dst = psT[0].ap()[:, kb * 128:(kb + 1) * 128]
                tk1 = p.op("tensor", lambda e, d=dst, s_=src:
                           e.transpose(d, s_, c_idb.ap()),
                           inc="tv" if kb == nkb - 1 else None)
            p.waits("tensor", [tok_plo, psT_free[1]])
            tk2 = None
            for kb in range(nkb):
                src = plo[par].ap()[:, kb * 128:(kb + 1) * 128]
                dst = psT[1].ap()[:, kb * 128:(kb + 1) * 128]
                tk2 = p.op("tensor", lambda e, d=dst, s_=src:
                           e.transpose(d, s_, c_idb.ap()),
                           inc="tv" if kb == nkb - 1 else None)
            p.wait("vector", tk1)
            psT_free[0] = p.op("vector", lambda e,
                               d=phiT[par].ap()[:, 0:n_k],
                               s_=psT[0].ap()[:, 0:n_k]: e.tensor_copy(d, s_),
                               inc="vv")
            p.wait("vector", tk2)
            psT_free[1] = p.op("vector", lambda e,
                               d=ploT[par].ap()[:, 0:n_k],
                               s_=psT[1].ap()[:, 0:n_k]: e.tensor_copy(d, s_),
                               inc="vv")
            opar = i % 2
            p.waits("tensor", [psT_free[0], psT_free[1], kv_toks[(g, j)],
                               psO_freeB[opar]])
            mms = []
            for kb in range(nkb):
                ph = phiT[par].ap()[:, kb * 128:(kb + 1) * 128]
                pl = ploT[par].ap()[:, kb * 128:(kb + 1) * 128]
                for h in range(2):
                    vh = vhiT[kpar].ap()[:, kb * 1024 + h * 512:
                                         kb * 1024 + (h + 1) * 512]
                    vl = vloT[kpar].ap()[:, kb * 1024 + h * 512:
                                         kb * 1024 + (h + 1) * 512]
                    mms.append((ph, vh, h))
                    mms.append((ph, vl, h))
                    mms.append((pl, vh, h))
            halves = [psO[opar].ap()[:, 0:512], psO[opar].ap()[:, 512:1024]]
            tok_pv = p.mm_group_halves(halves, mms)
            p.waits("vector", [tok_pv, tok_alpha, tok_p])
            lsl = lr.ap()[:, s:s + 1]
            dsl = o_sb.ap()[:, s * 1024:(s + 1) * 1024]
            if first:
                if prev_c1_free is not None:
                    p.waits("vector", prev_c1_free)
                psO_freeB[opar] = p.op("vector", lambda e, d=dsl,
                                       ps_=psO[opar].ap():
                                       e.tensor_copy(d, ps_), inc="vv")
                alpha_busy[par] = p.op(
                    "vector", lambda e, d=lsl, lp=lpart[par].ap():
                    e.tensor_copy(d, lp), inc="vv")
            else:
                psO_freeB[opar] = p.op("vector", lambda e, d=dsl,
                                       al=alph[par].ap(), ps_=psO[opar].ap():
                                       e.scalar_tensor_tensor(
                                           d, d, al, ps_, op0=OP.mult,
                                           op1=OP.add), inc="vv")
                alpha_busy[par] = p.op(
                    "vector", lambda e, d=lsl, al=alph[par].ap(),
                    lp=lpart[par].ap():
                    e.scalar_tensor_tensor(d, d, al, lp, op0=OP.mult,
                                           op1=OP.add), inc="vv")

        npairs = len(pairs)
        tokS = [None] * npairs
        tokA = [None] * npairs
        tv_after_qk = None
        tokS[0] = emit_qk(0)
        tokA[0] = emit_abs(0, tokS[0])
        psS_free[0] = tokA[0]
        for ii in (1, 2):
            if ii < npairs:
                jn = pairs[ii][0]
                if (g, jn) not in kv_toks:
                    load_kv(g, jn)
                tokS[ii] = emit_qk(ii)
                if ii == 1:
                    tokA[1] = emit_abs(1, tokS[1])
                    psS_free[1] = tokA[1]
        for i in range(npairs):
            j, s, n_k = pairs[i]
            if i + 3 < npairs:
                jn3 = pairs[i + 3][0]
                if (g, jn3) not in kv_toks:
                    load_kv(g, jn3)
                tokS[i + 3] = emit_qk(i + 3)
            if i + 2 < npairs:
                tokA[i + 2] = emit_abs(i + 2, tokS[i + 2])
                psS_free[i % 2] = tokA[i + 2]
            if i + 3 >= npairs:
                tv_after_qk = ("tv", p.cnt["tv"])
            emit_rest(i, tokA[i])
            qi = g * SUBS + s
            if (qi + 1) * 128 - j * 512 <= 512:
                p.drain("vector")
                p.op("vector", lambda e, d=linv.ap(), s_=lr.ap()[:, s:s + 1]:
                     e.reciprocal(d, s_))
                p.drain("vector")
                tok_n = p.op("vector", lambda e,
                             d=o_sb.ap()[:, s * 1024:(s + 1) * 1024],
                             li=linv.ap():
                             e.tensor_scalar(d, d, li, None, op0=OP.mult),
                             inc="vv")
                p.drain("vector")
            if i + 1 == len(pairs) or pairs[i + 1][0] != j:
                kv_busy[j % 2] = ("tv", p.cnt["tv"])
        if g + 1 < NG:
            load_kv(g + 1, 0)
            load_kv(g + 1, 1)
        for k in range(4):
            psH_free[k] = ("vv", p.cnt["vv"])
        tok_base = p.dma_group("sync", [
            (sx1.ap()[g * 8 + b], xr3[g * 8 + b]) for b in range(8)])
        p.waits("scalar", [tok_n, tv_after_qk])
        p.waits("gpsimd", [tok_n, tv_after_qk])
        chunk_hi = [None] * NCH
        chunk_lo = [None] * NCH
        for c in range(NCH):
            sap3 = bass.AP(o_sb, c * 128, [[8192, 128], [1024, 8], [1, 128]])
            hap3 = bass.AP(o_hi, c * 128, [[8192, 128], [1024, 8], [1, 128]])
            lap3 = bass.AP(o_lo, c * 128, [[8192, 128], [1024, 8], [1, 128]])
            p.drain("scalar")
            chunk_hi[c] = p.op("scalar", lambda e, d=hap3, s_=sap3:
                               e.copy(d, s_), inc="sv")
            p.wait("gpsimd", chunk_hi[c])
            chunk_lo[c] = p.op("gpsimd", lambda e, d=lap3, s_=sap3, h=hap3:
                               e.tensor_tensor(d, s_, h, op=OP.subtract),
                               inc="gv")
        ev_toks = []
        for src_t, toks in ((o_hi, chunk_hi), (o_lo, chunk_lo)):
            dst_t = oTb_hi if src_t is o_hi else oTb_lo
            for c in range(NCH):
                tp = len(ev_toks) % 2
                p.waits("tensor", [toks[c], psT_free[tp]])
                tt = None
                for b in range(8):
                    sap = src_t.ap()[:, b * 1024 + c * 128:
                                     b * 1024 + (c + 1) * 128]
                    tt = p.op("tensor", lambda e,
                              d=psT[tp].ap()[:, b * 128:(b + 1) * 128],
                              s_=sap: e.transpose(d, s_, c_idb.ap()),
                              inc="tv" if b == 7 else None)
                p.wait("vector", tt)
                p.wait("vector", chunk_lo[NCH - 1])
                dap = dst_t.ap()[:, c * 1024:(c + 1) * 1024]
                psT_free[tp] = p.op("vector", lambda e, d=dap,
                                    s_=psT[tp].ap()[:, 0:1024]:
                                    e.tensor_copy(d, s_), inc="vv")
                ev_toks.append(psT_free[tp])
        tok_oT_done = ("tv", p.cnt["tv"])
        p.waits("tensor", wtoks_o + [ev_toks[-1]])
        for h in range(2):
            for mo in range(NCH):
                parh = ghb % 4
                p.wait("tensor", psH_free[parh])
                mms = hilo_mms(wo_hi, wo_lo, oTb_hi, oTb_lo,
                               lambda t, ki, m=mo: wsl(t, ki, m),
                               lambda t, ki, hh=h:
                               t.ap()[:, ki * 1024 + hh * 512:
                                      ki * 1024 + (hh + 1) * 512])
                tok_mm = p.mm_group(psH[parh], mms)
                para = ghb % 2
                p.waits("vector", [tok_mm, at_free2[para]])
                bcol = c_bo.ap()[:, mo:mo + 1]
                tok_at = p.op("vector", lambda e, d=at_bk[para].ap(),
                              s_=psH[parh], bc=bcol:
                              e.tensor_scalar(d, s_, bc, None, op0=OP.add),
                              inc="vv")
                psH_free[parh] = tok_at
                p.waits("tensor", [tok_at])
                part = ghb % 2
                p.wait("tensor", psS_free[part])
                tt = None
                for blk in range(4):
                    sap = at_bk[para].ap()[:, blk * 128:(blk + 1) * 128]
                    tt = p.op("tensor", lambda e,
                              d=psS[part].ap()[:, blk * 128:(blk + 1) * 128],
                              s_=sap: e.transpose(d, s_, c_idf.ap()),
                              inc="tv" if blk == 3 else None)
                at_free2[para] = tt
                p.wait("vector", tt)
                p.wait("vector", acc_toks[para])
                tok_ev = p.op("vector", lambda e, d=at_bk2[para].ap(),
                              s_=psS[part].ap()[:, 0:512]:
                              e.tensor_copy(d, s_), inc="vv")
                psS_free[part] = tok_ev
                p.wait("gpsimd", tok_base)
                p.wait("gpsimd", tok_ev)
                dst3 = bass.AP(sx1, (g * 8 + h * 4) * 128 * 1024 + mo * 128,
                               [[1024, 128], [128 * 1024, 4], [1, 128]])
                src3 = bass.AP(at_bk2[para], 0, [[512, 128], [128, 4], [1, 128]])
                acc_toks[para] = p.dma_group("gpsimd", [(dst3, src3)],
                                             accum=True)
                ghb += 1
                if g == NG - 1 and g >= 1:
                    slot = h * NCH + mo
                    if 1 <= slot <= 8:
                        if slot == 1:
                            p.wait("sync", tok_oT_done)
                        weight_piece(wd["wr"], wr_hi, wr_lo, stgD,
                                     stg_c2, stg_s2, slot - 1)
                        if slot == 8:
                            wtoks_wr = stg_c2[:] + stg_s2[:]
                            tok_kv_dead = [kv_busy[0], kv_busy[1]]
                            c2_front_a_load(0)
                            c2_hooked[0] = True
                    elif slot == 10:
                        c2_front_a_stats(0)
                    elif slot == 12:
                        c2_front_a_norm(0)
        prev_c1_free = [("vv", p.cnt["vv"]), ("tv", p.cnt["tv"])]
        psO_freeB[0] = ("vv", p.cnt["vv"])
        psO_freeB[1] = ("vv", p.cnt["vv"])
        acc_group.append([t for t in acc_toks if t is not None])
        oT_done_last[0] = tok_oT_done
        if g + 1 < NG:
            p.wait("sync", tok_oT_done)
            tok_q_next = load_q(g + 1)

    tok_pb_stores = p.all_dma_toks("scalar") + p.all_dma_toks("gpsimd")
    tok_kv_dead = [kv_busy[0], kv_busy[1]]
    tok_q_dead = oT_done_last[0]
    tok_wo_dead = ("tv", p.cnt["tv"])

    p.wait("vector", tok_wo_dead)
    tok_sliv = p.op("vector", lambda e: e.memset(sliv.ap(), 0.0), inc="vv")
    p.drain("vector")

    wk2_toks = []
    wv2_toks = []
    if not c2_hooked[0]:
        c2_front_a(0)
        p.wait("sync", tok_q_dead)
        wtoks_wr = load_weight(wd["wr"], wr_hi, wr_lo, stgD, stg_c2, stg_s2)
    ch0, s0, cm0 = c2_front_b(0)
    p.waits("gpsimd", acc_group[-1] + [tok_wo_dead])
    for f in ch0:
        f()
    p.wait("sync", tok_wo_dead)
    wk2_toks = load_weight_sg(wd["wk2"], wk2_hi, wk2_lo, stgD,
                              stg_c2, stg_s2)
    splits = {0: list(s0)}
    chmaps = {0: cm0}
    for u in range(1, NU2):
        c2_front_a(u)
        chu, su, cmu = c2_front_b(u)
        c2_back(u - 1, splits[u - 1], chmaps[u - 1], fillers=chu)
        splits[u] = list(su)
        chmaps[u] = cmu
    c2_back(NU2 - 1, splits[NU2 - 1], chmaps[NU2 - 1])

    p.waits("sync", p.all_dma_toks("scalar"))
    p.waits("sync", p.all_dma_toks("sync"))

    with nc.Block() as block:
        @block.sync
        def _(eng):
            for name, fn in p.events:
                if name == "sync":
                    fn(eng)

        @block.tensor
        def _(eng):
            for name, fn in p.events:
                if name == "tensor":
                    fn(eng)

        @block.vector
        def _(eng):
            for name, fn in p.events:
                if name == "vector":
                    fn(eng)

        @block.scalar
        def _(eng):
            for name, fn in p.events:
                if name == "scalar":
                    fn(eng)

        @block.gpsimd
        def _(eng):
            for name, fn in p.events:
                if name == "gpsimd":
                    fn(eng)

    return nc


def _host_prep(inputs, S):
    f32 = np.float32
    w = {}
    w["wqt"] = np.ascontiguousarray(inputs["L0_attn_Wq"].T.astype(f32))
    w["wkt"] = np.ascontiguousarray(inputs["L0_attn_Wk"].T.astype(f32))
    w["wvt"] = np.ascontiguousarray(inputs["L0_attn_Wv"].T.astype(f32))
    w["wot"] = np.ascontiguousarray(inputs["L0_attn_Wo"].T.astype(f32))
    w["wrt"] = np.ascontiguousarray(inputs["L0_ffn_W_r"].T.astype(f32))
    w["wk2t"] = np.ascontiguousarray(inputs["L0_ffn_W_k"].T.astype(f32))
    w["wv2t"] = np.ascontiguousarray(inputs["L0_ffn_W_v"].T.astype(f32))

    def fm(v):
        return np.ascontiguousarray(np.asarray(v, f32).reshape(8, 128).T)

    w["bqf"] = fm(inputs["L0_attn_bq"])
    w["bkf"] = fm(inputs["L0_attn_bk"])
    w["bof"] = fm(inputs["L0_attn_bo"])
    w["murf"] = fm(inputs["L0_ffn_mu_r"])
    w["mukf"] = fm(inputs["L0_ffn_mu_k"])
    w["bvb"] = np.ascontiguousarray(
        np.broadcast_to(np.asarray(inputs["L0_attn_bv"], f32)[None, :],
                        (128, D)))
    w["idb"] = np.eye(128, dtype=ml_dtypes.bfloat16)
    w["epsc"] = np.full((128, 1), 1e-5, f32)
    w["idf"] = np.eye(128, dtype=f32)
    mask = np.zeros((128, 128), f32)
    iu = np.triu_indices(128, 1)
    mask[iu] = -1.0e30
    w["mask"] = mask
    return w


def kernel(**inputs):
    x = np.asarray(inputs["L0_attn_x"], np.float32)
    Bsz, S, _ = x.shape
    assert Bsz == 8
    nc = build_core(S)
    shared = _host_prep(inputs, S)
    in_maps = []
    for b in range(Bsz):
        m = dict(shared)
        m["x"] = np.ascontiguousarray(x[b])
        in_maps.append(m)
    res = run_bass_kernel_spmd(nc, in_maps, core_ids=list(range(8)))
    return np.stack([np.asarray(r["y"], np.float32) for r in res.results],
                    axis=0)
